# version 07
# baseline (speedup 1.0000x reference)
import math
from contextlib import ExitStack

import numpy as np

import concourse.bass as bass
import concourse.mybir as mybir
from concourse.bass_utils import run_bass_kernel_spmd

F32 = mybir.dt.float32
BF16 = mybir.dt.bfloat16
I32 = mybir.dt.int32
AF = mybir.ActivationFunctionType
ALU = mybir.AluOpType
AX = mybir.AxisListType

DEPTH = 12
LAMBDA_INIT = 0.8 - 0.6 * math.exp(-0.3 * DEPTH)
LNC_CONST = math.log((1.0 - LAMBDA_INIT) * math.sqrt(128.0))
LNC_HI = float(np.float32(LNC_CONST))
LNC_LO = float(np.float32(LNC_CONST - LNC_HI))


class Cfg:
    def __init__(self, T, D, H, F, NG):
        self.T = T
        self.D = D
        self.H = H
        self.F = F
        self.NG = NG
        self.HD = 64
        assert D == 128 * H
        self.NTT = T // 128
        self.NDC = D // 128
        self.QB = 512
        assert T % self.QB == 0
        self.NQB = T // self.QB
        self.GT = T // NG
        self.GTT = self.GT // 128
        assert self.GTT % 2 == 0
        self.NFC = F // 512
        self.NFC128 = F // 128
        self.NDOC = D // 512
        self.phases = 5
        self.a_tiles = self.NTT
        self.a_steps = 9
        self.init_only = False
        self.debug = False


CFG_FULL = Cfg(T=2048, D=2048, H=16, F=8192, NG=4)


class Emit:

    def __init__(self, nc, block, stack):
        self.nc = nc
        self.block = block
        self.stack = stack
        self.sems = {}
        self.cnt = {}
        self.waited = {}
        self.res = {}
        self.cur = None

    def sem(self, name):
        h = self.stack.enter_context(self.nc.semaphore(name))
        self.sems[name] = h
        self.cnt[name] = 0

    def sec(self, engname, body):
        def f(eng):
            prev = self.cur
            self.cur = engname
            body(eng)
            self.cur = prev
        getattr(self.block, engname)(f)

    def wait(self, eng, semname, val):
        if val <= 0:
            return
        if semname.startswith("d"):
            val = self.cnt[semname]
        key = (self.cur, semname)
        if self.waited.get(key, -1) >= val:
            return
        eng.wait_ge(self.sems[semname], val)
        self.waited[key] = val

    def inc(self, instr, semname, k=1):
        instr.then_inc(self.sems[semname], k)
        self.cnt[semname] += k
        key = (self.cur, semname)
        return self.cnt[semname]

    def _get(self, rname):
        return self.res.setdefault(rname, {"w": None, "r": []})

    def rd(self, eng, rnames):
        for rn in rnames:
            r = self._get(rn)
            if r["w"] is not None:
                self.wait(eng, r["w"][0], r["w"][1])

    def wr(self, eng, rnames):
        for rn in rnames:
            r = self._get(rn)
            if r["w"] is not None:
                self.wait(eng, r["w"][0], r["w"][1])
            for (s, v) in r["r"]:
                self.wait(eng, s, v)

    def did_rd(self, rnames, semname):
        v = self.cnt[semname]
        for rn in rnames:
            self._get(rn)["r"].append((semname, v))

    def did_wr(self, rnames, semname):
        v = self.cnt[semname]
        for rn in rnames:
            r = self._get(rn)
            r["w"] = (semname, v)
            r["r"] = []

    def fence(self, engnames=None):
        sems = list(self.sems)
        for en in (engnames or ["tensor", "vector", "scalar", "gpsimd",
                                "sync"]):
            def f(eng, en=en):
                for s in sems:
                    self.wait(eng, s, self.cnt[s])
            self.sec(en, f)

    def tfence(self):
        def f(eng):
            for s in ("sve", "ssc", "sgp"):
                self.wait(eng, s, self.cnt[s])
        self.sec("tensor", f)

    def grp(self, engname, semname, reads, writes, body, dmak=None):
        def f(eng):
            self.rd(eng, reads)
            self.wr(eng, writes)
            last = body(eng)
            if last is not None:
                self.inc(last, semname, 16 if dmak else 1)
            self.did_rd(reads, semname)
            self.did_wr(writes, semname)
        self.sec(engname, f)


def build(cfg):
    T, D, H, F = cfg.T, cfg.D, cfg.H, cfg.F
    NTT, NDC, NQB, QB = cfg.NTT, cfg.NDC, cfg.NQB, cfg.QB
    NG, GT, GTT = cfg.NG, cfg.GT, cfg.GTT
    NFC, NFC128, NDOC = cfg.NFC, cfg.NFC128, cfg.NDOC
    JB = QB // 128

    nc = bass.Bass("TRN2", target_bir_lowering=False)

    x_d = nc.dram_tensor("x", [T, D], F32, kind="ExternalInput").ap()
    wq_d = nc.dram_tensor("Wq", [D, D], F32, kind="ExternalInput").ap()
    wk_d = nc.dram_tensor("Wk", [D, D], F32, kind="ExternalInput").ap()
    wv_d = nc.dram_tensor("Wv", [D, D], F32, kind="ExternalInput").ap()
    wo_d = nc.dram_tensor("Wo", [D, D], F32, kind="ExternalInput").ap()
    lq1_d = nc.dram_tensor("lq1", [64], F32, kind="ExternalInput").ap()
    lk1_d = nc.dram_tensor("lk1", [64], F32, kind="ExternalInput").ap()
    lq2_d = nc.dram_tensor("lq2", [64], F32, kind="ExternalInput").ap()
    lk2_d = nc.dram_tensor("lk2", [64], F32, kind="ExternalInput").ap()
    w1_d = nc.dram_tensor("W1", [F, D], F32, kind="ExternalInput").ap()
    w2_d = nc.dram_tensor("W2", [D, F], F32, kind="ExternalInput").ap()
    y_d = nc.dram_tensor("y", [T, D], F32, kind="ExternalOutput").ap()

    scrkind = "ExternalOutput" if cfg.debug else "Internal"
    w1q_d = nc.dram_tensor("w1q_scr", [F, D], BF16, kind=scrkind).ap()
    w2q_d = nc.dram_tensor("w2q_scr", [D, F], BF16, kind=scrkind).ap()
    wo_scr_d = nc.dram_tensor("wo_scr", [T, D], BF16, kind=scrkind).ap()
    if cfg.debug:
        xnT_dbg = nc.dram_tensor("xnT_dbg", [128, cfg.NDC * T], BF16,
                                 kind="ExternalOutput").ap()
        attnT_dbg = nc.dram_tensor("attnT_dbg", [128, H * T], BF16,
                                   kind="ExternalOutput").ap()
        cells_dbg = nc.dram_tensor("cells_dbg", [1, 16], F32,
                                   kind="ExternalOutput").ap()
        lni_dbg = nc.dram_tensor("lni_dbg", [128, 2 * cfg.NTT], F32,
                                 kind="ExternalOutput").ap()

    if getattr(cfg, "dbgqx", None) is not None:
        dxq_i32 = nc.dram_tensor("dxq_i32", [128, 2048], I32,
                                 kind="ExternalOutput").ap()
        dxq_rp = nc.dram_tensor("dxq_rp", [128, 2048], F32,
                                kind="ExternalOutput").ap()
        dxq_out = nc.dram_tensor("dxq_out", [128, 2048], BF16,
                                 kind="ExternalOutput").ap()
        dxq_in = nc.dram_tensor("dxq_in", [128, 2048], F32,
                                kind="ExternalOutput").ap()
        dxq_s = nc.dram_tensor("dxq_s", [128, 4], F32,
                               kind="ExternalOutput").ap()
    if getattr(cfg, "dbgq", None) is not None:
        dhq_i32 = nc.dram_tensor("dhq_i32", [128, 1024], I32,
                                 kind="ExternalOutput").ap()
        dhq_rp = nc.dram_tensor("dhq_rp", [128, 1024], F32,
                                kind="ExternalOutput").ap()
        dhq_out = nc.dram_tensor("dhq_out", [128, 1024], BF16,
                                 kind="ExternalOutput").ap()
        dhq_in = nc.dram_tensor("dhq_in", [128, 1024], BF16,
                                kind="ExternalOutput").ap()
        dhq_s = nc.dram_tensor("dhq_s", [128, 4], F32,
                               kind="ExternalOutput").ap()

    gctx = ExitStack()

    with nc.Block() as block:
        em = Emit(nc, block, gctx)
        for s in ["dg", "dstore", "dout", "ddbg",
                  "dwq0", "dwq1", "dxa0", "dxa1", "dw1t0", "dw1t1",
                  "dw2t0", "dw2t1", "dres0", "dres1", "dwoB0", "dwoB1",
                  "dwoC0", "dwoC1",
                  "ste", "sve", "ssc", "sgp"]:
            em.sem(s)

        def sb(name, shape, dt, stk=None):
            return (stk or gctx).enter_context(nc.sbuf_tensor(name, shape, dt))

        ones_col = sb("ones_col", [128, 1], BF16)
        ones_colf = sb("ones_colf", [128, 1], F32)
        ones_rowb = sb("ones_rowb", [1, 128], BF16)
        ones_rowf = sb("ones_rowf", [1, 128], F32)
        eye = sb("eye", [128, 128], BF16)
        tri = sb("tri", [128, 128], BF16)
        lamq = sb("lamq", [1, 256], F32)
        cells = sb("cells", [1, 16], F32)
        m1b = sb("m1b", [128, 1], F32)
        m2b = sb("m2b", [128, 1], F32)
        lni = sb("lni", [128, NTT], F32)
        lnb2 = sb("lnb2", [128, NTT], F32)
        scol = sb("scol", [128, 8], F32)
        epscol = sb("epscol", [128, 1], F32)
        scr128 = sb("scr128", [128, 128], BF16)

        lam11 = cells[0:1, 0:1]

        def init_v(v):
            em.inc(v.memset(cells[:], 0.0), "sve")
            em.wait(v, "sve", em.cnt["sve"])
            em.inc(v.memset(ones_col[:], 1.0), "sve")
            em.inc(v.memset(ones_colf[:], 1.0), "sve")
            em.inc(v.memset(ones_rowb[:], 1.0), "sve")
            em.inc(v.memset(ones_rowf[:], 1.0), "sve")
            em.inc(v.memset(epscol[:], 1e-5), "sve")
            em.inc(v.memset(cells[0:1, 15:16], LNC_HI), "sve")
            em.inc(v.memset(cells[0:1, 14:15], LNC_LO), "sve")
            em.inc(v.memset(scr128[:], 1.0), "sve")
            em.wait(v, "sve", em.cnt["sve"])
            em.inc(v.tensor_add(cells[0:1, 15:16], cells[0:1, 15:16],
                                cells[0:1, 14:15]), "sve")
            em.did_wr(["ones", "scr128", "lnccell"], "sve")
        em.sec("vector", init_v)

        def init_eye(g):
            return g.affine_select(out=eye[:], in_=scr128[:],
                                   pattern=[[1, 128]],
                                   compare_op=ALU.is_equal, fill=0.0, base=0,
                                   channel_multiplier=-1)
        em.grp("gpsimd", "sgp", ["scr128"], ["eye"], init_eye)

        def init_tri(g):
            return g.affine_select(out=tri[:], in_=scr128[:],
                                   pattern=[[1, 128]],
                                   compare_op=ALU.is_ge, fill=0.0, base=0,
                                   channel_multiplier=-1)
        em.grp("gpsimd", "sgp", ["scr128"], ["tri"], init_tri)

        def init_lamdma(g):
            for i, dsrc in enumerate([lq1_d, lk1_d, lq2_d, lk2_d]):
                ins = g.dma_start(out=lamq[0:1, i * 64:(i + 1) * 64],
                                  in_=dsrc[:])
                ins.then_inc(em.sems["dg"], 16)
                em.cnt["dg"] += 16
            em.did_wr(["lamq"], "dg")
        em.sec("gpsimd", init_lamdma)

        def init_lam_v(v):
            em.rd(v, ["lamq"])
            em.wr(v, ["lamq", "lamcells"])
            em.inc(v.tensor_mul(lamq[0:1, 0:64], lamq[0:1, 0:64],
                                lamq[0:1, 64:128]), "sve")
            em.wait(v, "sve", em.cnt["sve"])
            em.inc(v.reduce_sum(out=cells[0:1, 5:6], in_=lamq[0:1, 0:64],
                                axis=AX.X), "sve")
            em.inc(v.tensor_mul(lamq[0:1, 128:192], lamq[0:1, 128:192],
                                lamq[0:1, 192:256]), "sve")
            em.wait(v, "sve", em.cnt["sve"])
            em.inc(v.reduce_sum(out=cells[0:1, 6:7], in_=lamq[0:1, 128:192],
                                axis=AX.X), "sve")
            em.did_wr(["lamcells"], "sve")
        em.sec("vector", init_lam_v)

        def init_lam_s(s):
            em.rd(s, ["lamcells"])
            em.wr(s, ["lamexp"])
            em.inc(s.activation(out=cells[0:1, 1:2], in_=cells[0:1, 5:6],
                                func=AF.Exp), "ssc")
            em.inc(s.activation(out=cells[0:1, 2:3], in_=cells[0:1, 6:7],
                                func=AF.Exp), "ssc")
            em.did_wr(["lamexp"], "ssc")
        em.sec("scalar", init_lam_s)

        def init_lam_v2(v):
            em.rd(v, ["lamexp"])
            em.wr(v, ["lam"])
            em.inc(v.tensor_sub(cells[0:1, 0:1], cells[0:1, 1:2],
                                cells[0:1, 2:3]), "sve")
            em.wait(v, "sve", em.cnt["sve"])
            em.inc(v.tensor_scalar_add(cells[0:1, 0:1], cells[0:1, 0:1],
                                       LAMBDA_INIT), "sve")
            em.did_wr(["lam"], "sve")
        em.sec("vector", init_lam_v2)

        stkAB = ExitStack()
        stkAW = ExitStack()
        xnT = sb("xnT", [128, NDC * T], BF16, stkAB)
        attnT = sb("attnT", [128, H * T], BF16, stkAB)
        xf32 = sb("xf32", [128, 2 * D], F32, stkAW)
        xnb = sb("xnb", [128, 2 * D], BF16, stkAW)
        junk = sb("junk", [128, D], F32, stkAW)

        em.tfence()
        with nc.psum_tensor("psTA", [128, 2048], BF16) as psTA:
            for tt in (range(cfg.a_tiles) if cfg.phases >= 1 else []):
                par = tt % 2
                dsem = "dxa0" if par == 0 else "dxa1"
                xr = "xf.%d" % par
                xnr = "xnb.%d" % par
                xs = xf32[:, par * D:(par + 1) * D]
                xns = xnb[:, par * D:(par + 1) * D]

                if cfg.a_steps >= 1:
                    em.grp("sync", dsem, [], [xr],
                           lambda sy, tt=tt, xs=xs: sy.dma_start(
                               out=xs, in_=x_d[tt * 128:(tt + 1) * 128, :]),
                           dmak=True)
                if cfg.a_steps >= 2:
                    em.grp("vector", "sve", [xr], ["scol.sum"],
                           lambda v, xs=xs: v.reduce_sum(
                               out=scol[:, 0:1], in_=xs, axis=AX.X))
                if cfg.a_steps >= 3:
                    em.grp("scalar", "ssc", [xr], ["junk"],
                           lambda s, xs=xs: s.activation(
                               out=junk[:], in_=xs, func=AF.Square))
                if cfg.a_steps >= 4:
                    em.grp("vector", "sve", ["junk"], ["scol.sq"],
                           lambda v: v.reduce_sum(out=scol[:, 1:2],
                                                  in_=junk[:], axis=AX.X))

                if cfg.a_steps < 5:
                    continue

                def a_ln_v(v):
                    em.rd(v, ["scol.sum", "scol.sq"])
                    em.wr(v, ["scol.mv"])
                    em.inc(v.tensor_scalar_mul(scol[:, 2:3], scol[:, 0:1],
                                               1.0 / D), "sve")
                    em.wait(v, "sve", em.cnt["sve"])
                    em.inc(v.tensor_mul(scol[:, 3:4], scol[:, 2:3],
                                        scol[:, 2:3]), "sve")
                    em.wait(v, "sve", em.cnt["sve"])
                    em.inc(v.scalar_tensor_tensor(
                        out=scol[:, 4:5], in0=scol[:, 1:2], scalar=1.0 / D,
                        in1=scol[:, 3:4], op0=ALU.mult, op1=ALU.subtract),
                        "sve")
                    em.did_wr(["scol.mv"], "sve")
                em.sec("vector", a_ln_v)

                if cfg.a_steps < 6:
                    continue

                def a_ln_s(s, tt=tt):
                    em.rd(s, ["scol.mv"])
                    em.wr(s, ["lni", "scol.lnv"])
                    em.inc(s.activation(out=scol[:, 5:6], in_=scol[:, 4:5],
                                        func=AF.Ln, bias=epscol[:]), "ssc")
                    em.wait(s, "ssc", em.cnt["ssc"])
                    em.inc(s.activation(out=lni[:, tt:tt + 1],
                                        in_=scol[:, 5:6], func=AF.Exp,
                                        scale=-0.5), "ssc")
                    em.did_rd(["scol.mv"], "ssc")
                    em.did_wr(["lni", "scol.lnv"], "ssc")
                em.sec("scalar", a_ln_s)

                if cfg.a_steps < 7:
                    continue
                em.grp("vector", "sve", ["lni", "scol.mv"], ["lnb2"],
                       lambda v, tt=tt: v.scalar_tensor_tensor(
                           out=lnb2[:, tt:tt + 1], in0=scol[:, 2:3],
                           scalar=-1.0, in1=lni[:, tt:tt + 1],
                           op0=ALU.mult, op1=ALU.mult))

                if cfg.a_steps < 8:
                    continue
                em.grp("vector", "sve", [xr, "lni", "lnb2"], [xnr],
                       lambda v, tt=tt, xs=xs, xns=xns: v.tensor_scalar(
                           out=xns, in0=xs, scalar1=lni[:, tt:tt + 1],
                           scalar2=lnb2[:, tt:tt + 1], op0=ALU.mult,
                           op1=ALU.add))

                if cfg.a_steps < 9:
                    continue
                for cb in range(NDC // 4):
                    pr = "psTA.%d" % (cb % 2)
                    po = (cb % 2) * 1024

                    def a_tr(t, cb=cb, po=po, xns=xns):
                        for ci in range(4):
                            c = cb * 4 + ci
                            ins = t.transpose(
                                psTA[:, po + ci * 128:po + (ci + 1) * 128],
                                xns[:, c * 128:(c + 1) * 128], eye[:])
                        return ins
                    em.grp("tensor", "ste", [xnr, "eye"], [pr], a_tr)

                    def a_trev(v, tt=tt, cb=cb, po=po):
                        for ci in range(4):
                            c = cb * 4 + ci
                            ins = v.tensor_copy(
                                out=xnT[:, c * T + tt * 128:
                                        c * T + (tt + 1) * 128],
                                in_=psTA[:, po + ci * 128:po + (ci + 1) * 128])
                        return ins
                    em.grp("vector", "sve", [pr], ["xnT"], a_trev)

        i32buf = sb("i32buf", [128, D], I32, stkAW)
        wr32 = sb("wr32", [128, D], F32, stkAW)
        ac4 = sb("ac4", [128, D // 512], F32, stkAW)
        h4 = sb("h4", [128, D // 512], I32, stkAW)
        l4 = sb("l4", [128, D // 512], F32, stkAW)
        hlr = sb("hlr", [128, 4], F32, stkAW)
        wrows = sb("wrows", [1, 512], F32, stkAW)
        hp2 = sb("hp2", [1, 128], I32, stkAW)
        hlo = sb("hlo", [1, 128], F32, stkAW)

        def wprep(w_src, wq_dst, R, C, mcell, mcname, mbvec, tag):
            NP_C = C // D
            npieces = (R // 128) * NP_C
            assert npieces <= 128
            SC = float(2.0 ** 17)
            ISC = float(2.0 ** -17)
            NINV = 1.0 / (R * C)
            assert (R * C) & (R * C - 1) == 0

            with nc.psum_tensor("psW_" + tag, [128, 8], F32) as psW:
                em.tfence()
                for pi in range(npieces):
                    rr, cc = divmod(pi, NP_C)
                    par = pi % 2
                    dsem = "dxa0" if par == 0 else "dxa1"
                    xr = "xf.%d" % par
                    xs = xf32[:, par * D:(par + 1) * D]

                    em.grp("sync", dsem, [], [xr],
                           lambda sy, rr=rr, cc=cc, xs=xs: sy.dma_start(
                               out=xs, in_=w_src[rr * 128:(rr + 1) * 128,
                                                 cc * D:(cc + 1) * D]),
                           dmak=True)

                    em.grp("scalar", "ssc", [xr], ["junk"],
                           lambda s, xs=xs: s.activation(
                               out=junk[:], in_=xs, func=AF.Abs))

                    def wp_sum(v, pi=pi):
                        em.rd(v, ["junk"])
                        em.wr(v, ["i32buf", "wr32", "ac4", "h4", "l4",
                                  "hlr"])
                        em.inc(v.tensor_scalar(out=i32buf[:], in0=junk[:],
                                               scalar1=SC, scalar2=None,
                                               op0=ALU.mult), "sve")
                        em.wait(v, "sve", em.cnt["sve"])
                        em.inc(v.scalar_tensor_tensor(
                            out=wr32[:], in0=i32buf[:], scalar=ISC,
                            in1=junk[:], op0=ALU.mult, op1=ALU.subtract),
                            "sve")
                        for ch in range(D // 512):
                            em.inc(v.reduce_sum(
                                out=ac4[:, ch:ch + 1],
                                in_=i32buf[:, ch * 512:(ch + 1) * 512],
                                axis=AX.X), "sve")
                        em.wait(v, "sve", em.cnt["sve"])
                        em.inc(v.reduce_sum(out=hlr[:, 2:3], in_=wr32[:],
                                            axis=AX.X), "sve")
                        em.inc(v.tensor_scalar(out=h4[:], in0=ac4[:],
                                               scalar1=float(2.0 ** -10),
                                               scalar2=None, op0=ALU.mult),
                               "sve")
                        em.wait(v, "sve", em.cnt["sve"])
                        em.inc(v.scalar_tensor_tensor(
                            out=l4[:], in0=h4[:], scalar=-1024.0,
                            in1=ac4[:], op0=ALU.mult, op1=ALU.add), "sve")
                        em.wait(v, "sve", em.cnt["sve"])
                        em.inc(v.reduce_sum(out=hlr[:, 0:1], in_=h4[:],
                                            axis=AX.X), "sve")
                        em.inc(v.reduce_sum(out=hlr[:, 1:2], in_=l4[:],
                                            axis=AX.X), "sve")
                        em.did_rd(["junk"], "sve")
                        em.did_wr(["i32buf", "wr32", "ac4", "h4", "l4",
                                   "hlr"], "sve")
                    em.sec("vector", wp_sum)

                    def wp_mm(t, pi=pi):
                        for k in range(3):
                            ins = t.matmul(psW[0:1, k:k + 1],
                                           hlr[:, k:k + 1], ones_colf[:],
                                           start=True, stop=True)
                        return ins
                    em.grp("tensor", "ste", ["hlr", "ones"], ["psW"], wp_mm)

                    def wp_ev(v, pi=pi):
                        em.rd(v, ["psW"])
                        em.wr(v, ["wrows"])
                        for k in range(3):
                            ins = v.tensor_copy(
                                out=wrows[0:1, k * 128 + pi:
                                          k * 128 + pi + 1],
                                in_=psW[0:1, k:k + 1])
                        em.inc(ins, "sve")
                        em.did_rd(["psW"], "sve")
                        em.did_wr(["wrows"], "sve")
                    em.sec("vector", wp_ev)

                def wp_fin(v, npieces=npieces, mcell=mcell, tag=tag,
                           mcname=mcname):
                    em.rd(v, ["wrows"])
                    em.wr(v, ["hp2", "hlo", "meancells", "lamcells",
                              "lamexp", mcname, "cells.s" + tag])
                    np_ = npieces
                    em.inc(v.tensor_scalar(out=hp2[0:1, 0:np_],
                                           in0=wrows[0:1, 0:np_],
                                           scalar1=float(2.0 ** -10),
                                           scalar2=None, op0=ALU.mult),
                           "sve")
                    em.wait(v, "sve", em.cnt["sve"])
                    em.inc(v.scalar_tensor_tensor(
                        out=hlo[0:1, 0:np_], in0=hp2[0:1, 0:np_],
                        scalar=-1024.0, in1=wrows[0:1, 0:np_],
                        op0=ALU.mult, op1=ALU.add), "sve")
                    em.wait(v, "sve", em.cnt["sve"])
                    em.inc(v.reduce_sum(out=cells[0:1, 5:6],
                                        in_=hp2[0:1, 0:np_], axis=AX.X),
                           "sve")
                    em.inc(v.reduce_sum(out=cells[0:1, 6:7],
                                        in_=hlo[0:1, 0:np_], axis=AX.X),
                           "sve")
                    em.inc(v.reduce_sum(out=cells[0:1, 8:9],
                                        in_=wrows[0:1, 128:128 + np_],
                                        axis=AX.X), "sve")
                    em.inc(v.reduce_sum(out=cells[0:1, 9:10],
                                        in_=wrows[0:1, 256:256 + np_],
                                        axis=AX.X), "sve")
                    em.wait(v, "sve", em.cnt["sve"])
                    em.inc(v.tensor_scalar(out=cells[0:1, 10:11],
                                           in0=cells[0:1, 5:6],
                                           scalar1=(2.0 ** 20) * ISC * NINV,
                                           scalar2=None, op0=ALU.mult),
                           "sve")
                    em.inc(v.tensor_scalar(out=cells[0:1, 11:12],
                                           in0=cells[0:1, 6:7],
                                           scalar1=(2.0 ** 10) * ISC * NINV,
                                           scalar2=None, op0=ALU.mult),
                           "sve")
                    em.inc(v.tensor_scalar(out=cells[0:1, 12:13],
                                           in0=cells[0:1, 8:9],
                                           scalar1=ISC * NINV, scalar2=None,
                                           op0=ALU.mult), "sve")
                    em.inc(v.tensor_scalar(out=cells[0:1, 13:14],
                                           in0=cells[0:1, 9:10],
                                           scalar1=-NINV, scalar2=None,
                                           op0=ALU.mult), "sve")
                    em.wait(v, "sve", em.cnt["sve"])
                    em.inc(v.tensor_add(cells[0:1, 12:13],
                                        cells[0:1, 12:13],
                                        cells[0:1, 13:14]), "sve")
                    em.wait(v, "sve", em.cnt["sve"])
                    em.inc(v.tensor_add(cells[0:1, 11:12],
                                        cells[0:1, 11:12],
                                        cells[0:1, 12:13]), "sve")
                    em.wait(v, "sve", em.cnt["sve"])
                    em.inc(v.tensor_add(cells[0:1, 10:11],
                                        cells[0:1, 10:11],
                                        cells[0:1, 11:12]), "sve")
                    em.wait(v, "sve", em.cnt["sve"])
                    em.inc(v.tensor_scalar(out=mcell, in0=cells[0:1, 10:11],
                                           scalar1=1e-5, scalar2=None,
                                           op0=ALU.max), "sve")
                    em.wait(v, "sve", em.cnt["sve"])
                    em.inc(v.reciprocal(out=cells[0:1, 7:8], in_=mcell),
                           "sve")
                    em.wait(v, "sve", em.cnt["sve"])
                    em.inc(v.tensor_mul(cells[0:1, 13:14], mcell,
                                        cells[0:1, 7:8]), "sve")
                    em.wait(v, "sve", em.cnt["sve"])
                    em.inc(v.tensor_scalar_add(cells[0:1, 13:14],
                                               cells[0:1, 13:14], -1.0),
                           "sve")
                    em.wait(v, "sve", em.cnt["sve"])
                    em.inc(v.tensor_mul(cells[0:1, 13:14],
                                        cells[0:1, 13:14],
                                        cells[0:1, 7:8]), "sve")
                    em.wait(v, "sve", em.cnt["sve"])
                    em.inc(v.tensor_sub(cells[0:1, 7:8], cells[0:1, 7:8],
                                        cells[0:1, 13:14]), "sve")
                    em.did_rd(["wrows"], "sve")
                    em.did_wr(["hp2", "hlo", "meancells", "lamcells",
                               "lamexp", mcname, "cells.s" + tag], "sve")
                em.sec("vector", wp_fin)

                def w_bc_t(t, mcell=mcell):
                    ins = t.matmul(psW[:, 4:5], ones_rowf[0:1, :],
                                   cells[0:1, 7:8], start=True, stop=True)
                    ins = t.matmul(psW[:, 5:6], ones_rowf[0:1, :], mcell,
                                   start=True, stop=True)
                    return ins
                em.grp("tensor", "ste", [mcname, "cells.s" + tag, "ones"],
                       ["psW2"], w_bc_t)

                def w_bc_v(v, mbvec=mbvec):
                    em.rd(v, ["psW2"])
                    em.wr(v, ["srecip." + tag, "mb." + tag])
                    em.inc(v.tensor_copy(out=scol[:, 6:7], in_=psW[:, 4:5]),
                           "sve")
                    em.inc(v.tensor_copy(out=mbvec, in_=psW[:, 5:6]), "sve")
                    em.did_rd(["psW2"], "sve")
                    em.did_wr(["srecip." + tag, "mb." + tag], "sve")
                em.sec("vector", w_bc_v)

                for pi in range(npieces):
                    rr, cc = divmod(pi, NP_C)
                    par = pi % 2
                    dsem = "dxa0" if par == 0 else "dxa1"
                    xr = "xf.%d" % par
                    xnr = "xnb.%d" % par
                    xs = xf32[:, par * D:(par + 1) * D]
                    xns = xnb[:, par * D:(par + 1) * D]

                    em.grp("sync", dsem, [], [xr],
                           lambda sy, rr=rr, cc=cc, xs=xs: sy.dma_start(
                               out=xs, in_=w_src[rr * 128:(rr + 1) * 128,
                                                 cc * D:(cc + 1) * D]),
                           dmak=True)

                    def wq_q(v, xs=xs, xr=xr, xnr=xnr, tag=tag):
                        em.rd(v, [xr, "srecip." + tag])
                        em.wr(v, ["wr32", "junk", xnr])
                        em.inc(v.tensor_scalar(out=wr32[:], in0=xs,
                                               scalar1=scol[:, 6:7],
                                               scalar2=None, op0=ALU.mult),
                               "sve")
                        em.wait(v, "sve", em.cnt["sve"])
                        em.inc(v.tensor_scalar(out=junk[:], in0=wr32[:],
                                               scalar1=-0.5, scalar2=None,
                                               op0=ALU.is_le), "sve")
                        em.wait(v, "sve", em.cnt["sve"])
                        em.inc(v.scalar_tensor_tensor(
                            out=xns, in0=wr32[:], scalar=0.5, in1=junk[:],
                            op0=ALU.is_ge, op1=ALU.subtract), "sve")
                        em.did_rd([xr, "srecip." + tag], "sve")
                        em.did_wr(["wr32", "junk", xnr], "sve")
                    em.sec("vector", wq_q)

                    em.grp("sync", "dstore", [xnr], [],
                           lambda sy, rr=rr, cc=cc, xns=xns: sy.dma_start(
                               out=wq_dst[rr * 128:(rr + 1) * 128,
                                          cc * D:(cc + 1) * D], in_=xns),
                           dmak=True)

        if cfg.phases >= 2:
            wprep(w1_d, w1q_d, F, D, cells[0:1, 3:4], "m1c", m1b[:, 0:1],
                  "w1")
            wprep(w2_d, w2q_d, D, F, cells[0:1, 4:5], "m2c", m2b[:, 0:1],
                  "w2")

        stkAW.close()
        em.fence()
        stkB = ExitStack()
        wqb = sb("wqb", [128, 2 * D], BF16, stkB)
        wkb = sb("wkb", [128, 2 * D], BF16, stkB)
        wvb = sb("wvb", [128, 2 * D], BF16, stkB)
        Qc = sb("Qc", [128, T], BF16, stkB)
        Kc = sb("Kc", [128, T], BF16, stkB)
        Vt = sb("Vt", [128, T], BF16, stkB)
        Vk = sb("Vk", [128, T], BF16, stkB)
        sqQK = sb("sqQK", [128, T], BF16, stkB)
        P0T = sb("P0T", [128, 1024], BF16, stkB)
        P1T = sb("P1T", [128, 1024], BF16, stkB)
        bcol = sb("bcol", [128, 2], F32, stkB)
        maxc = sb("maxc", [1, 64], F32, stkB)
        s01sb = sb("s01sb", [1, 1024], F32, stkB)
        r1row = sb("r1row", [1, 512], F32, stkB)
        s0sq = sb("s0sq", [1, 512], F32, stkB)
        erow = sb("erow", [1, 512], F32, stkB)
        crow = sb("crow", [1, 512], BF16, stkB)
        frow = sb("frow", [1, 512], BF16, stkB)
        Csb = sb("Csb", [128, 512], BF16, stkB)
        t1b = sb("t1b", [128, 512], F32, stkB)
        apb = sb("apb", [128, 512], F32, stkB)
        sqb = sb("sqb", [128, 512], BF16, stkB)

        QKSC = float(8.0 ** -0.5)
        LNC = math.log((1.0 - LAMBDA_INIT) * math.sqrt(128.0))
        nbq = T // 512

        for h in (range(H) if cfg.phases >= 3 else []):
            par = h % 2
            dsem = "dwq0" if par == 0 else "dwq1"
            wres = "whd.%d" % par

            def b_wdma(g, h=h, par=par, dsem=dsem):
                for wdst, wsrc in ((wqb, wq_d), (wkb, wk_d), (wvb, wv_d)):
                    dst = wdst[:, par * D:(par + 1) * D].rearrange(
                        "p (c j) -> p c j", j=128)
                    src = wsrc[:, h * 128:(h + 1) * 128].rearrange(
                        "(c p) j -> p c j", p=128)
                    em.inc(g.dma_start(out=dst, in_=src), dsem, 16)
                return None
            em.grp("gpsimd", dsem, [], [wres], b_wdma, dmak=True)

            em.tfence()
            with ExitStack() as pstk:
                psQ = pstk.enter_context(
                    nc.psum_tensor("psQ_%d" % h, [128, T], F32))
                psK = pstk.enter_context(
                    nc.psum_tensor("psK_%d" % h, [128, T], F32))

                def b_projqk(t, par=par):
                    for wsb, ps in ((wqb, psQ), (wkb, psK)):
                        for nb in range(T // 512):
                            for c in range(NDC):
                                ins = t.matmul(
                                    ps[:, nb * 512:(nb + 1) * 512],
                                    wsb[:, par * D + c * 128:
                                        par * D + (c + 1) * 128],
                                    xnT[:, c * T + nb * 512:
                                        c * T + (nb + 1) * 512],
                                    start=(c == 0), stop=(c == NDC - 1))
                    return ins
                em.grp("tensor", "ste", [wres, "xnT"], ["psQ", "psK"],
                       b_projqk)

                def b_evqk(v):
                    ins = v.tensor_scalar_mul(Qc[:], psQ[:], QKSC)
                    ins = v.tensor_scalar_mul(Kc[:], psK[:], QKSC)
                    return ins
                em.grp("vector", "sve", ["psQ", "psK"], ["Qc", "Kc"], b_evqk)

                def b_projv(t, par=par):
                    for nb in range(T // 512):
                        for c in range(NDC):
                            ins = t.matmul(
                                psQ[:, nb * 512:(nb + 1) * 512],
                                wvb[:, par * D + c * 128:
                                    par * D + (c + 1) * 128],
                                xnT[:, c * T + nb * 512:c * T + (nb + 1) * 512],
                                start=(c == 0), stop=(c == NDC - 1))
                    return ins
                em.grp("tensor", "ste", [wres, "xnT"], ["psQ"], b_projv)

                em.grp("vector", "sve", ["psQ"], ["Vt"],
                       lambda v: v.tensor_copy(out=Vt[:], in_=psQ[:]))

            em.tfence()
            with ExitStack() as pstk:
                psV = pstk.enter_context(
                    nc.psum_tensor("psV_%d" % h, [128, 2048], BF16))
                psN = pstk.enter_context(
                    nc.psum_tensor("psN_%d" % h, [1, 1024], F32))
                psB = pstk.enter_context(
                    nc.psum_tensor("psB_%d" % h, [128, 2], F32))

                for cb in range(NTT // 4):
                    pr = "psV.%d" % (cb % 2)
                    po = (cb % 2) * 1024

                    def b_vtr(t, cb=cb, po=po):
                        for ci in range(4):
                            c = cb * 4 + ci
                            ins = t.transpose(
                                psV[:, po + ci * 128:po + (ci + 1) * 128],
                                Vt[:, c * 128:(c + 1) * 128], eye[:])
                        return ins
                    em.grp("tensor", "ste", ["Vt", "eye"], [pr], b_vtr)

                    em.grp("vector", "sve", [pr], ["Vk"],
                           lambda v, cb=cb, po=po: v.tensor_copy(
                               out=Vk[:, cb * 512:(cb + 1) * 512],
                               in_=psV[:, po:po + 512]))

                for src_name, base in (("q", 0), ("k", 2 * nbq)):
                    srcb = Qc if src_name == "q" else Kc
                    em.grp("vector", "sve", ["Qc" if src_name == "q"
                                             else "Kc"], ["sqQK"],
                           lambda v, srcb=srcb: v.tensor_mul(
                               sqQK[:], srcb[:], srcb[:]))
                    for hh in range(2):
                        hs = slice(hh * 64, hh * 64 + 64)
                        for nb in range(nbq):
                            pn = "psN.%d" % hh
                            em.grp("tensor", "ste", ["sqQK", "ones"], [pn],
                                   lambda t, hh=hh, nb=nb, hs=hs: t.matmul(
                                       psN[0:1, hh * 512:hh * 512 + 512],
                                       ones_col[hs, 0:1],
                                       sqQK[hs, nb * 512:(nb + 1) * 512],
                                       start=True, stop=True))
                            em.grp("vector", "sve", [pn], ["maxc"],
                                   lambda v, hh=hh, nb=nb, base=base:
                                   v.reduce_max(
                                       out=maxc[0:1, base + hh * nbq + nb:
                                                base + hh * nbq + nb + 1],
                                       in_=psN[0:1, hh * 512:hh * 512 + 512],
                                       axis=AX.X))

                def b_bnd_v(v):
                    em.rd(v, ["maxc"])
                    em.wr(v, ["bndcells"])
                    for hh in range(2):
                        em.inc(v.reduce_max(
                            out=cells[0:1, 8 + hh:9 + hh],
                            in_=maxc[0:1, hh * nbq:(hh + 1) * nbq],
                            axis=AX.X), "sve")
                        em.inc(v.reduce_max(
                            out=cells[0:1, 10 + hh:11 + hh],
                            in_=maxc[0:1, 2 * nbq + hh * nbq:
                                     2 * nbq + (hh + 1) * nbq],
                            axis=AX.X), "sve")
                    em.wait(v, "sve", em.cnt["sve"])
                    for hh in range(2):
                        ins = v.tensor_add(cells[0:1, 12 + hh:13 + hh],
                                           cells[0:1, 8 + hh:9 + hh],
                                           cells[0:1, 10 + hh:11 + hh])
                        em.inc(ins, "sve")
                    em.did_rd(["maxc"], "sve")
                    em.did_wr(["bndcells"], "sve")
                em.sec("vector", b_bnd_v)

                def b_bnd_t(t):
                    for hh in range(2):
                        ins = t.matmul(psB[:, hh:hh + 1], ones_rowf[0:1, :],
                                       cells[0:1, 12 + hh:13 + hh],
                                       start=True, stop=True)
                    return ins
                em.grp("tensor", "ste", ["bndcells", "ones"], ["psB"],
                       b_bnd_t)

                def b_bnd_s(s):
                    for hh in range(2):
                        ins = s.activation(out=bcol[:, hh:hh + 1],
                                           in_=psB[:, hh:hh + 1],
                                           func=AF.Copy, scale=-0.5)
                    return ins
                em.grp("scalar", "ssc", ["psB"], ["bcol"], b_bnd_s)

            em.tfence()
            with ExitStack() as pstk:
                psS0 = pstk.enter_context(
                    nc.psum_tensor("psS0_%d" % h, [128, 1024], F32))
                psS1 = pstk.enter_context(
                    nc.psum_tensor("psS1_%d" % h, [128, 1024], F32))
                psA0 = pstk.enter_context(
                    nc.psum_tensor("psA0_%d" % h, [128, 512], F32))
                psA1 = pstk.enter_context(
                    nc.psum_tensor("psA1_%d" % h, [128, 512], F32))
                psRW = pstk.enter_context(
                    nc.psum_tensor("psRW_%d" % h, [1, 1024], F32))

                for qb in range(NQB):
                    jmax = (qb + 1) * JB - 1
                    for j in range(jmax + 1):
                        jp = j % 2
                        diag = (j >= qb * JB)
                        off = 128 * (j - qb * JB) if diag else 0
                        q0 = qb * QB + off
                        Nn = QB - off
                        rS0 = "psS0.%d" % jp
                        rS1 = "psS1.%d" % jp
                        rP0 = "P0T.%d" % jp
                        rP1 = "P1T.%d" % jp

                        def att_s(t, j=j, jp=jp, off=off, q0=q0, Nn=Nn):
                            t.matmul(
                                psS0[:, jp * 512 + off:(jp + 1) * 512],
                                Kc[0:64, j * 128:(j + 1) * 128],
                                Qc[0:64, q0:q0 + Nn], start=True, stop=True)
                            return t.matmul(
                                psS1[:, jp * 512 + off:(jp + 1) * 512],
                                Kc[64:128, j * 128:(j + 1) * 128],
                                Qc[64:128, q0:q0 + Nn], start=True, stop=True)
                        em.grp("tensor", "ste", ["Qc", "Kc"], [rS0, rS1],
                               att_s)

                        def att_exp(s, jp=jp, off=off):
                            s.activation(
                                out=P0T[:, jp * 512 + off:(jp + 1) * 512],
                                in_=psS0[:, jp * 512 + off:(jp + 1) * 512],
                                func=AF.Exp, bias=bcol[:, 0:1])
                            return s.activation(
                                out=P1T[:, jp * 512 + off:(jp + 1) * 512],
                                in_=psS1[:, jp * 512 + off:(jp + 1) * 512],
                                func=AF.Exp, bias=bcol[:, 1:2])
                        em.grp("scalar", "ssc", [rS0, rS1, "bcol"],
                               [rP0, rP1], att_exp)

                        if diag:
                            def att_mask(v, jp=jp, off=off):
                                v.tensor_mul(
                                    P0T[:, jp * 512 + off:jp * 512 + off + 128],
                                    P0T[:, jp * 512 + off:jp * 512 + off + 128],
                                    tri[:])
                                return v.tensor_mul(
                                    P1T[:, jp * 512 + off:jp * 512 + off + 128],
                                    P1T[:, jp * 512 + off:jp * 512 + off + 128],
                                    tri[:])
                            em.grp("vector", "sve", [rP0, rP1, "tri"],
                                   [rP0, rP1], att_mask)

                        def att_av(t, j=j, jp=jp, off=off, qb=qb, jmax=jmax):
                            st = (j == 0)
                            sp = (j == jmax)
                            t.matmul(psA0[:, off:512],
                                     Vk[:, j * 128:(j + 1) * 128],
                                     P0T[:, jp * 512 + off:(jp + 1) * 512],
                                     start=st, stop=sp)
                            t.matmul(psA1[:, off:512],
                                     Vk[:, j * 128:(j + 1) * 128],
                                     P1T[:, jp * 512 + off:(jp + 1) * 512],
                                     start=st, stop=sp)
                            t.matmul(psRW[0:1, off:512], ones_col[:, 0:1],
                                     P0T[:, jp * 512 + off:(jp + 1) * 512],
                                     start=st, stop=sp)
                            return t.matmul(
                                psRW[0:1, 512 + off:1024], ones_col[:, 0:1],
                                P1T[:, jp * 512 + off:(jp + 1) * 512],
                                start=st, stop=sp)
                        em.grp("tensor", "ste",
                               [rP0, rP1, "Vk", "ones"],
                               ["psA0", "psA1", "psRW"], att_av)

                    em.grp("scalar", "ssc", ["psRW"], ["s01sb"],
                           lambda s: s.activation(out=s01sb[:],
                                                  in_=psRW[0:1, :],
                                                  func=AF.Copy))

                    def ep_v1(v):
                        em.rd(v, ["s01sb", "lam"])
                        em.wr(v, ["r1row", "s0sq", "crow"])
                        em.inc(v.reciprocal(out=r1row[:],
                                            in_=s01sb[0:1, 512:1024]), "sve")
                        em.wait(v, "sve", em.cnt["sve"])
                        em.inc(v.tensor_mul(s0sq[:], r1row[:],
                                            s01sb[0:1, 512:1024]), "sve")
                        em.wait(v, "sve", em.cnt["sve"])
                        em.inc(v.tensor_scalar(out=s0sq[:], in0=s0sq[:],
                                               scalar1=-1.0, scalar2=2.0,
                                               op0=ALU.mult, op1=ALU.add),
                               "sve")
                        em.wait(v, "sve", em.cnt["sve"])
                        em.inc(v.tensor_mul(r1row[:], r1row[:], s0sq[:]),
                               "sve")
                        em.wait(v, "sve", em.cnt["sve"])
                        em.inc(v.scalar_tensor_tensor(
                            out=crow[:], in0=r1row[:], scalar=lam11,
                            in1=s01sb[0:1, 0:512], op0=ALU.mult,
                            op1=ALU.mult), "sve")
                        em.did_rd(["s01sb", "lam"], "sve")
                        em.did_wr(["r1row", "s0sq", "crow"], "sve")
                    em.sec("vector", ep_v1)

                    em.grp("tensor", "ste", ["crow", "ones"], ["psS0.1"],
                           lambda t: t.matmul(psS0[:, 512:1024],
                                              ones_rowb[0:1, :], crow[:],
                                              start=True, stop=True))

                    em.grp("scalar", "ssc", ["psS0.1"], ["Csb"],
                           lambda s: s.activation(out=Csb[:],
                                                  in_=psS0[:, 512:1024],
                                                  func=AF.Copy))

                    def ep_v2(v):
                        em.rd(v, ["Csb", "psA0", "psA1"])
                        em.wr(v, ["t1b", "apb", "sqb"])
                        em.inc(v.tensor_mul(t1b[:], psA1[:], Csb[:]), "sve")
                        em.wait(v, "sve", em.cnt["sve"])
                        em.inc(v.tensor_sub(apb[:], psA0[:], t1b[:]), "sve")
                        em.wait(v, "sve", em.cnt["sve"])
                        em.inc(v.tensor_mul(sqb[:], apb[:], apb[:]), "sve")
                        em.did_rd(["Csb", "psA0", "psA1"], "sve")
                        em.did_wr(["t1b", "apb", "sqb"], "sve")
                    em.sec("vector", ep_v2)

                    em.grp("tensor", "ste", ["sqb", "ones"], ["psRW"],
                           lambda t: t.matmul(psRW[0:1, 0:512],
                                              ones_col[:, 0:1], sqb[:],
                                              start=True, stop=True))

                    def ep_v3(v):
                        em.rd(v, ["psRW", "s01sb"])
                        em.wr(v, ["s0sq", "erow"])
                        em.inc(v.tensor_mul(s0sq[:], s01sb[0:1, 0:512],
                                            s01sb[0:1, 0:512]), "sve")
                        em.wait(v, "sve", em.cnt["sve"])
                        em.inc(v.scalar_tensor_tensor(
                            out=erow[:], in0=s0sq[:], scalar=128.0 * 1e-5,
                            in1=psRW[0:1, 0:512], op0=ALU.mult, op1=ALU.add),
                            "sve")
                        em.did_rd(["psRW", "s01sb"], "sve")
                        em.did_wr(["s0sq", "erow"], "sve")
                    em.sec("vector", ep_v3)

                    def ep_s3(s):
                        em.rd(s, ["erow"])
                        em.wr(s, ["erow", "frow"])
                        em.inc(s.activation(out=erow[:], in_=erow[:],
                                            func=AF.Ln), "ssc")
                        em.wait(s, "ssc", em.cnt["ssc"])
                        em.inc(s.activation(out=frow[:], in_=erow[:],
                                            func=AF.Exp, scale=-0.5,
                                            bias=cells[0:1, 15:16]), "ssc")
                        em.did_wr(["erow", "frow"], "ssc")
                    em.sec("scalar", ep_s3)

                    em.grp("tensor", "ste", ["frow", "ones"], ["psS1.1"],
                           lambda t: t.matmul(psS1[:, 512:1024],
                                              ones_rowb[0:1, :], frow[:],
                                              start=True, stop=True))

                    em.grp("vector", "sve", ["psS1.1", "apb"], ["attnT"],
                           lambda v, h=h, qb=qb: v.tensor_mul(
                               attnT[:, h * T + qb * QB:h * T + (qb + 1) * QB],
                               apb[:], psS1[:, 512:1024]))

        stkB.close()
        em.fence()
        stkB2 = ExitStack()
        wob = sb("wob", [128, 2 * NDC * 512], BF16, stkB2)
        wotile = sb("wotile", [128, 2 * 512], BF16, stkB2)

        em.tfence()
        with nc.psum_tensor("psO", [128, 1024], F32) as psO:
            for dc in (range(NDOC) if cfg.phases >= 4 else []):
                par = dc % 2
                dsem = "dwoB0" if par == 0 else "dwoB1"
                wres = "wob.%d" % par

                def b2_dma(g, dc=dc, par=par):
                    dst = wob[:, par * NDC * 512:
                              (par + 1) * NDC * 512].rearrange(
                        "p (c j) -> p c j", j=512)
                    src = wo_d[:, dc * 512:(dc + 1) * 512].rearrange(
                        "(c p) j -> p c j", p=128)
                    return g.dma_start(out=dst, in_=src)
                em.grp("gpsimd", dsem, [], [wres], b2_dma, dmak=True)

                for tt in range(NTT):
                    pp = tt % 2
                    pr = "psO.%d" % pp
                    wtr = "wotile.%d" % pp

                    def b2_mm(t, tt=tt, pp=pp, par=par):
                        for c in range(H):
                            ins = t.matmul(
                                psO[:, pp * 512:(pp + 1) * 512],
                                attnT[:, c * T + tt * 128:
                                      c * T + (tt + 1) * 128],
                                wob[:, par * NDC * 512 + c * 512:
                                    par * NDC * 512 + (c + 1) * 512],
                                start=(c == 0), stop=(c == H - 1))
                        return ins
                    em.grp("tensor", "ste", [wres, "attnT"], [pr], b2_mm)

                    em.grp("vector", "sve", [pr], [wtr],
                           lambda v, pp=pp: v.tensor_copy(
                               out=wotile[:, pp * 512:(pp + 1) * 512],
                               in_=psO[:, pp * 512:(pp + 1) * 512]))

                    em.grp("sync", "dstore", [wtr], [],
                           lambda sy, dc=dc, tt=tt, pp=pp: sy.dma_start(
                               out=wo_scr_d[tt * 128:(tt + 1) * 128,
                                            dc * 512:(dc + 1) * 512],
                               in_=wotile[:, pp * 512:(pp + 1) * 512]),
                           dmak=True)

        if cfg.debug:
            def dbg_dump(sy):
                em.rd(sy, ["xnT", "attnT", "lam", "lni", "lnb2"])
                em.inc(sy.dma_start(out=xnT_dbg[:], in_=xnT[:]), "dstore", 16)
                em.inc(sy.dma_start(out=attnT_dbg[:], in_=attnT[:]),
                       "dstore", 16)
                em.inc(sy.dma_start(out=cells_dbg[:], in_=cells[:]),
                       "dstore", 16)
                em.inc(sy.dma_start(out=lni_dbg[:, 0:cfg.NTT], in_=lni[:]),
                       "dstore", 16)
                em.inc(sy.dma_start(out=lni_dbg[:, cfg.NTT:], in_=lnb2[:]),
                       "dstore", 16)
                em.wait(sy, "dstore", em.cnt["dstore"])
            em.sec("sync", dbg_dump)
        stkB2.close()
        stkAB.close()
        em.fence()
        stkC = ExitStack()
        qxT = sb("qxT", [128, NDC * GT], BF16, stkC)
        qtile = sb("qtile", [128, D], BF16, stkC)
        xf32 = sb("xf32c", [128, 2 * D], F32, stkC)
        i32buf = sb("i32bufc", [128, 1024], I32, stkC)
        fbuf = sb("fbufc", [128, 1024], F32, stkC)
        rbuf = sb("rbufc", [128, 1024], F32, stkC)
        ebuf = sb("ebufc", [128, 1024], F32, stkC)
        hbuf = sb("hbuf", [128, GTT * F], BF16, stkC)
        hmax = sb("hmax", [128, GTT * NFC], F32, stkC)
        axc = sb("axc", [128, GTT], F32, stkC)
        sxq = sb("sxq", [128, GTT], F32, stkC)
        hscol = sb("hscol", [128, GTT], F32, stkC)
        shq = sb("shq", [128, GTT], F32, stkC)
        hs2 = sb("hs2", [128, GTT], F32, stkC)
        hqT = sb("hqT", [128, NFC128 * 256], BF16, stkC)
        w1t = sb("w1t", [128, 2 * NDC * 512], BF16, stkC)
        w2t = sb("w2t", [128, 2 * 1024], BF16, stkC)
        wotc = sb("wotc", [128, 2 * 512], BF16, stkC)
        xrest = sb("xrest", [128, 2 * 512], F32, stkC)
        tAb = sb("tAb", [128, 512], F32, stkC)
        outb = sb("outb", [128, 2 * 512], F32, stkC)

        dstore_final = em.cnt["dstore"]

        for g in (range(NG) if cfg.phases >= 5 else []):
            em.tfence()
            with nc.psum_tensor("psTq_%d" % g, [128, 2048], BF16) as psTq:
                for tl in range(GTT):
                    tt = g * GTT + tl
                    par = tt % 2
                    dsem = "dxa0" if par == 0 else "dxa1"
                    xr = "xf.%d" % par
                    xnr = "xnb.%d" % par
                    xs = xf32[:, par * D:(par + 1) * D]
                    xns = xs

                    em.grp("sync", dsem, [], [xr, xnr],
                           lambda sy, tt=tt, xs=xs: sy.dma_start(
                               out=xs, in_=x_d[tt * 128:(tt + 1) * 128, :]),
                           dmak=True)

                    em.grp("vector", "sve", [xr, "lni", "lnb2"], [xnr],
                           lambda v, tt=tt, xs=xs, xns=xns: v.tensor_scalar(
                               out=xns, in0=xs, scalar1=lni[:, tt:tt + 1],
                               scalar2=lnb2[:, tt:tt + 1], op0=ALU.mult,
                               op1=ALU.add))

                    def c_ax(v, tl=tl, xns=xns, xnr=xnr):
                        em.rd(v, [xnr, "mb.w1"])
                        em.wr(v, ["axc", "sxq", "hscol"])
                        em.inc(v.reduce_max(out=axc[:, tl:tl + 1], in_=xns,
                                            axis=AX.X,
                                            apply_absolute_value=True),
                               "sve")
                        em.wait(v, "sve", em.cnt["sve"])
                        em.inc(v.tensor_scalar_max(axc[:, tl:tl + 1],
                                                   axc[:, tl:tl + 1], 1e-5),
                               "sve")
                        em.wait(v, "sve", em.cnt["sve"])
                        em.inc(v.reciprocal(out=sxq[:, tl:tl + 1],
                                            in_=axc[:, tl:tl + 1]), "sve")
                        em.wait(v, "sve", em.cnt["sve"])
                        em.inc(v.tensor_mul(scol[:, 7:8], axc[:, tl:tl + 1],
                                            sxq[:, tl:tl + 1]), "sve")
                        em.wait(v, "sve", em.cnt["sve"])
                        em.inc(v.tensor_scalar(out=scol[:, 7:8],
                                               in0=scol[:, 7:8],
                                               scalar1=-1.0, scalar2=2.0,
                                               op0=ALU.mult, op1=ALU.add),
                               "sve")
                        em.wait(v, "sve", em.cnt["sve"])
                        em.inc(v.tensor_mul(sxq[:, tl:tl + 1],
                                            sxq[:, tl:tl + 1],
                                            scol[:, 7:8]), "sve")
                        em.wait(v, "sve", em.cnt["sve"])
                        em.inc(v.tensor_scalar_mul(sxq[:, tl:tl + 1],
                                                   sxq[:, tl:tl + 1], 127.0),
                               "sve")
                        em.inc(v.scalar_tensor_tensor(
                            out=hscol[:, tl:tl + 1], in0=axc[:, tl:tl + 1],
                            scalar=1.0 / 127.0, in1=m1b[:], op0=ALU.mult,
                            op1=ALU.mult), "sve")
                        em.did_rd([xnr, "mb.w1"], "sve")
                        em.did_wr(["axc", "sxq", "hscol", "scol.nt"], "sve")
                    em.sec("vector", c_ax)

                    for qh in range(D // 1024):
                        xh = xns[:, qh * 1024:(qh + 1) * 1024]
                        qth = qtile[:, qh * 1024:(qh + 1) * 1024]

                        def c_q(v, tl=tl, xh=xh, xnr=xnr):
                            em.rd(v, [xnr, "sxq"])
                            em.wr(v, ["i32buf"])
                            em.inc(v.tensor_scalar(
                                out=i32buf[:], in0=xh,
                                scalar1=sxq[:, tl:tl + 1], scalar2=None,
                                op0=ALU.mult), "sve")
                            em.did_rd([xnr, "sxq"], "sve")
                            em.did_wr(["i32buf"], "sve")
                        em.sec("vector", c_q)

                        em.grp("vector", "sve", ["i32buf"], ["f0c"],
                               lambda v: v.tensor_scalar(
                                   out=fbuf[:], in0=i32buf[:], scalar1=2.0,
                                   scalar2=None, op0=ALU.subtract))

                        def c_q2(v, tl=tl, xh=xh, qth=qth, xnr=xnr):
                            em.rd(v, [xnr, "sxq", "f0c"])
                            em.wr(v, ["rbuf", "ebuf", "f0c", "qtile"])
                            em.inc(v.scalar_tensor_tensor(
                                out=rbuf[:], in0=xh,
                                scalar=sxq[:, tl:tl + 1], in1=fbuf[:],
                                op0=ALU.mult, op1=ALU.subtract), "sve")
                            em.wait(v, "sve", em.cnt["sve"])
                            em.inc(v.scalar_tensor_tensor(
                                out=ebuf[:], in0=rbuf[:], scalar=0.5,
                                in1=fbuf[:], op0=ALU.is_gt, op1=ALU.add),
                                "sve")
                            em.wait(v, "sve", em.cnt["sve"])
                            em.inc(v.scalar_tensor_tensor(
                                out=fbuf[:], in0=rbuf[:], scalar=1.5,
                                in1=ebuf[:], op0=ALU.is_ge, op1=ALU.add),
                                "sve")
                            em.wait(v, "sve", em.cnt["sve"])
                            em.inc(v.scalar_tensor_tensor(
                                out=ebuf[:], in0=rbuf[:], scalar=2.5,
                                in1=fbuf[:], op0=ALU.is_gt, op1=ALU.add),
                                "sve")
                            em.wait(v, "sve", em.cnt["sve"])
                            em.inc(v.scalar_tensor_tensor(
                                out=qth, in0=rbuf[:], scalar=3.5,
                                in1=ebuf[:], op0=ALU.is_ge, op1=ALU.add),
                                "sve")
                            em.did_rd([xnr, "sxq"], "sve")
                            em.did_wr(["rbuf", "ebuf", "f0c", "qtile"],
                                      "sve")
                        em.sec("vector", c_q2)

                        if getattr(cfg, "dbgqx", None) == (g, tl):
                            def dbgx(sy, qh=qh, xh=xh, qth=qth, xnr=xnr):
                                em.rd(sy, ["i32buf", "rbuf", "qtile",
                                           "sxq", xnr])
                                em.inc(sy.dma_start(
                                    out=dxq_i32[:, qh * 1024:
                                                (qh + 1) * 1024],
                                    in_=i32buf[:]), "ddbg", 16)
                                em.inc(sy.dma_start(
                                    out=dxq_rp[:, qh * 1024:(qh + 1) * 1024],
                                    in_=rbuf[:]), "ddbg", 16)
                                em.inc(sy.dma_start(
                                    out=dxq_out[:, qh * 1024:
                                                (qh + 1) * 1024],
                                    in_=qth), "ddbg", 16)
                                em.inc(sy.dma_start(
                                    out=dxq_in[:, qh * 1024:(qh + 1) * 1024],
                                    in_=xh), "ddbg", 16)
                                em.inc(sy.dma_start(out=dxq_s[:],
                                                    in_=sxq[:, 0:4]),
                                       "ddbg", 16)
                                em.wait(sy, "ddbg", em.cnt["ddbg"])
                            em.sec("sync", dbgx)
                            em.sec("vector",
                                   lambda v: em.wait(v, "ddbg",
                                                     em.cnt["ddbg"]))

                    for cb in range(NDC // 4):
                        pr = "psTq.%d" % (cb % 2)
                        po = (cb % 2) * 1024

                        def c_qtr(t, cb=cb, po=po):
                            for ci in range(4):
                                c = cb * 4 + ci
                                ins = t.transpose(
                                    psTq[:, po + ci * 128:po + (ci + 1) * 128],
                                    qtile[:, c * 128:(c + 1) * 128], eye[:])
                            return ins
                        em.grp("tensor", "ste", ["qtile", "eye"], [pr],
                               c_qtr)

                        def c_qtrev(v, tl=tl, cb=cb, po=po):
                            for ci in range(4):
                                c = cb * 4 + ci
                                ins = v.tensor_copy(
                                    out=qxT[:, c * GT + tl * 128:
                                            c * GT + (tl + 1) * 128],
                                    in_=psTq[:, po + ci * 128:po + (ci + 1) * 128])
                            return ins
                        em.grp("vector", "sve", [pr], ["qxT"], c_qtrev)

            em.tfence()
            with nc.psum_tensor("psH_%d" % g, [128, 1024], F32) as psH:
                for fc in range(NFC):
                    par = fc % 2
                    dsem = "dw1t0" if par == 0 else "dw1t1"
                    wres = "w1t.%d" % par

                    def c_w1dma(sy, fc=fc, par=par, wres=wres, dsem=dsem):
                        em.wr(sy, [wres])
                        em.wait(sy, "dstore", dstore_final)
                        for c in range(NDC):
                            em.inc(sy.dma_start(
                                out=w1t[:, par * NDC * 512 + c * 512:
                                        par * NDC * 512 + (c + 1) * 512],
                                in_=w1q_d[fc * 512:(fc + 1) * 512,
                                          c * 128:(c + 1) * 128],
                                transpose=True), dsem, 16)
                        em.did_wr([wres], dsem)
                    em.sec("sync", c_w1dma)

                    for tl in range(GTT):
                        pp = (fc * GTT + tl) % 2
                        pr = "psH.%d" % pp

                        def c_mm1(t, fc=fc, tl=tl, pp=pp, par=par):
                            for c in range(NDC):
                                ins = t.matmul(
                                    psH[:, pp * 512:(pp + 1) * 512],
                                    qxT[:, c * GT + tl * 128:
                                        c * GT + (tl + 1) * 128],
                                    w1t[:, par * NDC * 512 + c * 512:
                                        par * NDC * 512 + (c + 1) * 512],
                                    start=(c == 0), stop=(c == NDC - 1))
                            return ins
                        em.grp("tensor", "ste", [wres, "qxT"], [pr], c_mm1)

                        em.grp("scalar", "ssc", [pr, "hscol"],
                               ["h.%d.%d" % (tl, fc)],
                               lambda s, fc=fc, tl=tl, pp=pp: s.activation(
                                   out=hbuf[:, tl * F + fc * 512:
                                            tl * F + (fc + 1) * 512],
                                   in_=psH[:, pp * 512:(pp + 1) * 512],
                                   func=AF.Silu, scale=hscol[:, tl:tl + 1]))

                        em.grp("vector", "sve", ["h.%d.%d" % (tl, fc)],
                               ["hmax"],
                               lambda v, fc=fc, tl=tl: v.reduce_max(
                                   out=hmax[:, tl * NFC + fc:
                                            tl * NFC + fc + 1],
                                   in_=hbuf[:, tl * F + fc * 512:
                                            tl * F + (fc + 1) * 512],
                                   axis=AX.X, apply_absolute_value=True))

            for th in range(GTT // 2):
                em.tfence()
                with nc.psum_tensor("psTh_%d_%d" % (g, th), [128, 2048],
                                    BF16) as psTh:
                    for tl in (2 * th, 2 * th + 1):
                        def c_hscale(v, tl=tl):
                            em.rd(v, ["hmax", "mb.w2"])
                            em.wr(v, ["axc", "shq", "hs2"])
                            em.inc(v.reduce_max(
                                out=axc[:, tl:tl + 1],
                                in_=hmax[:, tl * NFC:(tl + 1) * NFC],
                                axis=AX.X), "sve")
                            em.wait(v, "sve", em.cnt["sve"])
                            em.inc(v.tensor_scalar_max(axc[:, tl:tl + 1],
                                                       axc[:, tl:tl + 1],
                                                       1e-5), "sve")
                            em.wait(v, "sve", em.cnt["sve"])
                            em.inc(v.reciprocal(out=shq[:, tl:tl + 1],
                                                in_=axc[:, tl:tl + 1]),
                                   "sve")
                            em.wait(v, "sve", em.cnt["sve"])
                            em.inc(v.tensor_mul(scol[:, 7:8],
                                                axc[:, tl:tl + 1],
                                                shq[:, tl:tl + 1]), "sve")
                            em.wait(v, "sve", em.cnt["sve"])
                            em.inc(v.tensor_scalar(out=scol[:, 7:8],
                                                   in0=scol[:, 7:8],
                                                   scalar1=-1.0, scalar2=2.0,
                                                   op0=ALU.mult,
                                                   op1=ALU.add), "sve")
                            em.wait(v, "sve", em.cnt["sve"])
                            em.inc(v.tensor_mul(shq[:, tl:tl + 1],
                                                shq[:, tl:tl + 1],
                                                scol[:, 7:8]), "sve")
                            em.wait(v, "sve", em.cnt["sve"])
                            em.inc(v.tensor_scalar_mul(shq[:, tl:tl + 1],
                                                       shq[:, tl:tl + 1],
                                                       127.0), "sve")
                            em.inc(v.scalar_tensor_tensor(
                                out=hs2[:, tl:tl + 1],
                                in0=axc[:, tl:tl + 1], scalar=1.0 / 127.0,
                                in1=m2b[:], op0=ALU.mult, op1=ALU.mult),
                                "sve")
                            em.did_rd(["hmax", "mb.w2"], "sve")
                            em.did_wr(["axc", "shq", "hs2"], "sve")
                        em.sec("vector", c_hscale)

                        for sl in range(F // D):
                            hres = ["h.%d.%d" % (tl, fc) for fc in
                                    range(sl * (D // 512),
                                          (sl + 1) * (D // 512))]

                            for qh in range(D // 1024):
                                hh_ap = hbuf[:, tl * F + sl * D + qh * 1024:
                                             tl * F + sl * D + (qh + 1) * 1024]
                                qth = qtile[:, qh * 1024:(qh + 1) * 1024]

                                def c_hq(v, tl=tl, hh_ap=hh_ap, hres=hres):
                                    em.rd(v, ["shq"] + hres)
                                    em.wr(v, ["i32buf"])
                                    em.inc(v.tensor_scalar(
                                        out=i32buf[:], in0=hh_ap,
                                        scalar1=shq[:, tl:tl + 1],
                                        scalar2=None, op0=ALU.mult), "sve")
                                    em.did_rd(["shq"] + hres, "sve")
                                    em.did_wr(["i32buf"], "sve")
                                em.sec("vector", c_hq)

                                em.grp("vector", "sve", ["i32buf"], ["f0c"],
                                       lambda v: v.tensor_scalar(
                                           out=fbuf[:], in0=i32buf[:],
                                           scalar1=2.0, scalar2=None,
                                           op0=ALU.subtract))

                                def c_hq2(v, tl=tl, hh_ap=hh_ap, qth=qth,
                                          hres=hres):
                                    em.rd(v, ["shq", "f0c"] + hres)
                                    em.wr(v, ["rbuf", "ebuf", "f0c",
                                              "qtile"])
                                    em.inc(v.scalar_tensor_tensor(
                                        out=rbuf[:], in0=hh_ap,
                                        scalar=shq[:, tl:tl + 1],
                                        in1=fbuf[:], op0=ALU.mult,
                                        op1=ALU.subtract), "sve")
                                    em.wait(v, "sve", em.cnt["sve"])
                                    em.inc(v.scalar_tensor_tensor(
                                        out=ebuf[:], in0=rbuf[:], scalar=0.5,
                                        in1=fbuf[:], op0=ALU.is_gt,
                                        op1=ALU.add), "sve")
                                    em.wait(v, "sve", em.cnt["sve"])
                                    em.inc(v.scalar_tensor_tensor(
                                        out=fbuf[:], in0=rbuf[:], scalar=1.5,
                                        in1=ebuf[:], op0=ALU.is_ge,
                                        op1=ALU.add), "sve")
                                    em.wait(v, "sve", em.cnt["sve"])
                                    em.inc(v.scalar_tensor_tensor(
                                        out=ebuf[:], in0=rbuf[:], scalar=2.5,
                                        in1=fbuf[:], op0=ALU.is_gt,
                                        op1=ALU.add), "sve")
                                    em.wait(v, "sve", em.cnt["sve"])
                                    em.inc(v.scalar_tensor_tensor(
                                        out=qth, in0=rbuf[:], scalar=3.5,
                                        in1=ebuf[:], op0=ALU.is_ge,
                                        op1=ALU.add), "sve")
                                    em.did_rd(["shq"] + hres, "sve")
                                    em.did_wr(["rbuf", "ebuf", "f0c",
                                               "qtile"], "sve")
                                em.sec("vector", c_hq2)

                                if getattr(cfg, "dbgq", None) == \
                                        (g, tl, sl, qh):
                                    def dbgh(sy, hh_ap=hh_ap, qth=qth):
                                        em.rd(sy, ["i32buf", "rbuf",
                                                   "qtile", "shq"])
                                        em.inc(sy.dma_start(
                                            out=dhq_i32[:], in_=i32buf[:]),
                                            "ddbg", 16)
                                        em.inc(sy.dma_start(
                                            out=dhq_rp[:], in_=rbuf[:]),
                                            "ddbg", 16)
                                        em.inc(sy.dma_start(
                                            out=dhq_out[:], in_=qth),
                                            "ddbg", 16)
                                        em.inc(sy.dma_start(
                                            out=dhq_in[:], in_=hh_ap),
                                            "ddbg", 16)
                                        em.inc(sy.dma_start(
                                            out=dhq_s[:], in_=shq[:, 0:4]),
                                            "ddbg", 16)
                                        em.wait(sy, "ddbg",
                                                em.cnt["ddbg"])
                                    em.sec("sync", dbgh)
                                    em.sec("vector",
                                           lambda v: em.wait(
                                               v, "ddbg", em.cnt["ddbg"]))

                            for cb in range(NDC // 4):
                                pr = "psTh.%d" % (cb % 2)
                                po = (cb % 2) * 1024

                                def c_htr(t, cb=cb, po=po):
                                    for ci in range(4):
                                        c = cb * 4 + ci
                                        ins = t.transpose(
                                            psTh[:, po + ci * 128:
                                                 po + (ci + 1) * 128],
                                            qtile[:, c * 128:(c + 1) * 128],
                                            eye[:])
                                    return ins
                                em.grp("tensor", "ste", ["qtile", "eye"],
                                       [pr], c_htr)

                                def c_htrev(v, tl=tl, sl=sl, cb=cb, th=th,
                                            po=po):
                                    ttp = tl - 2 * th
                                    for ci in range(4):
                                        fc128 = sl * NDC + cb * 4 + ci
                                        ins = v.tensor_copy(
                                            out=hqT[:, fc128 * 256 +
                                                    ttp * 128:
                                                    fc128 * 256 +
                                                    (ttp + 1) * 128],
                                            in_=psTh[:, po + ci * 128:
                                                     po + (ci + 1) * 128])
                                    return ins
                                em.grp("vector", "sve", [pr], ["hqT"],
                                       c_htrev)

                em.tfence()
                with nc.psum_tensor("psO2_%d_%d" % (g, th), [128, 4096],
                                    F32) as psO2:
                    for dcp in range(NDOC // 2):
                        pb = dcp % 2
                        rrs = ["psO2.%d.%d" % (pb, k) for k in range(4)]
                        for fc128 in range(NFC128):
                            par = fc128 % 2
                            dsem = "dw2t0" if par == 0 else "dw2t1"
                            wres = "w2t.%d" % par

                            def c_w2sec(sy, dcp=dcp, fc128=fc128, par=par,
                                        dsem=dsem, wres=wres):
                                em.wr(sy, [wres])
                                em.wait(sy, "dstore", dstore_final)
                                ins = sy.dma_start(
                                    out=w2t[:, par * 1024:(par + 1) * 1024],
                                    in_=w2q_d[dcp * 1024:(dcp + 1) * 1024,
                                              fc128 * 128:(fc128 + 1) * 128],
                                    transpose=True)
                                em.inc(ins, dsem, 16)
                                em.did_wr([wres], dsem)
                            em.sec("sync", c_w2sec)

                            def c_mm2(t, fc128=fc128, par=par, pb=pb,
                                      wres=wres, rrs=rrs):
                                st = (fc128 == 0)
                                sp = (fc128 == NFC128 - 1)
                                em.rd(t, [wres, "hqT"])
                                if st:
                                    em.wr(t, rrs)
                                for k in range(4):
                                    ti, di = divmod(k, 2)
                                    ins = t.matmul(
                                        psO2[:, (pb * 4 + k) * 512:
                                             (pb * 4 + k + 1) * 512],
                                        hqT[:, fc128 * 256 + ti * 128:
                                            fc128 * 256 + (ti + 1) * 128],
                                        w2t[:, par * 1024 + di * 512:
                                            par * 1024 + (di + 1) * 512],
                                        start=st, stop=sp)
                                em.inc(ins, "ste")
                                em.did_rd([wres, "hqT"], "ste")
                                if sp:
                                    em.did_wr(rrs, "ste")
                            em.sec("tensor", c_mm2)

                        for k in range(4):
                            ti, di = divmod(k, 2)
                            tl = 2 * th + ti
                            tt = g * GTT + tl
                            dc = dcp * 2 + di
                            pp = k % 2
                            rr = rrs[k]
                            wor = "wotc.%d" % pp
                            xrr = "xrest.%d" % pp
                            obr = "outb.%d" % pp
                            dsemw = "dwoC0" if pp == 0 else "dwoC1"
                            dsemx = "dres0" if pp == 0 else "dres1"

                            def c_lodma(sy, tt=tt, dc=dc, pp=pp, wor=wor,
                                        xrr=xrr, dsemw=dsemw, dsemx=dsemx):
                                em.wr(sy, [wor, xrr])
                                em.wait(sy, "dstore", dstore_final)
                                i1 = sy.dma_start(
                                    out=wotc[:, pp * 512:(pp + 1) * 512],
                                    in_=wo_scr_d[tt * 128:(tt + 1) * 128,
                                                 dc * 512:(dc + 1) * 512])
                                em.inc(i1, dsemw, 16)
                                i2 = sy.dma_start(
                                    out=xrest[:, pp * 512:(pp + 1) * 512],
                                    in_=x_d[tt * 128:(tt + 1) * 128,
                                            dc * 512:(dc + 1) * 512])
                                em.inc(i2, dsemx, 16)
                                em.did_wr([wor], dsemw)
                                em.did_wr([xrr], dsemx)
                            em.sec("sync", c_lodma)

                            em.grp("vector", "sve", [rr, "hs2"], ["tAb"],
                                   lambda v, pb=pb, k=k, tl=tl:
                                   v.tensor_scalar_mul(
                                       tAb[:],
                                       psO2[:, (pb * 4 + k) * 512:
                                            (pb * 4 + k + 1) * 512],
                                       hs2[:, tl:tl + 1]))

                            def c_out(v, pp=pp, wor=wor, xrr=xrr, obr=obr):
                                em.rd(v, ["tAb", wor, xrr])
                                em.wr(v, [obr])
                                em.inc(v.tensor_add(
                                    outb[:, pp * 512:(pp + 1) * 512],
                                    tAb[:],
                                    wotc[:, pp * 512:(pp + 1) * 512]), "sve")
                                em.wait(v, "sve", em.cnt["sve"])
                                em.inc(v.tensor_add(
                                    outb[:, pp * 512:(pp + 1) * 512],
                                    outb[:, pp * 512:(pp + 1) * 512],
                                    xrest[:, pp * 512:(pp + 1) * 512]),
                                    "sve")
                                em.did_rd(["tAb", wor, xrr], "sve")
                                em.did_wr([obr], "sve")
                            em.sec("vector", c_out)

                            em.grp("sync", "dout", [obr], [],
                                   lambda sy, tt=tt, dc=dc, pp=pp:
                                   sy.dma_start(
                                       out=y_d[tt * 128:(tt + 1) * 128,
                                               dc * 512:(dc + 1) * 512],
                                       in_=outb[:, pp * 512:(pp + 1) * 512]),
                                   dmak=True)

        def fin(sy):
            em.wait(sy, "dout", em.cnt["dout"])
            em.wait(sy, "dstore", em.cnt["dstore"])
        em.sec("sync", fin)
        stkC.close()

    gctx.close()
    nc.finalize()
    return nc


_NC_CACHE = {}


def _get_nc():
    if "nc" not in _NC_CACHE:
        _NC_CACHE["nc"] = build(CFG_FULL)
    return _NC_CACHE["nc"]


def kernel(L0_attn_x, L0_attn_Wq, L0_attn_Wk, L0_attn_Wv, L0_attn_Wo,
           L0_attn_lq1, L0_attn_lk1, L0_attn_lq2, L0_attn_lk2,
           L0_ffn_W1, L0_ffn_W2):
    x = np.ascontiguousarray(np.asarray(L0_attn_x, dtype=np.float32))
    ws = {
        "Wq": np.ascontiguousarray(np.asarray(L0_attn_Wq, dtype=np.float32)),
        "Wk": np.ascontiguousarray(np.asarray(L0_attn_Wk, dtype=np.float32)),
        "Wv": np.ascontiguousarray(np.asarray(L0_attn_Wv, dtype=np.float32)),
        "Wo": np.ascontiguousarray(np.asarray(L0_attn_Wo, dtype=np.float32)),
        "lq1": np.ascontiguousarray(np.asarray(L0_attn_lq1, dtype=np.float32)),
        "lk1": np.ascontiguousarray(np.asarray(L0_attn_lk1, dtype=np.float32)),
        "lq2": np.ascontiguousarray(np.asarray(L0_attn_lq2, dtype=np.float32)),
        "lk2": np.ascontiguousarray(np.asarray(L0_attn_lk2, dtype=np.float32)),
        "W1": np.ascontiguousarray(np.asarray(L0_ffn_W1, dtype=np.float32)),
        "W2": np.ascontiguousarray(np.asarray(L0_ffn_W2, dtype=np.float32)),
    }
    nc = _get_nc()
    n = x.shape[0]
    in_maps = [dict(ws, x=np.ascontiguousarray(x[b])) for b in range(n)]
    res = run_bass_kernel_spmd(nc, in_maps, core_ids=list(range(n)))
    return np.stack([np.asarray(res.results[b]["y"]) for b in range(n)],
                    axis=0)


# version 08
# speedup vs baseline: 1.1616x; 1.1616x over previous
import math
from contextlib import ExitStack

import numpy as np

import concourse.bass as bass
import concourse.mybir as mybir
from concourse.bass_utils import run_bass_kernel_spmd

F32 = mybir.dt.float32
BF16 = mybir.dt.bfloat16
I32 = mybir.dt.int32
AF = mybir.ActivationFunctionType
ALU = mybir.AluOpType
AX = mybir.AxisListType

DEPTH = 12
LAMBDA_INIT = 0.8 - 0.6 * math.exp(-0.3 * DEPTH)
LNC_CONST = math.log((1.0 - LAMBDA_INIT) * math.sqrt(128.0))
LNC_HI = float(np.float32(LNC_CONST))
LNC_LO = float(np.float32(LNC_CONST - LNC_HI))


class Cfg:
    def __init__(self, T, D, H, F, NG):
        self.T = T
        self.D = D
        self.H = H
        self.F = F
        self.NG = NG
        self.HD = 64
        assert D == 128 * H
        self.NTT = T // 128
        self.NDC = D // 128
        self.QB = 512
        assert T % self.QB == 0
        self.NQB = T // self.QB
        self.GT = T // NG
        self.GTT = self.GT // 128
        assert self.GTT % 2 == 0
        self.NFC = F // 512
        self.NFC128 = F // 128
        self.NDOC = D // 512
        self.phases = 5
        self.a_tiles = self.NTT
        self.a_steps = 9
        self.init_only = False
        self.debug = False


CFG_FULL = Cfg(T=2048, D=2048, H=16, F=8192, NG=4)


class Emit:

    def __init__(self, nc, block, stack):
        self.nc = nc
        self.block = block
        self.stack = stack
        self.sems = {}
        self.cnt = {}
        self.waited = {}
        self.res = {}
        self.cur = None

    def sem(self, name):
        h = self.stack.enter_context(self.nc.semaphore(name))
        self.sems[name] = h
        self.cnt[name] = 0

    def sec(self, engname, body):
        def f(eng):
            prev = self.cur
            self.cur = engname
            body(eng)
            self.cur = prev
        getattr(self.block, engname)(f)

    def wait(self, eng, semname, val):
        if val <= 0:
            return
        if semname.startswith("d"):
            val = self.cnt[semname]
        key = (self.cur, semname)
        if self.waited.get(key, -1) >= val:
            return
        eng.wait_ge(self.sems[semname], val)
        self.waited[key] = val

    def inc(self, instr, semname, k=1):
        instr.then_inc(self.sems[semname], k)
        self.cnt[semname] += k
        key = (self.cur, semname)
        return self.cnt[semname]

    def _get(self, rname):
        return self.res.setdefault(rname, {"w": None, "r": []})

    def rd(self, eng, rnames):
        for rn in rnames:
            r = self._get(rn)
            if r["w"] is not None:
                self.wait(eng, r["w"][0], r["w"][1])

    def wr(self, eng, rnames):
        for rn in rnames:
            r = self._get(rn)
            if r["w"] is not None:
                self.wait(eng, r["w"][0], r["w"][1])
            for (s, v) in r["r"]:
                self.wait(eng, s, v)

    def did_rd(self, rnames, semname):
        v = self.cnt[semname]
        for rn in rnames:
            self._get(rn)["r"].append((semname, v))

    def did_wr(self, rnames, semname):
        v = self.cnt[semname]
        for rn in rnames:
            r = self._get(rn)
            r["w"] = (semname, v)
            r["r"] = []

    def fence(self, engnames=None):
        sems = list(self.sems)
        for en in (engnames or ["tensor", "vector", "scalar", "gpsimd",
                                "sync"]):
            def f(eng, en=en):
                for s in sems:
                    self.wait(eng, s, self.cnt[s])
            self.sec(en, f)

    def tfence(self):
        def f(eng):
            for s in ("sve", "ssc", "sgp"):
                self.wait(eng, s, self.cnt[s])
        self.sec("tensor", f)

    def grp(self, engname, semname, reads, writes, body, dmak=None):
        def f(eng):
            self.rd(eng, reads)
            self.wr(eng, writes)
            last = body(eng)
            if last is not None:
                self.inc(last, semname, 16 if dmak else 1)
            self.did_rd(reads, semname)
            self.did_wr(writes, semname)
        self.sec(engname, f)


def build(cfg):
    T, D, H, F = cfg.T, cfg.D, cfg.H, cfg.F
    NTT, NDC, NQB, QB = cfg.NTT, cfg.NDC, cfg.NQB, cfg.QB
    NG, GT, GTT = cfg.NG, cfg.GT, cfg.GTT
    NFC, NFC128, NDOC = cfg.NFC, cfg.NFC128, cfg.NDOC
    JB = QB // 128

    nc = bass.Bass("TRN2", target_bir_lowering=False)

    x_d = nc.dram_tensor("x", [T, D], F32, kind="ExternalInput").ap()
    wq_d = nc.dram_tensor("Wq", [D, D], F32, kind="ExternalInput").ap()
    wk_d = nc.dram_tensor("Wk", [D, D], F32, kind="ExternalInput").ap()
    wv_d = nc.dram_tensor("Wv", [D, D], F32, kind="ExternalInput").ap()
    wo_d = nc.dram_tensor("Wo", [D, D], F32, kind="ExternalInput").ap()
    lq1_d = nc.dram_tensor("lq1", [64], F32, kind="ExternalInput").ap()
    lk1_d = nc.dram_tensor("lk1", [64], F32, kind="ExternalInput").ap()
    lq2_d = nc.dram_tensor("lq2", [64], F32, kind="ExternalInput").ap()
    lk2_d = nc.dram_tensor("lk2", [64], F32, kind="ExternalInput").ap()
    w1_d = nc.dram_tensor("W1", [F, D], F32, kind="ExternalInput").ap()
    w2_d = nc.dram_tensor("W2", [D, F], F32, kind="ExternalInput").ap()
    y_d = nc.dram_tensor("y", [T, D], F32, kind="ExternalOutput").ap()

    scrkind = "ExternalOutput" if cfg.debug else "Internal"
    w1q_d = nc.dram_tensor("w1q_scr", [F, D], BF16, kind=scrkind).ap()
    w2q_d = nc.dram_tensor("w2q_scr", [D, F], BF16, kind=scrkind).ap()
    wo_scr_d = nc.dram_tensor("wo_scr", [T, D], BF16, kind=scrkind).ap()
    if cfg.debug:
        xnT_dbg = nc.dram_tensor("xnT_dbg", [128, cfg.NDC * T], BF16,
                                 kind="ExternalOutput").ap()
        attnT_dbg = nc.dram_tensor("attnT_dbg", [128, H * T], BF16,
                                   kind="ExternalOutput").ap()
        cells_dbg = nc.dram_tensor("cells_dbg", [1, 16], F32,
                                   kind="ExternalOutput").ap()
        lni_dbg = nc.dram_tensor("lni_dbg", [128, 2 * cfg.NTT], F32,
                                 kind="ExternalOutput").ap()

    if getattr(cfg, "dbgqx", None) is not None:
        dxq_i32 = nc.dram_tensor("dxq_i32", [128, 2048], I32,
                                 kind="ExternalOutput").ap()
        dxq_rp = nc.dram_tensor("dxq_rp", [128, 2048], F32,
                                kind="ExternalOutput").ap()
        dxq_out = nc.dram_tensor("dxq_out", [128, 2048], BF16,
                                 kind="ExternalOutput").ap()
        dxq_in = nc.dram_tensor("dxq_in", [128, 2048], F32,
                                kind="ExternalOutput").ap()
        dxq_s = nc.dram_tensor("dxq_s", [128, 4], F32,
                               kind="ExternalOutput").ap()
    if getattr(cfg, "dbgq", None) is not None:
        dhq_i32 = nc.dram_tensor("dhq_i32", [128, 1024], I32,
                                 kind="ExternalOutput").ap()
        dhq_rp = nc.dram_tensor("dhq_rp", [128, 1024], F32,
                                kind="ExternalOutput").ap()
        dhq_out = nc.dram_tensor("dhq_out", [128, 1024], BF16,
                                 kind="ExternalOutput").ap()
        dhq_in = nc.dram_tensor("dhq_in", [128, 1024], BF16,
                                kind="ExternalOutput").ap()
        dhq_s = nc.dram_tensor("dhq_s", [128, 4], F32,
                               kind="ExternalOutput").ap()

    gctx = ExitStack()

    with nc.Block() as block:
        em = Emit(nc, block, gctx)
        for s in ["dg", "dstore", "dout", "ddbg",
                  "dwq0", "dwq1", "dxa0", "dxa1", "dw1t0", "dw1t1",
                  "dw2t0", "dw2t1", "dres0", "dres1", "dwoB0", "dwoB1",
                  "dwoC0", "dwoC1",
                  "ste", "sve", "ssc", "sgp"]:
            em.sem(s)

        def sb(name, shape, dt, stk=None):
            return (stk or gctx).enter_context(nc.sbuf_tensor(name, shape, dt))

        ones_col = sb("ones_col", [128, 1], BF16)
        ones_colf = sb("ones_colf", [128, 1], F32)
        ones_rowb = sb("ones_rowb", [1, 128], BF16)
        ones_rowf = sb("ones_rowf", [1, 128], F32)
        eye = sb("eye", [128, 128], BF16)
        tri = sb("tri", [128, 128], BF16)
        lamq = sb("lamq", [1, 256], F32)
        cells = sb("cells", [1, 16], F32)
        m1b = sb("m1b", [128, 1], F32)
        m2b = sb("m2b", [128, 1], F32)
        lni = sb("lni", [128, NTT], F32)
        lnb2 = sb("lnb2", [128, NTT], F32)
        scol = sb("scol", [128, 8], F32)
        epscol = sb("epscol", [128, 1], F32)
        scr128 = sb("scr128", [128, 128], BF16)

        lam11 = cells[0:1, 0:1]

        def init_v(v):
            em.inc(v.memset(cells[:], 0.0), "sve")
            em.wait(v, "sve", em.cnt["sve"])
            em.inc(v.memset(ones_col[:], 1.0), "sve")
            em.inc(v.memset(ones_colf[:], 1.0), "sve")
            em.inc(v.memset(ones_rowb[:], 1.0), "sve")
            em.inc(v.memset(ones_rowf[:], 1.0), "sve")
            em.inc(v.memset(epscol[:], 1e-5), "sve")
            em.inc(v.memset(cells[0:1, 15:16], LNC_HI), "sve")
            em.inc(v.memset(cells[0:1, 14:15], LNC_LO), "sve")
            em.inc(v.memset(scr128[:], 1.0), "sve")
            em.wait(v, "sve", em.cnt["sve"])
            em.inc(v.tensor_add(cells[0:1, 15:16], cells[0:1, 15:16],
                                cells[0:1, 14:15]), "sve")
            em.did_wr(["ones", "scr128", "lnccell"], "sve")
        em.sec("vector", init_v)

        def init_eye(g):
            return g.affine_select(out=eye[:], in_=scr128[:],
                                   pattern=[[1, 128]],
                                   compare_op=ALU.is_equal, fill=0.0, base=0,
                                   channel_multiplier=-1)
        em.grp("gpsimd", "sgp", ["scr128"], ["eye"], init_eye)

        def init_tri(g):
            return g.affine_select(out=tri[:], in_=scr128[:],
                                   pattern=[[1, 128]],
                                   compare_op=ALU.is_ge, fill=0.0, base=0,
                                   channel_multiplier=-1)
        em.grp("gpsimd", "sgp", ["scr128"], ["tri"], init_tri)

        def init_lamdma(g):
            for i, dsrc in enumerate([lq1_d, lk1_d, lq2_d, lk2_d]):
                ins = g.dma_start(out=lamq[0:1, i * 64:(i + 1) * 64],
                                  in_=dsrc[:])
                ins.then_inc(em.sems["dg"], 16)
                em.cnt["dg"] += 16
            em.did_wr(["lamq"], "dg")
        em.sec("gpsimd", init_lamdma)

        def init_lam_v(v):
            em.rd(v, ["lamq"])
            em.wr(v, ["lamq", "lamcells"])
            em.inc(v.tensor_mul(lamq[0:1, 0:64], lamq[0:1, 0:64],
                                lamq[0:1, 64:128]), "sve")
            em.wait(v, "sve", em.cnt["sve"])
            em.inc(v.reduce_sum(out=cells[0:1, 5:6], in_=lamq[0:1, 0:64],
                                axis=AX.X), "sve")
            em.inc(v.tensor_mul(lamq[0:1, 128:192], lamq[0:1, 128:192],
                                lamq[0:1, 192:256]), "sve")
            em.wait(v, "sve", em.cnt["sve"])
            em.inc(v.reduce_sum(out=cells[0:1, 6:7], in_=lamq[0:1, 128:192],
                                axis=AX.X), "sve")
            em.did_wr(["lamcells"], "sve")
        em.sec("vector", init_lam_v)

        def init_lam_s(s):
            em.rd(s, ["lamcells"])
            em.wr(s, ["lamexp"])
            em.inc(s.activation(out=cells[0:1, 1:2], in_=cells[0:1, 5:6],
                                func=AF.Exp), "ssc")
            em.inc(s.activation(out=cells[0:1, 2:3], in_=cells[0:1, 6:7],
                                func=AF.Exp), "ssc")
            em.did_wr(["lamexp"], "ssc")
        em.sec("scalar", init_lam_s)

        def init_lam_v2(v):
            em.rd(v, ["lamexp"])
            em.wr(v, ["lam"])
            em.inc(v.tensor_sub(cells[0:1, 0:1], cells[0:1, 1:2],
                                cells[0:1, 2:3]), "sve")
            em.wait(v, "sve", em.cnt["sve"])
            em.inc(v.tensor_scalar_add(cells[0:1, 0:1], cells[0:1, 0:1],
                                       LAMBDA_INIT), "sve")
            em.did_wr(["lam"], "sve")
        em.sec("vector", init_lam_v2)

        stkAB = ExitStack()
        stkAW = ExitStack()
        xnT = sb("xnT", [128, NDC * T], BF16, stkAB)
        attnT = sb("attnT", [128, H * T], BF16, stkAB)
        xf32 = sb("xf32", [128, 2 * D], F32, stkAW)
        xnb = sb("xnb", [128, 2 * D], BF16, stkAW)
        junk = sb("junk", [128, D], F32, stkAW)

        em.tfence()
        with nc.psum_tensor("psTA", [128, 2048], BF16) as psTA:
            for tt in (range(cfg.a_tiles) if cfg.phases >= 1 else []):
                par = tt % 2
                dsem = "dxa0" if par == 0 else "dxa1"
                xr = "xf.%d" % par
                xnr = "xnb.%d" % par
                xs = xf32[:, par * D:(par + 1) * D]
                xns = xnb[:, par * D:(par + 1) * D]

                if cfg.a_steps >= 1:
                    em.grp("sync", dsem, [], [xr],
                           lambda sy, tt=tt, xs=xs: sy.dma_start(
                               out=xs, in_=x_d[tt * 128:(tt + 1) * 128, :]),
                           dmak=True)
                if cfg.a_steps >= 2:
                    em.grp("vector", "sve", [xr], ["scol.sum"],
                           lambda v, xs=xs: v.reduce_sum(
                               out=scol[:, 0:1], in_=xs, axis=AX.X))
                if cfg.a_steps >= 3:
                    em.grp("scalar", "ssc", [xr], ["junk"],
                           lambda s, xs=xs: s.activation(
                               out=junk[:], in_=xs, func=AF.Square))
                if cfg.a_steps >= 4:
                    em.grp("vector", "sve", ["junk"], ["scol.sq"],
                           lambda v: v.reduce_sum(out=scol[:, 1:2],
                                                  in_=junk[:], axis=AX.X))

                if cfg.a_steps < 5:
                    continue

                def a_ln_v(v):
                    em.rd(v, ["scol.sum", "scol.sq"])
                    em.wr(v, ["scol.mv"])
                    em.inc(v.tensor_scalar_mul(scol[:, 2:3], scol[:, 0:1],
                                               1.0 / D), "sve")
                    em.wait(v, "sve", em.cnt["sve"])
                    em.inc(v.tensor_mul(scol[:, 3:4], scol[:, 2:3],
                                        scol[:, 2:3]), "sve")
                    em.wait(v, "sve", em.cnt["sve"])
                    em.inc(v.scalar_tensor_tensor(
                        out=scol[:, 4:5], in0=scol[:, 1:2], scalar=1.0 / D,
                        in1=scol[:, 3:4], op0=ALU.mult, op1=ALU.subtract),
                        "sve")
                    em.did_wr(["scol.mv"], "sve")
                em.sec("vector", a_ln_v)

                if cfg.a_steps < 6:
                    continue

                def a_ln_s(s, tt=tt):
                    em.rd(s, ["scol.mv"])
                    em.wr(s, ["lni", "scol.lnv"])
                    em.inc(s.activation(out=scol[:, 5:6], in_=scol[:, 4:5],
                                        func=AF.Ln, bias=epscol[:]), "ssc")
                    em.wait(s, "ssc", em.cnt["ssc"])
                    em.inc(s.activation(out=lni[:, tt:tt + 1],
                                        in_=scol[:, 5:6], func=AF.Exp,
                                        scale=-0.5), "ssc")
                    em.did_rd(["scol.mv"], "ssc")
                    em.did_wr(["lni", "scol.lnv"], "ssc")
                em.sec("scalar", a_ln_s)

                if cfg.a_steps < 7:
                    continue
                em.grp("vector", "sve", ["lni", "scol.mv"], ["lnb2"],
                       lambda v, tt=tt: v.scalar_tensor_tensor(
                           out=lnb2[:, tt:tt + 1], in0=scol[:, 2:3],
                           scalar=-1.0, in1=lni[:, tt:tt + 1],
                           op0=ALU.mult, op1=ALU.mult))

                if cfg.a_steps < 8:
                    continue
                em.grp("vector", "sve", [xr, "lni", "lnb2"], [xnr],
                       lambda v, tt=tt, xs=xs, xns=xns: v.tensor_scalar(
                           out=xns, in0=xs, scalar1=lni[:, tt:tt + 1],
                           scalar2=lnb2[:, tt:tt + 1], op0=ALU.mult,
                           op1=ALU.add))

                if cfg.a_steps < 9:
                    continue
                for cb in range(NDC // 4):
                    pr = "psTA.%d" % (cb % 2)
                    po = (cb % 2) * 1024

                    def a_tr(t, cb=cb, po=po, xns=xns):
                        for ci in range(4):
                            c = cb * 4 + ci
                            ins = t.transpose(
                                psTA[:, po + ci * 128:po + (ci + 1) * 128],
                                xns[:, c * 128:(c + 1) * 128], eye[:])
                        return ins
                    em.grp("tensor", "ste", [xnr, "eye"], [pr], a_tr)

                    def a_trev(v, tt=tt, cb=cb, po=po):
                        for ci in range(4):
                            c = cb * 4 + ci
                            ins = v.tensor_copy(
                                out=xnT[:, c * T + tt * 128:
                                        c * T + (tt + 1) * 128],
                                in_=psTA[:, po + ci * 128:po + (ci + 1) * 128])
                        return ins
                    em.grp("vector", "sve", [pr], ["xnT"], a_trev)

        i32buf = sb("i32buf", [128, D], I32, stkAW)
        wr32 = sb("wr32", [128, D], F32, stkAW)
        ac4 = sb("ac4", [128, D // 512], F32, stkAW)
        h4 = sb("h4", [128, D // 512], I32, stkAW)
        l4 = sb("l4", [128, D // 512], F32, stkAW)
        hlr = sb("hlr", [128, 4], F32, stkAW)
        wrows = sb("wrows", [1, 512], F32, stkAW)
        hp2 = sb("hp2", [1, 128], I32, stkAW)
        hlo = sb("hlo", [1, 128], F32, stkAW)

        def wprep(w_src, wq_dst, R, C, mcell, mcname, mbvec, tag):
            NP_C = C // D
            npieces = (R // 128) * NP_C
            assert npieces <= 128
            SC = float(2.0 ** 17)
            ISC = float(2.0 ** -17)
            NINV = 1.0 / (R * C)
            assert (R * C) & (R * C - 1) == 0

            with nc.psum_tensor("psW_" + tag, [128, 8], F32) as psW:
                em.tfence()
                for pi in range(npieces):
                    rr, cc = divmod(pi, NP_C)
                    par = pi % 2
                    dsem = "dxa0" if par == 0 else "dxa1"
                    xr = "xf.%d" % par
                    xs = xf32[:, par * D:(par + 1) * D]

                    em.grp("sync", dsem, [], [xr],
                           lambda sy, rr=rr, cc=cc, xs=xs: sy.dma_start(
                               out=xs, in_=w_src[rr * 128:(rr + 1) * 128,
                                                 cc * D:(cc + 1) * D]),
                           dmak=True)

                    em.grp("scalar", "ssc", [xr], ["junk"],
                           lambda s, xs=xs: s.activation(
                               out=junk[:], in_=xs, func=AF.Abs))

                    def wp_sum(v, pi=pi):
                        em.rd(v, ["junk"])
                        em.wr(v, ["i32buf", "wr32", "ac4", "h4", "l4",
                                  "hlr"])
                        em.inc(v.tensor_scalar(out=i32buf[:], in0=junk[:],
                                               scalar1=SC, scalar2=None,
                                               op0=ALU.mult), "sve")
                        em.wait(v, "sve", em.cnt["sve"])
                        em.inc(v.scalar_tensor_tensor(
                            out=wr32[:], in0=i32buf[:], scalar=ISC,
                            in1=junk[:], op0=ALU.mult, op1=ALU.subtract),
                            "sve")
                        for ch in range(D // 512):
                            em.inc(v.reduce_sum(
                                out=ac4[:, ch:ch + 1],
                                in_=i32buf[:, ch * 512:(ch + 1) * 512],
                                axis=AX.X), "sve")
                        em.wait(v, "sve", em.cnt["sve"])
                        em.inc(v.reduce_sum(out=hlr[:, 2:3], in_=wr32[:],
                                            axis=AX.X), "sve")
                        em.inc(v.tensor_scalar(out=h4[:], in0=ac4[:],
                                               scalar1=float(2.0 ** -10),
                                               scalar2=None, op0=ALU.mult),
                               "sve")
                        em.wait(v, "sve", em.cnt["sve"])
                        em.inc(v.scalar_tensor_tensor(
                            out=l4[:], in0=h4[:], scalar=-1024.0,
                            in1=ac4[:], op0=ALU.mult, op1=ALU.add), "sve")
                        em.wait(v, "sve", em.cnt["sve"])
                        em.inc(v.reduce_sum(out=hlr[:, 0:1], in_=h4[:],
                                            axis=AX.X), "sve")
                        em.inc(v.reduce_sum(out=hlr[:, 1:2], in_=l4[:],
                                            axis=AX.X), "sve")
                        em.did_rd(["junk"], "sve")
                        em.did_wr(["i32buf", "wr32", "ac4", "h4", "l4",
                                   "hlr"], "sve")
                    em.sec("vector", wp_sum)

                    def wp_mm(t, pi=pi):
                        for k in range(3):
                            ins = t.matmul(psW[0:1, k:k + 1],
                                           hlr[:, k:k + 1], ones_colf[:],
                                           start=True, stop=True)
                        return ins
                    em.grp("tensor", "ste", ["hlr", "ones"], ["psW"], wp_mm)

                    def wp_ev(v, pi=pi):
                        em.rd(v, ["psW"])
                        em.wr(v, ["wrows"])
                        for k in range(3):
                            ins = v.tensor_copy(
                                out=wrows[0:1, k * 128 + pi:
                                          k * 128 + pi + 1],
                                in_=psW[0:1, k:k + 1])
                        em.inc(ins, "sve")
                        em.did_rd(["psW"], "sve")
                        em.did_wr(["wrows"], "sve")
                    em.sec("vector", wp_ev)

                def wp_fin(v, npieces=npieces, mcell=mcell, tag=tag,
                           mcname=mcname):
                    em.rd(v, ["wrows"])
                    em.wr(v, ["hp2", "hlo", "meancells", "lamcells",
                              "lamexp", mcname, "cells.s" + tag])
                    np_ = npieces
                    em.inc(v.tensor_scalar(out=hp2[0:1, 0:np_],
                                           in0=wrows[0:1, 0:np_],
                                           scalar1=float(2.0 ** -10),
                                           scalar2=None, op0=ALU.mult),
                           "sve")
                    em.wait(v, "sve", em.cnt["sve"])
                    em.inc(v.scalar_tensor_tensor(
                        out=hlo[0:1, 0:np_], in0=hp2[0:1, 0:np_],
                        scalar=-1024.0, in1=wrows[0:1, 0:np_],
                        op0=ALU.mult, op1=ALU.add), "sve")
                    em.wait(v, "sve", em.cnt["sve"])
                    em.inc(v.reduce_sum(out=cells[0:1, 5:6],
                                        in_=hp2[0:1, 0:np_], axis=AX.X),
                           "sve")
                    em.inc(v.reduce_sum(out=cells[0:1, 6:7],
                                        in_=hlo[0:1, 0:np_], axis=AX.X),
                           "sve")
                    em.inc(v.reduce_sum(out=cells[0:1, 8:9],
                                        in_=wrows[0:1, 128:128 + np_],
                                        axis=AX.X), "sve")
                    em.inc(v.reduce_sum(out=cells[0:1, 9:10],
                                        in_=wrows[0:1, 256:256 + np_],
                                        axis=AX.X), "sve")
                    em.wait(v, "sve", em.cnt["sve"])
                    em.inc(v.tensor_scalar(out=cells[0:1, 10:11],
                                           in0=cells[0:1, 5:6],
                                           scalar1=(2.0 ** 20) * ISC * NINV,
                                           scalar2=None, op0=ALU.mult),
                           "sve")
                    em.inc(v.tensor_scalar(out=cells[0:1, 11:12],
                                           in0=cells[0:1, 6:7],
                                           scalar1=(2.0 ** 10) * ISC * NINV,
                                           scalar2=None, op0=ALU.mult),
                           "sve")
                    em.inc(v.tensor_scalar(out=cells[0:1, 12:13],
                                           in0=cells[0:1, 8:9],
                                           scalar1=ISC * NINV, scalar2=None,
                                           op0=ALU.mult), "sve")
                    em.inc(v.tensor_scalar(out=cells[0:1, 13:14],
                                           in0=cells[0:1, 9:10],
                                           scalar1=-NINV, scalar2=None,
                                           op0=ALU.mult), "sve")
                    em.wait(v, "sve", em.cnt["sve"])
                    em.inc(v.tensor_add(cells[0:1, 12:13],
                                        cells[0:1, 12:13],
                                        cells[0:1, 13:14]), "sve")
                    em.wait(v, "sve", em.cnt["sve"])
                    em.inc(v.tensor_add(cells[0:1, 11:12],
                                        cells[0:1, 11:12],
                                        cells[0:1, 12:13]), "sve")
                    em.wait(v, "sve", em.cnt["sve"])
                    em.inc(v.tensor_add(cells[0:1, 10:11],
                                        cells[0:1, 10:11],
                                        cells[0:1, 11:12]), "sve")
                    em.wait(v, "sve", em.cnt["sve"])
                    em.inc(v.tensor_scalar(out=mcell, in0=cells[0:1, 10:11],
                                           scalar1=1e-5, scalar2=None,
                                           op0=ALU.max), "sve")
                    em.wait(v, "sve", em.cnt["sve"])
                    em.inc(v.reciprocal(out=cells[0:1, 7:8], in_=mcell),
                           "sve")
                    em.wait(v, "sve", em.cnt["sve"])
                    em.inc(v.tensor_mul(cells[0:1, 13:14], mcell,
                                        cells[0:1, 7:8]), "sve")
                    em.wait(v, "sve", em.cnt["sve"])
                    em.inc(v.tensor_scalar_add(cells[0:1, 13:14],
                                               cells[0:1, 13:14], -1.0),
                           "sve")
                    em.wait(v, "sve", em.cnt["sve"])
                    em.inc(v.tensor_mul(cells[0:1, 13:14],
                                        cells[0:1, 13:14],
                                        cells[0:1, 7:8]), "sve")
                    em.wait(v, "sve", em.cnt["sve"])
                    em.inc(v.tensor_sub(cells[0:1, 7:8], cells[0:1, 7:8],
                                        cells[0:1, 13:14]), "sve")
                    em.did_rd(["wrows"], "sve")
                    em.did_wr(["hp2", "hlo", "meancells", "lamcells",
                               "lamexp", mcname, "cells.s" + tag], "sve")
                em.sec("vector", wp_fin)

                def w_bc_t(t, mcell=mcell):
                    ins = t.matmul(psW[:, 4:5], ones_rowf[0:1, :],
                                   cells[0:1, 7:8], start=True, stop=True)
                    ins = t.matmul(psW[:, 5:6], ones_rowf[0:1, :], mcell,
                                   start=True, stop=True)
                    return ins
                em.grp("tensor", "ste", [mcname, "cells.s" + tag, "ones"],
                       ["psW2"], w_bc_t)

                def w_bc_v(v, mbvec=mbvec):
                    em.rd(v, ["psW2"])
                    em.wr(v, ["srecip." + tag, "mb." + tag])
                    em.inc(v.tensor_copy(out=scol[:, 6:7], in_=psW[:, 4:5]),
                           "sve")
                    em.inc(v.tensor_copy(out=mbvec, in_=psW[:, 5:6]), "sve")
                    em.did_rd(["psW2"], "sve")
                    em.did_wr(["srecip." + tag, "mb." + tag], "sve")
                em.sec("vector", w_bc_v)

                for pi in range(npieces):
                    rr, cc = divmod(pi, NP_C)
                    par = pi % 2
                    dsem = "dxa0" if par == 0 else "dxa1"
                    xr = "xf.%d" % par
                    xnr = "xnb.%d" % par
                    xs = xf32[:, par * D:(par + 1) * D]
                    xns = xnb[:, par * D:(par + 1) * D]

                    em.grp("sync", dsem, [], [xr],
                           lambda sy, rr=rr, cc=cc, xs=xs: sy.dma_start(
                               out=xs, in_=w_src[rr * 128:(rr + 1) * 128,
                                                 cc * D:(cc + 1) * D]),
                           dmak=True)

                    def wq_q(v, xs=xs, xr=xr, xnr=xnr, tag=tag):
                        em.rd(v, [xr, "srecip." + tag])
                        em.wr(v, ["wr32", "junk", xnr])
                        em.inc(v.tensor_scalar(out=wr32[:], in0=xs,
                                               scalar1=scol[:, 6:7],
                                               scalar2=None, op0=ALU.mult),
                               "sve")
                        em.wait(v, "sve", em.cnt["sve"])
                        em.inc(v.tensor_scalar(out=junk[:], in0=wr32[:],
                                               scalar1=-0.5, scalar2=None,
                                               op0=ALU.is_le), "sve")
                        em.wait(v, "sve", em.cnt["sve"])
                        em.inc(v.scalar_tensor_tensor(
                            out=xns, in0=wr32[:], scalar=0.5, in1=junk[:],
                            op0=ALU.is_ge, op1=ALU.subtract), "sve")
                        em.did_rd([xr, "srecip." + tag], "sve")
                        em.did_wr(["wr32", "junk", xnr], "sve")
                    em.sec("vector", wq_q)

                    em.grp("sync", "dstore", [xnr], [],
                           lambda sy, rr=rr, cc=cc, xns=xns: sy.dma_start(
                               out=wq_dst[rr * 128:(rr + 1) * 128,
                                          cc * D:(cc + 1) * D], in_=xns),
                           dmak=True)

        if cfg.phases >= 2:
            wprep(w1_d, w1q_d, F, D, cells[0:1, 3:4], "m1c", m1b[:, 0:1],
                  "w1")
            wprep(w2_d, w2q_d, D, F, cells[0:1, 4:5], "m2c", m2b[:, 0:1],
                  "w2")

        stkAW.close()
        em.fence()
        stkB = ExitStack()
        wqb = sb("wqb", [128, 2 * D], BF16, stkB)
        wkb = sb("wkb", [128, 2 * D], BF16, stkB)
        wvb = sb("wvb", [128, 2 * D], BF16, stkB)
        Qc = sb("Qc", [128, T], BF16, stkB)
        Kc = sb("Kc", [128, T], BF16, stkB)
        Vt = sb("Vt", [128, T], BF16, stkB)
        Vk = sb("Vk", [128, T], BF16, stkB)
        sqQK = sb("sqQK", [128, T], BF16, stkB)
        P0T = sb("P0T", [128, 1024], BF16, stkB)
        P1T = sb("P1T", [128, 1024], BF16, stkB)
        bcol = sb("bcol", [128, 2], F32, stkB)
        maxc = sb("maxc", [1, 64], F32, stkB)
        s01sb = sb("s01sb", [1, 1024], F32, stkB)
        r1row = sb("r1row", [1, 512], F32, stkB)
        s0sq = sb("s0sq", [1, 512], F32, stkB)
        erow = sb("erow", [1, 512], F32, stkB)
        crow = sb("crow", [1, 512], BF16, stkB)
        frow = sb("frow", [1, 512], BF16, stkB)
        Csb = sb("Csb", [128, 512], BF16, stkB)
        t1b = sb("t1b", [128, 512], F32, stkB)
        apb = sb("apb", [128, 512], F32, stkB)
        sqb = sb("sqb", [128, 512], BF16, stkB)

        QKSC = float(8.0 ** -0.5)
        LNC = math.log((1.0 - LAMBDA_INIT) * math.sqrt(128.0))
        nbq = T // 512

        for h in (range(H) if cfg.phases >= 3 else []):
            par = h % 2
            dsem = "dwq0" if par == 0 else "dwq1"
            wres = "whd.%d" % par

            def b_wdma(g, h=h, par=par, dsem=dsem):
                for wdst, wsrc in ((wqb, wq_d), (wkb, wk_d), (wvb, wv_d)):
                    dst = wdst[:, par * D:(par + 1) * D].rearrange(
                        "p (c j) -> p c j", j=128)
                    src = wsrc[:, h * 128:(h + 1) * 128].rearrange(
                        "(c p) j -> p c j", p=128)
                    em.inc(g.dma_start(out=dst, in_=src), dsem, 16)
                return None
            em.grp("gpsimd", dsem, [], [wres], b_wdma, dmak=True)

            em.tfence()
            with ExitStack() as pstk:
                psQ = pstk.enter_context(
                    nc.psum_tensor("psQ_%d" % h, [128, T], F32))
                psK = pstk.enter_context(
                    nc.psum_tensor("psK_%d" % h, [128, T], F32))

                def b_projqk(t, par=par):
                    for wsb, ps in ((wqb, psQ), (wkb, psK)):
                        for nb in range(T // 512):
                            for c in range(NDC):
                                ins = t.matmul(
                                    ps[:, nb * 512:(nb + 1) * 512],
                                    wsb[:, par * D + c * 128:
                                        par * D + (c + 1) * 128],
                                    xnT[:, c * T + nb * 512:
                                        c * T + (nb + 1) * 512],
                                    start=(c == 0), stop=(c == NDC - 1))
                    return ins
                em.grp("tensor", "ste", [wres, "xnT"], ["psQ", "psK"],
                       b_projqk)

                def b_evqk(v):
                    ins = v.tensor_scalar_mul(Qc[:], psQ[:], QKSC)
                    ins = v.tensor_scalar_mul(Kc[:], psK[:], QKSC)
                    return ins
                em.grp("vector", "sve", ["psQ", "psK"], ["Qc", "Kc"], b_evqk)

                def b_projv(t, par=par):
                    for nb in range(T // 512):
                        for c in range(NDC):
                            ins = t.matmul(
                                psQ[:, nb * 512:(nb + 1) * 512],
                                wvb[:, par * D + c * 128:
                                    par * D + (c + 1) * 128],
                                xnT[:, c * T + nb * 512:c * T + (nb + 1) * 512],
                                start=(c == 0), stop=(c == NDC - 1))
                    return ins
                em.grp("tensor", "ste", [wres, "xnT"], ["psQ"], b_projv)

                em.grp("vector", "sve", ["psQ"], ["Vt"],
                       lambda v: v.tensor_copy(out=Vt[:], in_=psQ[:]))

            em.tfence()
            with ExitStack() as pstk:
                psV = pstk.enter_context(
                    nc.psum_tensor("psV_%d" % h, [128, 2048], BF16))
                psN = pstk.enter_context(
                    nc.psum_tensor("psN_%d" % h, [1, 1024], F32))
                psB = pstk.enter_context(
                    nc.psum_tensor("psB_%d" % h, [128, 2], F32))

                for cb in range(NTT // 4):
                    pr = "psV.%d" % (cb % 2)
                    po = (cb % 2) * 1024

                    def b_vtr(t, cb=cb, po=po):
                        for ci in range(4):
                            c = cb * 4 + ci
                            ins = t.transpose(
                                psV[:, po + ci * 128:po + (ci + 1) * 128],
                                Vt[:, c * 128:(c + 1) * 128], eye[:])
                        return ins
                    em.grp("tensor", "ste", ["Vt", "eye"], [pr], b_vtr)

                    em.grp("vector", "sve", [pr], ["Vk"],
                           lambda v, cb=cb, po=po: v.tensor_copy(
                               out=Vk[:, cb * 512:(cb + 1) * 512],
                               in_=psV[:, po:po + 512]))

                for src_name, base in (("q", 0), ("k", 2 * nbq)):
                    srcb = Qc if src_name == "q" else Kc
                    em.grp("vector", "sve", ["Qc" if src_name == "q"
                                             else "Kc"], ["sqQK"],
                           lambda v, srcb=srcb: v.tensor_mul(
                               sqQK[:], srcb[:], srcb[:]))
                    for hh in range(2):
                        hs = slice(hh * 64, hh * 64 + 64)
                        for nb in range(nbq):
                            pn = "psN.%d" % hh
                            em.grp("tensor", "ste", ["sqQK", "ones"], [pn],
                                   lambda t, hh=hh, nb=nb, hs=hs: t.matmul(
                                       psN[0:1, hh * 512:hh * 512 + 512],
                                       ones_col[hs, 0:1],
                                       sqQK[hs, nb * 512:(nb + 1) * 512],
                                       start=True, stop=True))
                            em.grp("vector", "sve", [pn], ["maxc"],
                                   lambda v, hh=hh, nb=nb, base=base:
                                   v.reduce_max(
                                       out=maxc[0:1, base + hh * nbq + nb:
                                                base + hh * nbq + nb + 1],
                                       in_=psN[0:1, hh * 512:hh * 512 + 512],
                                       axis=AX.X))

                def b_bnd_v(v):
                    em.rd(v, ["maxc"])
                    em.wr(v, ["bndcells"])
                    for hh in range(2):
                        em.inc(v.reduce_max(
                            out=cells[0:1, 8 + hh:9 + hh],
                            in_=maxc[0:1, hh * nbq:(hh + 1) * nbq],
                            axis=AX.X), "sve")
                        em.inc(v.reduce_max(
                            out=cells[0:1, 10 + hh:11 + hh],
                            in_=maxc[0:1, 2 * nbq + hh * nbq:
                                     2 * nbq + (hh + 1) * nbq],
                            axis=AX.X), "sve")
                    em.wait(v, "sve", em.cnt["sve"])
                    for hh in range(2):
                        ins = v.tensor_add(cells[0:1, 12 + hh:13 + hh],
                                           cells[0:1, 8 + hh:9 + hh],
                                           cells[0:1, 10 + hh:11 + hh])
                        em.inc(ins, "sve")
                    em.did_rd(["maxc"], "sve")
                    em.did_wr(["bndcells"], "sve")
                em.sec("vector", b_bnd_v)

                def b_bnd_t(t):
                    for hh in range(2):
                        ins = t.matmul(psB[:, hh:hh + 1], ones_rowf[0:1, :],
                                       cells[0:1, 12 + hh:13 + hh],
                                       start=True, stop=True)
                    return ins
                em.grp("tensor", "ste", ["bndcells", "ones"], ["psB"],
                       b_bnd_t)

                def b_bnd_s(v):
                    for hh in range(2):
                        ins = v.tensor_scalar_mul(bcol[:, hh:hh + 1],
                                                  psB[:, hh:hh + 1], -0.5)
                    return ins
                em.grp("vector", "sve", ["psB"], ["bcol"], b_bnd_s)

            em.tfence()
            with ExitStack() as pstk:
                psS0 = pstk.enter_context(
                    nc.psum_tensor("psS0_%d" % h, [128, 1024], F32))
                psS1 = pstk.enter_context(
                    nc.psum_tensor("psS1_%d" % h, [128, 1024], F32))
                psA0 = pstk.enter_context(
                    nc.psum_tensor("psA0_%d" % h, [128, 512], F32))
                psA1 = pstk.enter_context(
                    nc.psum_tensor("psA1_%d" % h, [128, 512], F32))
                psRW = pstk.enter_context(
                    nc.psum_tensor("psRW_%d" % h, [1, 1024], F32))

                for qb in range(NQB):
                    jmax = (qb + 1) * JB - 1
                    diag_js = [j for j in range(jmax + 1) if j >= qb * JB]
                    ste0 = em.cnt["ste"]
                    ssc0 = em.cnt["ssc"]
                    sve0 = em.cnt["sve"]

                    def jinfo(j, qb=qb):
                        jp = j % 2
                        dg = (j >= qb * JB)
                        off = 128 * (j - qb * JB) if dg else 0
                        return jp, dg, off

                    PSN = ["psS0.0", "psS0.1", "psS1.0", "psS1.1"]
                    PN = ["P0T.0", "P0T.1", "P1T.0", "P1T.1"]

                    def qb_tensor(t, qb=qb, jmax=jmax, ste0=ste0,
                                  ssc0=ssc0, sve0=sve0, jinfo=jinfo,
                                  diag_js=diag_js):
                        em.rd(t, ["Qc", "Kc", "Vk", "ones"] + PN)
                        em.wr(t, PSN + ["psA0", "psA1", "psRW"])

                        def emit_S(j):
                            jp, dg, off = jinfo(j)
                            q0 = qb * QB + off
                            Nn = QB - off
                            if j >= 2:
                                em.wait(t, "ssc", ssc0 + j - 1)
                            t.matmul(
                                psS0[:, jp * 512 + off:(jp + 1) * 512],
                                Kc[0:64, j * 128:(j + 1) * 128],
                                Qc[0:64, q0:q0 + Nn], start=True, stop=True)
                            ins = t.matmul(
                                psS1[:, jp * 512 + off:(jp + 1) * 512],
                                Kc[64:128, j * 128:(j + 1) * 128],
                                Qc[64:128, q0:q0 + Nn], start=True,
                                stop=True)
                            em.inc(ins, "ste")

                        def emit_AV(j):
                            jp, dg, off = jinfo(j)
                            em.wait(t, "ssc", ssc0 + j + 1)
                            if dg:
                                nm = sum(1 for x in diag_js if x <= j)
                                em.wait(t, "sve", sve0 + nm)
                            st = (j == 0)
                            sp = (j == jmax)
                            t.matmul(psA0[:, off:512],
                                     Vk[:, j * 128:(j + 1) * 128],
                                     P0T[:, jp * 512 + off:(jp + 1) * 512],
                                     start=st, stop=sp)
                            t.matmul(psA1[:, off:512],
                                     Vk[:, j * 128:(j + 1) * 128],
                                     P1T[:, jp * 512 + off:(jp + 1) * 512],
                                     start=st, stop=sp)
                            t.matmul(psRW[0:1, off:512], ones_col[:, 0:1],
                                     P0T[:, jp * 512 + off:(jp + 1) * 512],
                                     start=st, stop=sp)
                            ins = t.matmul(
                                psRW[0:1, 512 + off:1024], ones_col[:, 0:1],
                                P1T[:, jp * 512 + off:(jp + 1) * 512],
                                start=st, stop=sp)
                            em.inc(ins, "ste")

                        emit_S(0)
                        for j in range(1, jmax + 1):
                            emit_S(j)
                            emit_AV(j - 1)
                        emit_AV(jmax)
                        em.did_rd(["Qc", "Kc", "Vk", "ones"] + PN, "ste")
                        em.did_wr(PSN + ["psA0", "psA1", "psRW"], "ste")
                    em.sec("tensor", qb_tensor)

                    def qb_scalar(s, qb=qb, jmax=jmax, ste0=ste0,
                                  ssc0=ssc0, jinfo=jinfo):
                        em.rd(s, ["bcol"])
                        for j in range(jmax + 1):
                            em.wait(s, "ste",
                                    ste0 + (1 if j == 0 else 2 * j))
                            jp, dg, off = jinfo(j)
                            s.activation(
                                out=P0T[:, jp * 512 + off:(jp + 1) * 512],
                                in_=psS0[:, jp * 512 + off:(jp + 1) * 512],
                                func=AF.Exp, bias=bcol[:, 0:1])
                            ins = s.activation(
                                out=P1T[:, jp * 512 + off:(jp + 1) * 512],
                                in_=psS1[:, jp * 512 + off:(jp + 1) * 512],
                                func=AF.Exp, bias=bcol[:, 1:2])
                            em.inc(ins, "ssc")
                        em.did_rd(["bcol"] + PSN, "ssc")
                        em.did_wr(PN, "ssc")
                    em.sec("scalar", qb_scalar)

                    def qb_mask(v, qb=qb, ssc0=ssc0, jinfo=jinfo,
                                diag_js=diag_js):
                        em.rd(v, ["tri"])
                        for j in diag_js:
                            jp, dg, off = jinfo(j)
                            em.wait(v, "ssc", ssc0 + j + 1)
                            v.tensor_mul(
                                P0T[:, jp * 512 + off:jp * 512 + off + 128],
                                P0T[:, jp * 512 + off:jp * 512 + off + 128],
                                tri[:])
                            ins = v.tensor_mul(
                                P1T[:, jp * 512 + off:jp * 512 + off + 128],
                                P1T[:, jp * 512 + off:jp * 512 + off + 128],
                                tri[:])
                            em.inc(ins, "sve")
                        em.did_rd(["tri"], "sve")
                    em.sec("vector", qb_mask)

                    em.grp("vector", "sve", ["psRW"], ["s01sb"],
                           lambda v: v.tensor_copy(out=s01sb[:],
                                                   in_=psRW[0:1, :]))

                    def ep_v1(v):
                        em.rd(v, ["s01sb", "lam"])
                        em.wr(v, ["r1row", "s0sq", "crow"])
                        em.inc(v.reciprocal(out=r1row[:],
                                            in_=s01sb[0:1, 512:1024]), "sve")
                        em.wait(v, "sve", em.cnt["sve"])
                        em.inc(v.tensor_mul(s0sq[:], r1row[:],
                                            s01sb[0:1, 512:1024]), "sve")
                        em.wait(v, "sve", em.cnt["sve"])
                        em.inc(v.tensor_scalar(out=s0sq[:], in0=s0sq[:],
                                               scalar1=-1.0, scalar2=2.0,
                                               op0=ALU.mult, op1=ALU.add),
                               "sve")
                        em.wait(v, "sve", em.cnt["sve"])
                        em.inc(v.tensor_mul(r1row[:], r1row[:], s0sq[:]),
                               "sve")
                        em.wait(v, "sve", em.cnt["sve"])
                        em.inc(v.scalar_tensor_tensor(
                            out=crow[:], in0=r1row[:], scalar=lam11,
                            in1=s01sb[0:1, 0:512], op0=ALU.mult,
                            op1=ALU.mult), "sve")
                        em.did_rd(["s01sb", "lam"], "sve")
                        em.did_wr(["r1row", "s0sq", "crow"], "sve")
                    em.sec("vector", ep_v1)

                    em.grp("tensor", "ste", ["crow", "ones"], ["psS0.1"],
                           lambda t: t.matmul(psS0[:, 512:1024],
                                              ones_rowb[0:1, :], crow[:],
                                              start=True, stop=True))

                    em.grp("vector", "sve", ["psS0.1"], ["Csb"],
                           lambda v: v.tensor_copy(out=Csb[:],
                                                   in_=psS0[:, 512:1024]))

                    def ep_v2(v):
                        em.rd(v, ["Csb", "psA0", "psA1"])
                        em.wr(v, ["t1b", "apb", "sqb"])
                        em.inc(v.tensor_mul(t1b[:], psA1[:], Csb[:]), "sve")
                        em.wait(v, "sve", em.cnt["sve"])
                        em.inc(v.tensor_sub(apb[:], psA0[:], t1b[:]), "sve")
                        em.wait(v, "sve", em.cnt["sve"])
                        em.inc(v.tensor_mul(sqb[:], apb[:], apb[:]), "sve")
                        em.did_rd(["Csb", "psA0", "psA1"], "sve")
                        em.did_wr(["t1b", "apb", "sqb"], "sve")
                    em.sec("vector", ep_v2)

                    em.grp("tensor", "ste", ["sqb", "ones"], ["psRW"],
                           lambda t: t.matmul(psRW[0:1, 0:512],
                                              ones_col[:, 0:1], sqb[:],
                                              start=True, stop=True))

                    def ep_v3(v):
                        em.rd(v, ["psRW", "s01sb"])
                        em.wr(v, ["s0sq", "erow"])
                        em.inc(v.tensor_mul(s0sq[:], s01sb[0:1, 0:512],
                                            s01sb[0:1, 0:512]), "sve")
                        em.wait(v, "sve", em.cnt["sve"])
                        em.inc(v.scalar_tensor_tensor(
                            out=erow[:], in0=s0sq[:], scalar=128.0 * 1e-5,
                            in1=psRW[0:1, 0:512], op0=ALU.mult, op1=ALU.add),
                            "sve")
                        em.did_rd(["psRW", "s01sb"], "sve")
                        em.did_wr(["s0sq", "erow"], "sve")
                    em.sec("vector", ep_v3)

                    def ep_s3(s):
                        em.rd(s, ["erow"])
                        em.wr(s, ["erow", "frow"])
                        em.inc(s.activation(out=erow[:], in_=erow[:],
                                            func=AF.Ln), "ssc")
                        em.wait(s, "ssc", em.cnt["ssc"])
                        em.inc(s.activation(out=frow[:], in_=erow[:],
                                            func=AF.Exp, scale=-0.5,
                                            bias=cells[0:1, 15:16]), "ssc")
                        em.did_wr(["erow", "frow"], "ssc")
                    em.sec("scalar", ep_s3)

                    em.grp("tensor", "ste", ["frow", "ones"], ["psS1.1"],
                           lambda t: t.matmul(psS1[:, 512:1024],
                                              ones_rowb[0:1, :], frow[:],
                                              start=True, stop=True))

                    em.grp("vector", "sve", ["psS1.1", "apb"], ["attnT"],
                           lambda v, h=h, qb=qb: v.tensor_mul(
                               attnT[:, h * T + qb * QB:h * T + (qb + 1) * QB],
                               apb[:], psS1[:, 512:1024]))

        stkB.close()
        em.fence()
        stkB2 = ExitStack()
        wob = sb("wob", [128, 2 * NDC * 512], BF16, stkB2)
        wotile = sb("wotile", [128, 2 * 512], BF16, stkB2)

        em.tfence()
        with nc.psum_tensor("psO", [128, 1024], F32) as psO:
            for dc in (range(NDOC) if cfg.phases >= 4 else []):
                par = dc % 2
                dsem = "dwoB0" if par == 0 else "dwoB1"
                wres = "wob.%d" % par

                def b2_dma(g, dc=dc, par=par):
                    dst = wob[:, par * NDC * 512:
                              (par + 1) * NDC * 512].rearrange(
                        "p (c j) -> p c j", j=512)
                    src = wo_d[:, dc * 512:(dc + 1) * 512].rearrange(
                        "(c p) j -> p c j", p=128)
                    return g.dma_start(out=dst, in_=src)
                em.grp("gpsimd", dsem, [], [wres], b2_dma, dmak=True)

                for tt in range(NTT):
                    pp = tt % 2
                    pr = "psO.%d" % pp
                    wtr = "wotile.%d" % pp

                    def b2_mm(t, tt=tt, pp=pp, par=par):
                        for c in range(H):
                            ins = t.matmul(
                                psO[:, pp * 512:(pp + 1) * 512],
                                attnT[:, c * T + tt * 128:
                                      c * T + (tt + 1) * 128],
                                wob[:, par * NDC * 512 + c * 512:
                                    par * NDC * 512 + (c + 1) * 512],
                                start=(c == 0), stop=(c == H - 1))
                        return ins
                    em.grp("tensor", "ste", [wres, "attnT"], [pr], b2_mm)

                    em.grp("vector", "sve", [pr], [wtr],
                           lambda v, pp=pp: v.tensor_copy(
                               out=wotile[:, pp * 512:(pp + 1) * 512],
                               in_=psO[:, pp * 512:(pp + 1) * 512]))

                    em.grp("sync", "dstore", [wtr], [],
                           lambda sy, dc=dc, tt=tt, pp=pp: sy.dma_start(
                               out=wo_scr_d[tt * 128:(tt + 1) * 128,
                                            dc * 512:(dc + 1) * 512],
                               in_=wotile[:, pp * 512:(pp + 1) * 512]),
                           dmak=True)

        if cfg.debug:
            def dbg_dump(sy):
                em.rd(sy, ["xnT", "attnT", "lam", "lni", "lnb2"])
                em.inc(sy.dma_start(out=xnT_dbg[:], in_=xnT[:]), "dstore", 16)
                em.inc(sy.dma_start(out=attnT_dbg[:], in_=attnT[:]),
                       "dstore", 16)
                em.inc(sy.dma_start(out=cells_dbg[:], in_=cells[:]),
                       "dstore", 16)
                em.inc(sy.dma_start(out=lni_dbg[:, 0:cfg.NTT], in_=lni[:]),
                       "dstore", 16)
                em.inc(sy.dma_start(out=lni_dbg[:, cfg.NTT:], in_=lnb2[:]),
                       "dstore", 16)
                em.wait(sy, "dstore", em.cnt["dstore"])
            em.sec("sync", dbg_dump)
        stkB2.close()
        stkAB.close()
        em.fence()
        stkC = ExitStack()
        qxT = sb("qxT", [128, NDC * GT], BF16, stkC)
        qtile = sb("qtile", [128, D], BF16, stkC)
        xf32 = sb("xf32c", [128, 2 * D], F32, stkC)
        i32buf = sb("i32bufc", [128, 1024], I32, stkC)
        fbuf = sb("fbufc", [128, 1024], F32, stkC)
        rbuf = sb("rbufc", [128, 1024], F32, stkC)
        ebuf = sb("ebufc", [128, 1024], F32, stkC)
        hbuf = sb("hbuf", [128, GTT * F], BF16, stkC)
        hmax = sb("hmax", [128, GTT * NFC], F32, stkC)
        axc = sb("axc", [128, GTT], F32, stkC)
        sxq = sb("sxq", [128, GTT], F32, stkC)
        hscol = sb("hscol", [128, GTT], F32, stkC)
        shq = sb("shq", [128, GTT], F32, stkC)
        hs2 = sb("hs2", [128, GTT], F32, stkC)
        hqT = sb("hqT", [128, NFC128 * 256], BF16, stkC)
        w1t = sb("w1t", [128, 2 * NDC * 512], BF16, stkC)
        w2t = sb("w2t", [128, 2 * 1024], BF16, stkC)
        wotc = sb("wotc", [128, 2 * 512], BF16, stkC)
        xrest = sb("xrest", [128, 2 * 512], F32, stkC)
        tAb = sb("tAb", [128, 512], F32, stkC)
        outb = sb("outb", [128, 2 * 512], F32, stkC)

        dstore_final = em.cnt["dstore"]

        for g in (range(NG) if cfg.phases >= 5 else []):
            em.tfence()
            with nc.psum_tensor("psTq_%d" % g, [128, 2048], BF16) as psTq:
                for tl in range(GTT):
                    tt = g * GTT + tl
                    par = tt % 2
                    dsem = "dxa0" if par == 0 else "dxa1"
                    xr = "xf.%d" % par
                    xnr = "xnb.%d" % par
                    xs = xf32[:, par * D:(par + 1) * D]
                    xns = xs

                    em.grp("sync", dsem, [], [xr, xnr],
                           lambda sy, tt=tt, xs=xs: sy.dma_start(
                               out=xs, in_=x_d[tt * 128:(tt + 1) * 128, :]),
                           dmak=True)

                    em.grp("vector", "sve", [xr, "lni", "lnb2"], [xnr],
                           lambda v, tt=tt, xs=xs, xns=xns: v.tensor_scalar(
                               out=xns, in0=xs, scalar1=lni[:, tt:tt + 1],
                               scalar2=lnb2[:, tt:tt + 1], op0=ALU.mult,
                               op1=ALU.add))

                    def c_ax(v, tl=tl, xns=xns, xnr=xnr):
                        em.rd(v, [xnr, "mb.w1"])
                        em.wr(v, ["axc", "sxq", "hscol"])
                        em.inc(v.reduce_max(out=axc[:, tl:tl + 1], in_=xns,
                                            axis=AX.X,
                                            apply_absolute_value=True),
                               "sve")
                        em.wait(v, "sve", em.cnt["sve"])
                        em.inc(v.tensor_scalar_max(axc[:, tl:tl + 1],
                                                   axc[:, tl:tl + 1], 1e-5),
                               "sve")
                        em.wait(v, "sve", em.cnt["sve"])
                        em.inc(v.reciprocal(out=sxq[:, tl:tl + 1],
                                            in_=axc[:, tl:tl + 1]), "sve")
                        em.wait(v, "sve", em.cnt["sve"])
                        em.inc(v.tensor_mul(scol[:, 7:8], axc[:, tl:tl + 1],
                                            sxq[:, tl:tl + 1]), "sve")
                        em.wait(v, "sve", em.cnt["sve"])
                        em.inc(v.tensor_scalar(out=scol[:, 7:8],
                                               in0=scol[:, 7:8],
                                               scalar1=-1.0, scalar2=2.0,
                                               op0=ALU.mult, op1=ALU.add),
                               "sve")
                        em.wait(v, "sve", em.cnt["sve"])
                        em.inc(v.tensor_mul(sxq[:, tl:tl + 1],
                                            sxq[:, tl:tl + 1],
                                            scol[:, 7:8]), "sve")
                        em.wait(v, "sve", em.cnt["sve"])
                        em.inc(v.tensor_scalar_mul(sxq[:, tl:tl + 1],
                                                   sxq[:, tl:tl + 1], 127.0),
                               "sve")
                        em.inc(v.scalar_tensor_tensor(
                            out=hscol[:, tl:tl + 1], in0=axc[:, tl:tl + 1],
                            scalar=1.0 / 127.0, in1=m1b[:], op0=ALU.mult,
                            op1=ALU.mult), "sve")
                        em.did_rd([xnr, "mb.w1"], "sve")
                        em.did_wr(["axc", "sxq", "hscol", "scol.nt"], "sve")
                    em.sec("vector", c_ax)

                    for qh in range(D // 1024):
                        xh = xns[:, qh * 1024:(qh + 1) * 1024]
                        qth = qtile[:, qh * 1024:(qh + 1) * 1024]

                        def c_q(v, tl=tl, xh=xh, xnr=xnr):
                            em.rd(v, [xnr, "sxq"])
                            em.wr(v, ["i32buf"])
                            em.inc(v.tensor_scalar(
                                out=i32buf[:], in0=xh,
                                scalar1=sxq[:, tl:tl + 1], scalar2=None,
                                op0=ALU.mult), "sve")
                            em.did_rd([xnr, "sxq"], "sve")
                            em.did_wr(["i32buf"], "sve")
                        em.sec("vector", c_q)

                        em.grp("vector", "sve", ["i32buf"], ["f0c"],
                               lambda v: v.tensor_scalar(
                                   out=fbuf[:], in0=i32buf[:], scalar1=2.0,
                                   scalar2=None, op0=ALU.subtract))

                        def c_q2(v, tl=tl, xh=xh, qth=qth, xnr=xnr):
                            em.rd(v, [xnr, "sxq", "f0c"])
                            em.wr(v, ["rbuf", "ebuf", "f0c", "qtile"])
                            em.inc(v.scalar_tensor_tensor(
                                out=rbuf[:], in0=xh,
                                scalar=sxq[:, tl:tl + 1], in1=fbuf[:],
                                op0=ALU.mult, op1=ALU.subtract), "sve")
                            em.wait(v, "sve", em.cnt["sve"])
                            em.inc(v.scalar_tensor_tensor(
                                out=ebuf[:], in0=rbuf[:], scalar=0.5,
                                in1=fbuf[:], op0=ALU.is_gt, op1=ALU.add),
                                "sve")
                            em.wait(v, "sve", em.cnt["sve"])
                            em.inc(v.scalar_tensor_tensor(
                                out=fbuf[:], in0=rbuf[:], scalar=1.5,
                                in1=ebuf[:], op0=ALU.is_ge, op1=ALU.add),
                                "sve")
                            em.wait(v, "sve", em.cnt["sve"])
                            em.inc(v.scalar_tensor_tensor(
                                out=ebuf[:], in0=rbuf[:], scalar=2.5,
                                in1=fbuf[:], op0=ALU.is_gt, op1=ALU.add),
                                "sve")
                            em.wait(v, "sve", em.cnt["sve"])
                            em.inc(v.scalar_tensor_tensor(
                                out=qth, in0=rbuf[:], scalar=3.5,
                                in1=ebuf[:], op0=ALU.is_ge, op1=ALU.add),
                                "sve")
                            em.did_rd([xnr, "sxq"], "sve")
                            em.did_wr(["rbuf", "ebuf", "f0c", "qtile"],
                                      "sve")
                        em.sec("vector", c_q2)

                        if getattr(cfg, "dbgqx", None) == (g, tl):
                            def dbgx(sy, qh=qh, xh=xh, qth=qth, xnr=xnr):
                                em.rd(sy, ["i32buf", "rbuf", "qtile",
                                           "sxq", xnr])
                                em.inc(sy.dma_start(
                                    out=dxq_i32[:, qh * 1024:
                                                (qh + 1) * 1024],
                                    in_=i32buf[:]), "ddbg", 16)
                                em.inc(sy.dma_start(
                                    out=dxq_rp[:, qh * 1024:(qh + 1) * 1024],
                                    in_=rbuf[:]), "ddbg", 16)
                                em.inc(sy.dma_start(
                                    out=dxq_out[:, qh * 1024:
                                                (qh + 1) * 1024],
                                    in_=qth), "ddbg", 16)
                                em.inc(sy.dma_start(
                                    out=dxq_in[:, qh * 1024:(qh + 1) * 1024],
                                    in_=xh), "ddbg", 16)
                                em.inc(sy.dma_start(out=dxq_s[:],
                                                    in_=sxq[:, 0:4]),
                                       "ddbg", 16)
                                em.wait(sy, "ddbg", em.cnt["ddbg"])
                            em.sec("sync", dbgx)
                            em.sec("vector",
                                   lambda v: em.wait(v, "ddbg",
                                                     em.cnt["ddbg"]))

                    for cb in range(NDC // 4):
                        pr = "psTq.%d" % (cb % 2)
                        po = (cb % 2) * 1024

                        def c_qtr(t, cb=cb, po=po):
                            for ci in range(4):
                                c = cb * 4 + ci
                                ins = t.transpose(
                                    psTq[:, po + ci * 128:po + (ci + 1) * 128],
                                    qtile[:, c * 128:(c + 1) * 128], eye[:])
                            return ins
                        em.grp("tensor", "ste", ["qtile", "eye"], [pr],
                               c_qtr)

                        def c_qtrev(v, tl=tl, cb=cb, po=po):
                            for ci in range(4):
                                c = cb * 4 + ci
                                ins = v.tensor_copy(
                                    out=qxT[:, c * GT + tl * 128:
                                            c * GT + (tl + 1) * 128],
                                    in_=psTq[:, po + ci * 128:po + (ci + 1) * 128])
                            return ins
                        em.grp("vector", "sve", [pr], ["qxT"], c_qtrev)

            em.tfence()
            with nc.psum_tensor("psH_%d" % g, [128, 1024], F32) as psH:
                mm1_ssc0 = em.cnt["ssc"]
                mm1_ste_after = {}
                for fc in range(NFC):
                    par = fc % 2
                    dsem = "dw1t0" if par == 0 else "dw1t1"
                    wres = "w1t.%d" % par

                    def c_w1dma(sy, fc=fc, par=par, wres=wres, dsem=dsem):
                        em.wr(sy, [wres])
                        em.wait(sy, "dstore", dstore_final)
                        for c in range(NDC):
                            em.inc(sy.dma_start(
                                out=w1t[:, par * NDC * 512 + c * 512:
                                        par * NDC * 512 + (c + 1) * 512],
                                in_=w1q_d[fc * 512:(fc + 1) * 512,
                                          c * 128:(c + 1) * 128],
                                transpose=True), dsem, 16)
                        em.did_wr([wres], dsem)
                    em.sec("sync", c_w1dma)

                    for tl in range(GTT):
                        idx = fc * GTT + tl
                        pp = idx % 2

                        def c_mm1(t, fc=fc, tl=tl, pp=pp, par=par, idx=idx,
                                  wres=wres):
                            em.rd(t, [wres, "qxT"])
                            if idx >= 2:
                                em.wait(t, "ssc", mm1_ssc0 + idx - 1)
                            for c in range(NDC):
                                ins = t.matmul(
                                    psH[:, pp * 512:(pp + 1) * 512],
                                    qxT[:, c * GT + tl * 128:
                                        c * GT + (tl + 1) * 128],
                                    w1t[:, par * NDC * 512 + c * 512:
                                        par * NDC * 512 + (c + 1) * 512],
                                    start=(c == 0), stop=(c == NDC - 1))
                            em.inc(ins, "ste")
                            em.did_rd([wres, "qxT"], "ste")
                        em.sec("tensor", c_mm1)
                        mm1_ste_after[idx] = em.cnt["ste"]

                def c_silu_all(s, g=g):
                    em.rd(s, ["hscol"])
                    em.wr(s, ["h.%d.%d" % (tl, fc)
                              for fc in range(NFC) for tl in range(GTT)])
                    for fc in range(NFC):
                        for tl in range(GTT):
                            idx = fc * GTT + tl
                            pp = idx % 2
                            em.wait(s, "ste", mm1_ste_after[idx])
                            ins = s.activation(
                                out=hbuf[:, tl * F + fc * 512:
                                         tl * F + (fc + 1) * 512],
                                in_=psH[:, pp * 512:(pp + 1) * 512],
                                func=AF.Silu, scale=hscol[:, tl:tl + 1])
                            em.inc(ins, "ssc")
                            em.did_wr(["h.%d.%d" % (tl, fc)], "ssc")
                    em.did_rd(["hscol"], "ssc")
                em.sec("scalar", c_silu_all)

                def c_hmax_all(v):
                    for fc in range(NFC):
                        for tl in range(GTT):
                            idx = fc * GTT + tl
                            em.wait(v, "ssc", mm1_ssc0 + idx + 1)
                            ins = v.reduce_max(
                                out=hmax[:, tl * NFC + fc:
                                         tl * NFC + fc + 1],
                                in_=hbuf[:, tl * F + fc * 512:
                                         tl * F + (fc + 1) * 512],
                                axis=AX.X, apply_absolute_value=True)
                            em.inc(ins, "sve")
                            em.did_rd(["h.%d.%d" % (tl, fc)], "sve")
                    em.did_wr(["hmax"], "sve")
                em.sec("vector", c_hmax_all)

            for th in range(GTT // 2):
                em.tfence()
                with nc.psum_tensor("psTh_%d_%d" % (g, th), [128, 2048],
                                    BF16) as psTh:
                    for tl in (2 * th, 2 * th + 1):
                        def c_hscale(v, tl=tl):
                            em.rd(v, ["hmax", "mb.w2"])
                            em.wr(v, ["axc", "shq", "hs2"])
                            em.inc(v.reduce_max(
                                out=axc[:, tl:tl + 1],
                                in_=hmax[:, tl * NFC:(tl + 1) * NFC],
                                axis=AX.X), "sve")
                            em.wait(v, "sve", em.cnt["sve"])
                            em.inc(v.tensor_scalar_max(axc[:, tl:tl + 1],
                                                       axc[:, tl:tl + 1],
                                                       1e-5), "sve")
                            em.wait(v, "sve", em.cnt["sve"])
                            em.inc(v.reciprocal(out=shq[:, tl:tl + 1],
                                                in_=axc[:, tl:tl + 1]),
                                   "sve")
                            em.wait(v, "sve", em.cnt["sve"])
                            em.inc(v.tensor_mul(scol[:, 7:8],
                                                axc[:, tl:tl + 1],
                                                shq[:, tl:tl + 1]), "sve")
                            em.wait(v, "sve", em.cnt["sve"])
                            em.inc(v.tensor_scalar(out=scol[:, 7:8],
                                                   in0=scol[:, 7:8],
                                                   scalar1=-1.0, scalar2=2.0,
                                                   op0=ALU.mult,
                                                   op1=ALU.add), "sve")
                            em.wait(v, "sve", em.cnt["sve"])
                            em.inc(v.tensor_mul(shq[:, tl:tl + 1],
                                                shq[:, tl:tl + 1],
                                                scol[:, 7:8]), "sve")
                            em.wait(v, "sve", em.cnt["sve"])
                            em.inc(v.tensor_scalar_mul(shq[:, tl:tl + 1],
                                                       shq[:, tl:tl + 1],
                                                       127.0), "sve")
                            em.inc(v.scalar_tensor_tensor(
                                out=hs2[:, tl:tl + 1],
                                in0=axc[:, tl:tl + 1], scalar=1.0 / 127.0,
                                in1=m2b[:], op0=ALU.mult, op1=ALU.mult),
                                "sve")
                            em.did_rd(["hmax", "mb.w2"], "sve")
                            em.did_wr(["axc", "shq", "hs2"], "sve")
                        em.sec("vector", c_hscale)

                        for sl in range(F // D):
                            hres = ["h.%d.%d" % (tl, fc) for fc in
                                    range(sl * (D // 512),
                                          (sl + 1) * (D // 512))]

                            for qh in range(D // 1024):
                                hh_ap = hbuf[:, tl * F + sl * D + qh * 1024:
                                             tl * F + sl * D + (qh + 1) * 1024]
                                qth = qtile[:, qh * 1024:(qh + 1) * 1024]

                                def c_hq(v, tl=tl, hh_ap=hh_ap, hres=hres):
                                    em.rd(v, ["shq"] + hres)
                                    em.wr(v, ["i32buf"])
                                    em.inc(v.tensor_scalar(
                                        out=i32buf[:], in0=hh_ap,
                                        scalar1=shq[:, tl:tl + 1],
                                        scalar2=None, op0=ALU.mult), "sve")
                                    em.did_rd(["shq"] + hres, "sve")
                                    em.did_wr(["i32buf"], "sve")
                                em.sec("vector", c_hq)

                                em.grp("vector", "sve", ["i32buf"], ["f0c"],
                                       lambda v: v.tensor_scalar(
                                           out=fbuf[:], in0=i32buf[:],
                                           scalar1=2.0, scalar2=None,
                                           op0=ALU.subtract))

                                def c_hq2(v, tl=tl, hh_ap=hh_ap, qth=qth,
                                          hres=hres):
                                    em.rd(v, ["shq", "f0c"] + hres)
                                    em.wr(v, ["rbuf", "ebuf", "f0c",
                                              "qtile"])
                                    em.inc(v.scalar_tensor_tensor(
                                        out=rbuf[:], in0=hh_ap,
                                        scalar=shq[:, tl:tl + 1],
                                        in1=fbuf[:], op0=ALU.mult,
                                        op1=ALU.subtract), "sve")
                                    em.wait(v, "sve", em.cnt["sve"])
                                    em.inc(v.scalar_tensor_tensor(
                                        out=ebuf[:], in0=rbuf[:], scalar=0.5,
                                        in1=fbuf[:], op0=ALU.is_gt,
                                        op1=ALU.add), "sve")
                                    em.wait(v, "sve", em.cnt["sve"])
                                    em.inc(v.scalar_tensor_tensor(
                                        out=fbuf[:], in0=rbuf[:], scalar=1.5,
                                        in1=ebuf[:], op0=ALU.is_ge,
                                        op1=ALU.add), "sve")
                                    em.wait(v, "sve", em.cnt["sve"])
                                    em.inc(v.scalar_tensor_tensor(
                                        out=ebuf[:], in0=rbuf[:], scalar=2.5,
                                        in1=fbuf[:], op0=ALU.is_gt,
                                        op1=ALU.add), "sve")
                                    em.wait(v, "sve", em.cnt["sve"])
                                    em.inc(v.scalar_tensor_tensor(
                                        out=qth, in0=rbuf[:], scalar=3.5,
                                        in1=ebuf[:], op0=ALU.is_ge,
                                        op1=ALU.add), "sve")
                                    em.did_rd(["shq"] + hres, "sve")
                                    em.did_wr(["rbuf", "ebuf", "f0c",
                                               "qtile"], "sve")
                                em.sec("vector", c_hq2)

                                if getattr(cfg, "dbgq", None) == \
                                        (g, tl, sl, qh):
                                    def dbgh(sy, hh_ap=hh_ap, qth=qth):
                                        em.rd(sy, ["i32buf", "rbuf",
                                                   "qtile", "shq"])
                                        em.inc(sy.dma_start(
                                            out=dhq_i32[:], in_=i32buf[:]),
                                            "ddbg", 16)
                                        em.inc(sy.dma_start(
                                            out=dhq_rp[:], in_=rbuf[:]),
                                            "ddbg", 16)
                                        em.inc(sy.dma_start(
                                            out=dhq_out[:], in_=qth),
                                            "ddbg", 16)
                                        em.inc(sy.dma_start(
                                            out=dhq_in[:], in_=hh_ap),
                                            "ddbg", 16)
                                        em.inc(sy.dma_start(
                                            out=dhq_s[:], in_=shq[:, 0:4]),
                                            "ddbg", 16)
                                        em.wait(sy, "ddbg",
                                                em.cnt["ddbg"])
                                    em.sec("sync", dbgh)
                                    em.sec("vector",
                                           lambda v: em.wait(
                                               v, "ddbg", em.cnt["ddbg"]))

                            for cb in range(NDC // 4):
                                pr = "psTh.%d" % (cb % 2)
                                po = (cb % 2) * 1024

                                def c_htr(t, cb=cb, po=po):
                                    for ci in range(4):
                                        c = cb * 4 + ci
                                        ins = t.transpose(
                                            psTh[:, po + ci * 128:
                                                 po + (ci + 1) * 128],
                                            qtile[:, c * 128:(c + 1) * 128],
                                            eye[:])
                                    return ins
                                em.grp("tensor", "ste", ["qtile", "eye"],
                                       [pr], c_htr)

                                def c_htrev(v, tl=tl, sl=sl, cb=cb, th=th,
                                            po=po):
                                    ttp = tl - 2 * th
                                    for ci in range(4):
                                        fc128 = sl * NDC + cb * 4 + ci
                                        ins = v.tensor_copy(
                                            out=hqT[:, fc128 * 256 +
                                                    ttp * 128:
                                                    fc128 * 256 +
                                                    (ttp + 1) * 128],
                                            in_=psTh[:, po + ci * 128:
                                                     po + (ci + 1) * 128])
                                    return ins
                                em.grp("vector", "sve", [pr], ["hqT"],
                                       c_htrev)

                em.tfence()
                with nc.psum_tensor("psO2_%d_%d" % (g, th), [128, 4096],
                                    F32) as psO2:
                    for dcp in range(NDOC // 2):
                        pb = dcp % 2
                        rrs = ["psO2.%d.%d" % (pb, k) for k in range(4)]
                        for fc128 in range(NFC128):
                            par = fc128 % 2
                            dsem = "dw2t0" if par == 0 else "dw2t1"
                            wres = "w2t.%d" % par

                            def c_w2sec(sy, dcp=dcp, fc128=fc128, par=par,
                                        dsem=dsem, wres=wres):
                                em.wr(sy, [wres])
                                em.wait(sy, "dstore", dstore_final)
                                ins = sy.dma_start(
                                    out=w2t[:, par * 1024:(par + 1) * 1024],
                                    in_=w2q_d[dcp * 1024:(dcp + 1) * 1024,
                                              fc128 * 128:(fc128 + 1) * 128],
                                    transpose=True)
                                em.inc(ins, dsem, 16)
                                em.did_wr([wres], dsem)
                            em.sec("sync", c_w2sec)

                            def c_mm2(t, fc128=fc128, par=par, pb=pb,
                                      wres=wres, rrs=rrs):
                                st = (fc128 == 0)
                                sp = (fc128 == NFC128 - 1)
                                em.rd(t, [wres, "hqT"])
                                if st:
                                    em.wr(t, rrs)
                                for k in range(4):
                                    ti, di = divmod(k, 2)
                                    ins = t.matmul(
                                        psO2[:, (pb * 4 + k) * 512:
                                             (pb * 4 + k + 1) * 512],
                                        hqT[:, fc128 * 256 + ti * 128:
                                            fc128 * 256 + (ti + 1) * 128],
                                        w2t[:, par * 1024 + di * 512:
                                            par * 1024 + (di + 1) * 512],
                                        start=st, stop=sp)
                                em.inc(ins, "ste")
                                em.did_rd([wres, "hqT"], "ste")
                                if sp:
                                    em.did_wr(rrs, "ste")
                            em.sec("tensor", c_mm2)

                        for k in range(4):
                            ti, di = divmod(k, 2)
                            tl = 2 * th + ti
                            tt = g * GTT + tl
                            dc = dcp * 2 + di
                            pp = k % 2
                            rr = rrs[k]
                            wor = "wotc.%d" % pp
                            xrr = "xrest.%d" % pp
                            obr = "outb.%d" % pp
                            dsemw = "dwoC0" if pp == 0 else "dwoC1"
                            dsemx = "dres0" if pp == 0 else "dres1"

                            def c_lodma(sy, tt=tt, dc=dc, pp=pp, wor=wor,
                                        xrr=xrr, dsemw=dsemw, dsemx=dsemx):
                                em.wr(sy, [wor, xrr])
                                em.wait(sy, "dstore", dstore_final)
                                i1 = sy.dma_start(
                                    out=wotc[:, pp * 512:(pp + 1) * 512],
                                    in_=wo_scr_d[tt * 128:(tt + 1) * 128,
                                                 dc * 512:(dc + 1) * 512])
                                em.inc(i1, dsemw, 16)
                                i2 = sy.dma_start(
                                    out=xrest[:, pp * 512:(pp + 1) * 512],
                                    in_=x_d[tt * 128:(tt + 1) * 128,
                                            dc * 512:(dc + 1) * 512])
                                em.inc(i2, dsemx, 16)
                                em.did_wr([wor], dsemw)
                                em.did_wr([xrr], dsemx)
                            em.sec("sync", c_lodma)

                            em.grp("vector", "sve", [rr, "hs2"], ["tAb"],
                                   lambda v, pb=pb, k=k, tl=tl:
                                   v.tensor_scalar_mul(
                                       tAb[:],
                                       psO2[:, (pb * 4 + k) * 512:
                                            (pb * 4 + k + 1) * 512],
                                       hs2[:, tl:tl + 1]))

                            def c_out(v, pp=pp, wor=wor, xrr=xrr, obr=obr):
                                em.rd(v, ["tAb", wor, xrr])
                                em.wr(v, [obr])
                                em.inc(v.tensor_add(
                                    outb[:, pp * 512:(pp + 1) * 512],
                                    tAb[:],
                                    wotc[:, pp * 512:(pp + 1) * 512]), "sve")
                                em.wait(v, "sve", em.cnt["sve"])
                                em.inc(v.tensor_add(
                                    outb[:, pp * 512:(pp + 1) * 512],
                                    outb[:, pp * 512:(pp + 1) * 512],
                                    xrest[:, pp * 512:(pp + 1) * 512]),
                                    "sve")
                                em.did_rd(["tAb", wor, xrr], "sve")
                                em.did_wr([obr], "sve")
                            em.sec("vector", c_out)

                            em.grp("sync", "dout", [obr], [],
                                   lambda sy, tt=tt, dc=dc, pp=pp:
                                   sy.dma_start(
                                       out=y_d[tt * 128:(tt + 1) * 128,
                                               dc * 512:(dc + 1) * 512],
                                       in_=outb[:, pp * 512:(pp + 1) * 512]),
                                   dmak=True)

        def fin(sy):
            em.wait(sy, "dout", em.cnt["dout"])
            em.wait(sy, "dstore", em.cnt["dstore"])
        em.sec("sync", fin)
        stkC.close()

    gctx.close()
    nc.finalize()
    return nc


_NC_CACHE = {}


def _get_nc():
    if "nc" not in _NC_CACHE:
        _NC_CACHE["nc"] = build(CFG_FULL)
    return _NC_CACHE["nc"]


def kernel(L0_attn_x, L0_attn_Wq, L0_attn_Wk, L0_attn_Wv, L0_attn_Wo,
           L0_attn_lq1, L0_attn_lk1, L0_attn_lq2, L0_attn_lk2,
           L0_ffn_W1, L0_ffn_W2):
    x = np.ascontiguousarray(np.asarray(L0_attn_x, dtype=np.float32))
    ws = {
        "Wq": np.ascontiguousarray(np.asarray(L0_attn_Wq, dtype=np.float32)),
        "Wk": np.ascontiguousarray(np.asarray(L0_attn_Wk, dtype=np.float32)),
        "Wv": np.ascontiguousarray(np.asarray(L0_attn_Wv, dtype=np.float32)),
        "Wo": np.ascontiguousarray(np.asarray(L0_attn_Wo, dtype=np.float32)),
        "lq1": np.ascontiguousarray(np.asarray(L0_attn_lq1, dtype=np.float32)),
        "lk1": np.ascontiguousarray(np.asarray(L0_attn_lk1, dtype=np.float32)),
        "lq2": np.ascontiguousarray(np.asarray(L0_attn_lq2, dtype=np.float32)),
        "lk2": np.ascontiguousarray(np.asarray(L0_attn_lk2, dtype=np.float32)),
        "W1": np.ascontiguousarray(np.asarray(L0_ffn_W1, dtype=np.float32)),
        "W2": np.ascontiguousarray(np.asarray(L0_ffn_W2, dtype=np.float32)),
    }
    nc = _get_nc()
    n = x.shape[0]
    in_maps = [dict(ws, x=np.ascontiguousarray(x[b])) for b in range(n)]
    res = run_bass_kernel_spmd(nc, in_maps, core_ids=list(range(n)))
    return np.stack([np.asarray(res.results[b]["y"]) for b in range(n)],
                    axis=0)


# version 09
# speedup vs baseline: 1.1722x; 1.0091x over previous
import math
from contextlib import ExitStack

import numpy as np

import concourse.bass as bass
import concourse.mybir as mybir
from concourse.bass_utils import run_bass_kernel_spmd

F32 = mybir.dt.float32
BF16 = mybir.dt.bfloat16
I32 = mybir.dt.int32
AF = mybir.ActivationFunctionType
ALU = mybir.AluOpType
AX = mybir.AxisListType

DEPTH = 12
LAMBDA_INIT = 0.8 - 0.6 * math.exp(-0.3 * DEPTH)
LNC_CONST = math.log((1.0 - LAMBDA_INIT) * math.sqrt(128.0))
LNC_HI = float(np.float32(LNC_CONST))
LNC_LO = float(np.float32(LNC_CONST - LNC_HI))


class Cfg:
    def __init__(self, T, D, H, F, NG):
        self.T = T
        self.D = D
        self.H = H
        self.F = F
        self.NG = NG
        self.HD = 64
        assert D == 128 * H
        self.NTT = T // 128
        self.NDC = D // 128
        self.QB = 512
        assert T % self.QB == 0
        self.NQB = T // self.QB
        self.GT = T // NG
        self.GTT = self.GT // 128
        assert self.GTT % 2 == 0
        self.NFC = F // 512
        self.NFC128 = F // 128
        self.NDOC = D // 512
        self.phases = 5
        self.a_tiles = self.NTT
        self.a_steps = 9
        self.init_only = False
        self.debug = False


CFG_FULL = Cfg(T=2048, D=2048, H=16, F=8192, NG=4)


class Emit:

    def __init__(self, nc, block, stack):
        self.nc = nc
        self.block = block
        self.stack = stack
        self.sems = {}
        self.cnt = {}
        self.waited = {}
        self.res = {}
        self.cur = None

    def sem(self, name):
        h = self.stack.enter_context(self.nc.semaphore(name))
        self.sems[name] = h
        self.cnt[name] = 0

    def sec(self, engname, body):
        def f(eng):
            prev = self.cur
            self.cur = engname
            body(eng)
            self.cur = prev
        getattr(self.block, engname)(f)

    def wait(self, eng, semname, val):
        if val <= 0:
            return
        if semname.startswith("d"):
            val = self.cnt[semname]
        key = (self.cur, semname)
        if self.waited.get(key, -1) >= val:
            return
        eng.wait_ge(self.sems[semname], val)
        self.waited[key] = val

    def inc(self, instr, semname, k=1):
        instr.then_inc(self.sems[semname], k)
        self.cnt[semname] += k
        key = (self.cur, semname)
        return self.cnt[semname]

    def _get(self, rname):
        return self.res.setdefault(rname, {"w": None, "r": []})

    def rd(self, eng, rnames):
        for rn in rnames:
            r = self._get(rn)
            if r["w"] is not None:
                self.wait(eng, r["w"][0], r["w"][1])

    def wr(self, eng, rnames):
        for rn in rnames:
            r = self._get(rn)
            if r["w"] is not None:
                self.wait(eng, r["w"][0], r["w"][1])
            for (s, v) in r["r"]:
                self.wait(eng, s, v)

    def did_rd(self, rnames, semname):
        v = self.cnt[semname]
        for rn in rnames:
            self._get(rn)["r"].append((semname, v))

    def did_wr(self, rnames, semname):
        v = self.cnt[semname]
        for rn in rnames:
            r = self._get(rn)
            r["w"] = (semname, v)
            r["r"] = []

    def fence(self, engnames=None):
        sems = list(self.sems)
        for en in (engnames or ["tensor", "vector", "scalar", "gpsimd",
                                "sync"]):
            def f(eng, en=en):
                for s in sems:
                    self.wait(eng, s, self.cnt[s])
            self.sec(en, f)

    def tfence(self):
        def f(eng):
            for s in ("sve", "ssc", "sgp"):
                self.wait(eng, s, self.cnt[s])
        self.sec("tensor", f)

    def grp(self, engname, semname, reads, writes, body, dmak=None):
        def f(eng):
            self.rd(eng, reads)
            self.wr(eng, writes)
            last = body(eng)
            if last is not None:
                self.inc(last, semname, 16 if dmak else 1)
            self.did_rd(reads, semname)
            self.did_wr(writes, semname)
        self.sec(engname, f)


def build(cfg):
    T, D, H, F = cfg.T, cfg.D, cfg.H, cfg.F
    NTT, NDC, NQB, QB = cfg.NTT, cfg.NDC, cfg.NQB, cfg.QB
    NG, GT, GTT = cfg.NG, cfg.GT, cfg.GTT
    NFC, NFC128, NDOC = cfg.NFC, cfg.NFC128, cfg.NDOC
    JB = QB // 128

    nc = bass.Bass("TRN2", target_bir_lowering=False)

    x_d = nc.dram_tensor("x", [T, D], F32, kind="ExternalInput").ap()
    wq_d = nc.dram_tensor("Wq", [D, D], F32, kind="ExternalInput").ap()
    wk_d = nc.dram_tensor("Wk", [D, D], F32, kind="ExternalInput").ap()
    wv_d = nc.dram_tensor("Wv", [D, D], F32, kind="ExternalInput").ap()
    wo_d = nc.dram_tensor("Wo", [D, D], F32, kind="ExternalInput").ap()
    lq1_d = nc.dram_tensor("lq1", [64], F32, kind="ExternalInput").ap()
    lk1_d = nc.dram_tensor("lk1", [64], F32, kind="ExternalInput").ap()
    lq2_d = nc.dram_tensor("lq2", [64], F32, kind="ExternalInput").ap()
    lk2_d = nc.dram_tensor("lk2", [64], F32, kind="ExternalInput").ap()
    w1_d = nc.dram_tensor("W1", [F, D], F32, kind="ExternalInput").ap()
    w2_d = nc.dram_tensor("W2", [D, F], F32, kind="ExternalInput").ap()
    y_d = nc.dram_tensor("y", [T, D], F32, kind="ExternalOutput").ap()

    scrkind = "ExternalOutput" if cfg.debug else "Internal"
    w1q_d = nc.dram_tensor("w1q_scr", [F, D], BF16, kind=scrkind).ap()
    w2q_d = nc.dram_tensor("w2q_scr", [D, F], BF16, kind=scrkind).ap()
    wo_scr_d = nc.dram_tensor("wo_scr", [T, D], BF16, kind=scrkind).ap()
    if cfg.debug:
        xnT_dbg = nc.dram_tensor("xnT_dbg", [128, cfg.NDC * T], BF16,
                                 kind="ExternalOutput").ap()
        attnT_dbg = nc.dram_tensor("attnT_dbg", [128, H * T], BF16,
                                   kind="ExternalOutput").ap()
        cells_dbg = nc.dram_tensor("cells_dbg", [1, 16], F32,
                                   kind="ExternalOutput").ap()
        lni_dbg = nc.dram_tensor("lni_dbg", [128, 2 * cfg.NTT], F32,
                                 kind="ExternalOutput").ap()

    if getattr(cfg, "dbgqx", None) is not None:
        dxq_i32 = nc.dram_tensor("dxq_i32", [128, 2048], I32,
                                 kind="ExternalOutput").ap()
        dxq_rp = nc.dram_tensor("dxq_rp", [128, 2048], F32,
                                kind="ExternalOutput").ap()
        dxq_out = nc.dram_tensor("dxq_out", [128, 2048], BF16,
                                 kind="ExternalOutput").ap()
        dxq_in = nc.dram_tensor("dxq_in", [128, 2048], F32,
                                kind="ExternalOutput").ap()
        dxq_s = nc.dram_tensor("dxq_s", [128, 4], F32,
                               kind="ExternalOutput").ap()
    if getattr(cfg, "dbgq", None) is not None:
        dhq_i32 = nc.dram_tensor("dhq_i32", [128, 1024], I32,
                                 kind="ExternalOutput").ap()
        dhq_rp = nc.dram_tensor("dhq_rp", [128, 1024], F32,
                                kind="ExternalOutput").ap()
        dhq_out = nc.dram_tensor("dhq_out", [128, 1024], BF16,
                                 kind="ExternalOutput").ap()
        dhq_in = nc.dram_tensor("dhq_in", [128, 1024], BF16,
                                kind="ExternalOutput").ap()
        dhq_s = nc.dram_tensor("dhq_s", [128, 4], F32,
                               kind="ExternalOutput").ap()

    gctx = ExitStack()

    with nc.Block() as block:
        em = Emit(nc, block, gctx)
        for s in ["dg", "dstore", "dout", "ddbg",
                  "dwq0", "dwq1", "dxa0", "dxa1", "dw1t0", "dw1t1",
                  "dw2t0", "dw2t1", "dres0", "dres1", "dwoB0", "dwoB1",
                  "dwoC0", "dwoC1",
                  "ste", "sve", "ssc", "sgp"]:
            em.sem(s)

        def sb(name, shape, dt, stk=None):
            return (stk or gctx).enter_context(nc.sbuf_tensor(name, shape, dt))

        ones_col = sb("ones_col", [128, 1], BF16)
        ones_colf = sb("ones_colf", [128, 1], F32)
        ones_rowb = sb("ones_rowb", [1, 128], BF16)
        ones_rowf = sb("ones_rowf", [1, 128], F32)
        eye = sb("eye", [128, 128], BF16)
        tri = sb("tri", [128, 128], BF16)
        lamq = sb("lamq", [1, 256], F32)
        cells = sb("cells", [1, 16], F32)
        m1b = sb("m1b", [128, 1], F32)
        m2b = sb("m2b", [128, 1], F32)
        lni = sb("lni", [128, NTT], F32)
        lnb2 = sb("lnb2", [128, NTT], F32)
        scol = sb("scol", [128, 8], F32)
        epscol = sb("epscol", [128, 1], F32)
        scr128 = sb("scr128", [128, 128], BF16)

        lam11 = cells[0:1, 0:1]

        def init_v(v):
            em.inc(v.memset(cells[:], 0.0), "sve")
            em.wait(v, "sve", em.cnt["sve"])
            em.inc(v.memset(ones_col[:], 1.0), "sve")
            em.inc(v.memset(ones_colf[:], 1.0), "sve")
            em.inc(v.memset(ones_rowb[:], 1.0), "sve")
            em.inc(v.memset(ones_rowf[:], 1.0), "sve")
            em.inc(v.memset(epscol[:], 1e-5), "sve")
            em.inc(v.memset(cells[0:1, 15:16], LNC_HI), "sve")
            em.inc(v.memset(cells[0:1, 14:15], LNC_LO), "sve")
            em.inc(v.memset(scr128[:], 1.0), "sve")
            em.wait(v, "sve", em.cnt["sve"])
            em.inc(v.tensor_add(cells[0:1, 15:16], cells[0:1, 15:16],
                                cells[0:1, 14:15]), "sve")
            em.did_wr(["ones", "scr128", "lnccell"], "sve")
        em.sec("vector", init_v)

        def init_eye(g):
            return g.affine_select(out=eye[:], in_=scr128[:],
                                   pattern=[[1, 128]],
                                   compare_op=ALU.is_equal, fill=0.0, base=0,
                                   channel_multiplier=-1)
        em.grp("gpsimd", "sgp", ["scr128"], ["eye"], init_eye)

        def init_tri(g):
            return g.affine_select(out=tri[:], in_=scr128[:],
                                   pattern=[[1, 128]],
                                   compare_op=ALU.is_ge, fill=0.0, base=0,
                                   channel_multiplier=-1)
        em.grp("gpsimd", "sgp", ["scr128"], ["tri"], init_tri)

        def init_lamdma(g):
            for i, dsrc in enumerate([lq1_d, lk1_d, lq2_d, lk2_d]):
                ins = g.dma_start(out=lamq[0:1, i * 64:(i + 1) * 64],
                                  in_=dsrc[:])
                ins.then_inc(em.sems["dg"], 16)
                em.cnt["dg"] += 16
            em.did_wr(["lamq"], "dg")
        em.sec("gpsimd", init_lamdma)

        def init_lam_v(v):
            em.rd(v, ["lamq"])
            em.wr(v, ["lamq", "lamcells"])
            em.inc(v.tensor_mul(lamq[0:1, 0:64], lamq[0:1, 0:64],
                                lamq[0:1, 64:128]), "sve")
            em.wait(v, "sve", em.cnt["sve"])
            em.inc(v.reduce_sum(out=cells[0:1, 5:6], in_=lamq[0:1, 0:64],
                                axis=AX.X), "sve")
            em.inc(v.tensor_mul(lamq[0:1, 128:192], lamq[0:1, 128:192],
                                lamq[0:1, 192:256]), "sve")
            em.wait(v, "sve", em.cnt["sve"])
            em.inc(v.reduce_sum(out=cells[0:1, 6:7], in_=lamq[0:1, 128:192],
                                axis=AX.X), "sve")
            em.did_wr(["lamcells"], "sve")
        em.sec("vector", init_lam_v)

        def init_lam_s(s):
            em.rd(s, ["lamcells"])
            em.wr(s, ["lamexp"])
            em.inc(s.activation(out=cells[0:1, 1:2], in_=cells[0:1, 5:6],
                                func=AF.Exp), "ssc")
            em.inc(s.activation(out=cells[0:1, 2:3], in_=cells[0:1, 6:7],
                                func=AF.Exp), "ssc")
            em.did_wr(["lamexp"], "ssc")
        em.sec("scalar", init_lam_s)

        def init_lam_v2(v):
            em.rd(v, ["lamexp"])
            em.wr(v, ["lam"])
            em.inc(v.tensor_sub(cells[0:1, 0:1], cells[0:1, 1:2],
                                cells[0:1, 2:3]), "sve")
            em.wait(v, "sve", em.cnt["sve"])
            em.inc(v.tensor_scalar_add(cells[0:1, 0:1], cells[0:1, 0:1],
                                       LAMBDA_INIT), "sve")
            em.did_wr(["lam"], "sve")
        em.sec("vector", init_lam_v2)

        stkAB = ExitStack()
        stkAW = ExitStack()
        xnT = sb("xnT", [128, NDC * T], BF16, stkAB)
        attnT = sb("attnT", [128, H * T], BF16, stkAB)
        xf32 = sb("xf32", [128, 2 * D], F32, stkAW)
        xnb = sb("xnb", [128, 2 * D], BF16, stkAW)
        junk = sb("junk", [128, D], F32, stkAW)

        em.tfence()
        with nc.psum_tensor("psTA", [128, 2048], BF16) as psTA:
            for tt in (range(cfg.a_tiles) if cfg.phases >= 1 else []):
                par = tt % 2
                dsem = "dxa0" if par == 0 else "dxa1"
                xr = "xf.%d" % par
                xnr = "xnb.%d" % par
                xs = xf32[:, par * D:(par + 1) * D]
                xns = xnb[:, par * D:(par + 1) * D]

                if cfg.a_steps >= 1:
                    em.grp("sync", dsem, [], [xr],
                           lambda sy, tt=tt, xs=xs: sy.dma_start(
                               out=xs, in_=x_d[tt * 128:(tt + 1) * 128, :]),
                           dmak=True)
                if cfg.a_steps >= 2:
                    em.grp("vector", "sve", [xr], ["scol.sum"],
                           lambda v, xs=xs: v.reduce_sum(
                               out=scol[:, 0:1], in_=xs, axis=AX.X))
                if cfg.a_steps >= 3:
                    em.grp("scalar", "ssc", [xr], ["junk"],
                           lambda s, xs=xs: s.activation(
                               out=junk[:], in_=xs, func=AF.Square))
                if cfg.a_steps >= 4:
                    em.grp("vector", "sve", ["junk"], ["scol.sq"],
                           lambda v: v.reduce_sum(out=scol[:, 1:2],
                                                  in_=junk[:], axis=AX.X))

                if cfg.a_steps < 5:
                    continue

                def a_ln_v(v):
                    em.rd(v, ["scol.sum", "scol.sq"])
                    em.wr(v, ["scol.mv"])
                    em.inc(v.tensor_scalar_mul(scol[:, 2:3], scol[:, 0:1],
                                               1.0 / D), "sve")
                    em.wait(v, "sve", em.cnt["sve"])
                    em.inc(v.tensor_mul(scol[:, 3:4], scol[:, 2:3],
                                        scol[:, 2:3]), "sve")
                    em.wait(v, "sve", em.cnt["sve"])
                    em.inc(v.scalar_tensor_tensor(
                        out=scol[:, 4:5], in0=scol[:, 1:2], scalar=1.0 / D,
                        in1=scol[:, 3:4], op0=ALU.mult, op1=ALU.subtract),
                        "sve")
                    em.did_wr(["scol.mv"], "sve")
                em.sec("vector", a_ln_v)

                if cfg.a_steps < 6:
                    continue

                def a_ln_s(s, tt=tt):
                    em.rd(s, ["scol.mv"])
                    em.wr(s, ["lni", "scol.lnv"])
                    em.inc(s.activation(out=scol[:, 5:6], in_=scol[:, 4:5],
                                        func=AF.Ln, bias=epscol[:]), "ssc")
                    em.wait(s, "ssc", em.cnt["ssc"])
                    em.inc(s.activation(out=lni[:, tt:tt + 1],
                                        in_=scol[:, 5:6], func=AF.Exp,
                                        scale=-0.5), "ssc")
                    em.did_rd(["scol.mv"], "ssc")
                    em.did_wr(["lni", "scol.lnv"], "ssc")
                em.sec("scalar", a_ln_s)

                if cfg.a_steps < 7:
                    continue
                em.grp("vector", "sve", ["lni", "scol.mv"], ["lnb2"],
                       lambda v, tt=tt: v.scalar_tensor_tensor(
                           out=lnb2[:, tt:tt + 1], in0=scol[:, 2:3],
                           scalar=-1.0, in1=lni[:, tt:tt + 1],
                           op0=ALU.mult, op1=ALU.mult))

                if cfg.a_steps < 8:
                    continue
                em.grp("vector", "sve", [xr, "lni", "lnb2"], [xnr],
                       lambda v, tt=tt, xs=xs, xns=xns: v.tensor_scalar(
                           out=xns, in0=xs, scalar1=lni[:, tt:tt + 1],
                           scalar2=lnb2[:, tt:tt + 1], op0=ALU.mult,
                           op1=ALU.add))

                if cfg.a_steps < 9:
                    continue
                for cb in range(NDC // 4):
                    pr = "psTA.%d" % (cb % 2)
                    po = (cb % 2) * 1024

                    def a_tr(t, cb=cb, po=po, xns=xns):
                        for ci in range(4):
                            c = cb * 4 + ci
                            ins = t.transpose(
                                psTA[:, po + ci * 128:po + (ci + 1) * 128],
                                xns[:, c * 128:(c + 1) * 128], eye[:])
                        return ins
                    em.grp("tensor", "ste", [xnr, "eye"], [pr], a_tr)

                    def a_trev(v, tt=tt, cb=cb, po=po):
                        for ci in range(4):
                            c = cb * 4 + ci
                            ins = v.tensor_copy(
                                out=xnT[:, c * T + tt * 128:
                                        c * T + (tt + 1) * 128],
                                in_=psTA[:, po + ci * 128:po + (ci + 1) * 128])
                        return ins
                    em.grp("vector", "sve", [pr], ["xnT"], a_trev)

        i32buf = sb("i32buf", [128, D], I32, stkAW)
        wr32 = sb("wr32", [128, D], F32, stkAW)
        ac4 = sb("ac4", [128, D // 512], F32, stkAW)
        h4 = sb("h4", [128, D // 512], I32, stkAW)
        l4 = sb("l4", [128, D // 512], F32, stkAW)
        hlr = sb("hlr", [128, 4], F32, stkAW)
        wrows = sb("wrows", [1, 512], F32, stkAW)
        hp2 = sb("hp2", [1, 128], I32, stkAW)
        hlo = sb("hlo", [1, 128], F32, stkAW)

        def wprep(w_src, wq_dst, R, C, mcell, mcname, mbvec, tag):
            NP_C = C // D
            npieces = (R // 128) * NP_C
            assert npieces <= 128
            SC = float(2.0 ** 17)
            ISC = float(2.0 ** -17)
            NINV = 1.0 / (R * C)
            assert (R * C) & (R * C - 1) == 0

            with nc.psum_tensor("psW_" + tag, [128, 8], F32) as psW:
                em.tfence()
                for pi in range(npieces):
                    rr, cc = divmod(pi, NP_C)
                    par = pi % 2
                    dsem = "dxa0" if par == 0 else "dxa1"
                    xr = "xf.%d" % par
                    xs = xf32[:, par * D:(par + 1) * D]

                    em.grp("sync", dsem, [], [xr],
                           lambda sy, rr=rr, cc=cc, xs=xs: sy.dma_start(
                               out=xs, in_=w_src[rr * 128:(rr + 1) * 128,
                                                 cc * D:(cc + 1) * D]),
                           dmak=True)

                    em.grp("scalar", "ssc", [xr], ["junk"],
                           lambda s, xs=xs: s.activation(
                               out=junk[:], in_=xs, func=AF.Abs))

                    def wp_sum(v, pi=pi):
                        em.rd(v, ["junk"])
                        em.wr(v, ["i32buf", "wr32", "ac4", "h4", "l4",
                                  "hlr"])
                        em.inc(v.tensor_scalar(out=i32buf[:], in0=junk[:],
                                               scalar1=SC, scalar2=None,
                                               op0=ALU.mult), "sve")
                        em.wait(v, "sve", em.cnt["sve"])
                        em.inc(v.scalar_tensor_tensor(
                            out=wr32[:], in0=i32buf[:], scalar=ISC,
                            in1=junk[:], op0=ALU.mult, op1=ALU.subtract),
                            "sve")
                        for ch in range(D // 512):
                            em.inc(v.reduce_sum(
                                out=ac4[:, ch:ch + 1],
                                in_=i32buf[:, ch * 512:(ch + 1) * 512],
                                axis=AX.X), "sve")
                        em.wait(v, "sve", em.cnt["sve"])
                        em.inc(v.reduce_sum(out=hlr[:, 2:3], in_=wr32[:],
                                            axis=AX.X), "sve")
                        em.inc(v.tensor_scalar(out=h4[:], in0=ac4[:],
                                               scalar1=float(2.0 ** -10),
                                               scalar2=None, op0=ALU.mult),
                               "sve")
                        em.wait(v, "sve", em.cnt["sve"])
                        em.inc(v.scalar_tensor_tensor(
                            out=l4[:], in0=h4[:], scalar=-1024.0,
                            in1=ac4[:], op0=ALU.mult, op1=ALU.add), "sve")
                        em.wait(v, "sve", em.cnt["sve"])
                        em.inc(v.reduce_sum(out=hlr[:, 0:1], in_=h4[:],
                                            axis=AX.X), "sve")
                        em.inc(v.reduce_sum(out=hlr[:, 1:2], in_=l4[:],
                                            axis=AX.X), "sve")
                        em.did_rd(["junk"], "sve")
                        em.did_wr(["i32buf", "wr32", "ac4", "h4", "l4",
                                   "hlr"], "sve")
                    em.sec("vector", wp_sum)

                    def wp_mm(t, pi=pi):
                        for k in range(3):
                            ins = t.matmul(psW[0:1, k:k + 1],
                                           hlr[:, k:k + 1], ones_colf[:],
                                           start=True, stop=True)
                        return ins
                    em.grp("tensor", "ste", ["hlr", "ones"], ["psW"], wp_mm)

                    def wp_ev(v, pi=pi):
                        em.rd(v, ["psW"])
                        em.wr(v, ["wrows"])
                        for k in range(3):
                            ins = v.tensor_copy(
                                out=wrows[0:1, k * 128 + pi:
                                          k * 128 + pi + 1],
                                in_=psW[0:1, k:k + 1])
                        em.inc(ins, "sve")
                        em.did_rd(["psW"], "sve")
                        em.did_wr(["wrows"], "sve")
                    em.sec("vector", wp_ev)

                def wp_fin(v, npieces=npieces, mcell=mcell, tag=tag,
                           mcname=mcname):
                    em.rd(v, ["wrows"])
                    em.wr(v, ["hp2", "hlo", "meancells", "lamcells",
                              "lamexp", mcname, "cells.s" + tag])
                    np_ = npieces
                    em.inc(v.tensor_scalar(out=hp2[0:1, 0:np_],
                                           in0=wrows[0:1, 0:np_],
                                           scalar1=float(2.0 ** -10),
                                           scalar2=None, op0=ALU.mult),
                           "sve")
                    em.wait(v, "sve", em.cnt["sve"])
                    em.inc(v.scalar_tensor_tensor(
                        out=hlo[0:1, 0:np_], in0=hp2[0:1, 0:np_],
                        scalar=-1024.0, in1=wrows[0:1, 0:np_],
                        op0=ALU.mult, op1=ALU.add), "sve")
                    em.wait(v, "sve", em.cnt["sve"])
                    em.inc(v.reduce_sum(out=cells[0:1, 5:6],
                                        in_=hp2[0:1, 0:np_], axis=AX.X),
                           "sve")
                    em.inc(v.reduce_sum(out=cells[0:1, 6:7],
                                        in_=hlo[0:1, 0:np_], axis=AX.X),
                           "sve")
                    em.inc(v.reduce_sum(out=cells[0:1, 8:9],
                                        in_=wrows[0:1, 128:128 + np_],
                                        axis=AX.X), "sve")
                    em.inc(v.reduce_sum(out=cells[0:1, 9:10],
                                        in_=wrows[0:1, 256:256 + np_],
                                        axis=AX.X), "sve")
                    em.wait(v, "sve", em.cnt["sve"])
                    em.inc(v.tensor_scalar(out=cells[0:1, 10:11],
                                           in0=cells[0:1, 5:6],
                                           scalar1=(2.0 ** 20) * ISC * NINV,
                                           scalar2=None, op0=ALU.mult),
                           "sve")
                    em.inc(v.tensor_scalar(out=cells[0:1, 11:12],
                                           in0=cells[0:1, 6:7],
                                           scalar1=(2.0 ** 10) * ISC * NINV,
                                           scalar2=None, op0=ALU.mult),
                           "sve")
                    em.inc(v.tensor_scalar(out=cells[0:1, 12:13],
                                           in0=cells[0:1, 8:9],
                                           scalar1=ISC * NINV, scalar2=None,
                                           op0=ALU.mult), "sve")
                    em.inc(v.tensor_scalar(out=cells[0:1, 13:14],
                                           in0=cells[0:1, 9:10],
                                           scalar1=-NINV, scalar2=None,
                                           op0=ALU.mult), "sve")
                    em.wait(v, "sve", em.cnt["sve"])
                    em.inc(v.tensor_add(cells[0:1, 12:13],
                                        cells[0:1, 12:13],
                                        cells[0:1, 13:14]), "sve")
                    em.wait(v, "sve", em.cnt["sve"])
                    em.inc(v.tensor_add(cells[0:1, 11:12],
                                        cells[0:1, 11:12],
                                        cells[0:1, 12:13]), "sve")
                    em.wait(v, "sve", em.cnt["sve"])
                    em.inc(v.tensor_add(cells[0:1, 10:11],
                                        cells[0:1, 10:11],
                                        cells[0:1, 11:12]), "sve")
                    em.wait(v, "sve", em.cnt["sve"])
                    em.inc(v.tensor_scalar(out=mcell, in0=cells[0:1, 10:11],
                                           scalar1=1e-5, scalar2=None,
                                           op0=ALU.max), "sve")
                    em.wait(v, "sve", em.cnt["sve"])
                    em.inc(v.reciprocal(out=cells[0:1, 7:8], in_=mcell),
                           "sve")
                    em.wait(v, "sve", em.cnt["sve"])
                    em.inc(v.tensor_mul(cells[0:1, 13:14], mcell,
                                        cells[0:1, 7:8]), "sve")
                    em.wait(v, "sve", em.cnt["sve"])
                    em.inc(v.tensor_scalar_add(cells[0:1, 13:14],
                                               cells[0:1, 13:14], -1.0),
                           "sve")
                    em.wait(v, "sve", em.cnt["sve"])
                    em.inc(v.tensor_mul(cells[0:1, 13:14],
                                        cells[0:1, 13:14],
                                        cells[0:1, 7:8]), "sve")
                    em.wait(v, "sve", em.cnt["sve"])
                    em.inc(v.tensor_sub(cells[0:1, 7:8], cells[0:1, 7:8],
                                        cells[0:1, 13:14]), "sve")
                    em.did_rd(["wrows"], "sve")
                    em.did_wr(["hp2", "hlo", "meancells", "lamcells",
                               "lamexp", mcname, "cells.s" + tag], "sve")
                em.sec("vector", wp_fin)

                def w_bc_t(t, mcell=mcell):
                    ins = t.matmul(psW[:, 4:5], ones_rowf[0:1, :],
                                   cells[0:1, 7:8], start=True, stop=True)
                    ins = t.matmul(psW[:, 5:6], ones_rowf[0:1, :], mcell,
                                   start=True, stop=True)
                    return ins
                em.grp("tensor", "ste", [mcname, "cells.s" + tag, "ones"],
                       ["psW2"], w_bc_t)

                def w_bc_v(v, mbvec=mbvec):
                    em.rd(v, ["psW2"])
                    em.wr(v, ["srecip." + tag, "mb." + tag])
                    em.inc(v.tensor_copy(out=scol[:, 6:7], in_=psW[:, 4:5]),
                           "sve")
                    em.inc(v.tensor_copy(out=mbvec, in_=psW[:, 5:6]), "sve")
                    em.did_rd(["psW2"], "sve")
                    em.did_wr(["srecip." + tag, "mb." + tag], "sve")
                em.sec("vector", w_bc_v)

                for pi in range(npieces):
                    rr, cc = divmod(pi, NP_C)
                    par = pi % 2
                    dsem = "dxa0" if par == 0 else "dxa1"
                    xr = "xf.%d" % par
                    xnr = "xnb.%d" % par
                    xs = xf32[:, par * D:(par + 1) * D]
                    xns = xnb[:, par * D:(par + 1) * D]

                    em.grp("sync", dsem, [], [xr],
                           lambda sy, rr=rr, cc=cc, xs=xs: sy.dma_start(
                               out=xs, in_=w_src[rr * 128:(rr + 1) * 128,
                                                 cc * D:(cc + 1) * D]),
                           dmak=True)

                    def wq_q(v, xs=xs, xr=xr, xnr=xnr, tag=tag):
                        em.rd(v, [xr, "srecip." + tag])
                        em.wr(v, ["wr32", "junk", xnr])
                        em.inc(v.tensor_scalar(out=wr32[:], in0=xs,
                                               scalar1=scol[:, 6:7],
                                               scalar2=None, op0=ALU.mult),
                               "sve")
                        em.wait(v, "sve", em.cnt["sve"])
                        em.inc(v.tensor_scalar(out=junk[:], in0=wr32[:],
                                               scalar1=-0.5, scalar2=None,
                                               op0=ALU.is_le), "sve")
                        em.wait(v, "sve", em.cnt["sve"])
                        em.inc(v.scalar_tensor_tensor(
                            out=xns, in0=wr32[:], scalar=0.5, in1=junk[:],
                            op0=ALU.is_ge, op1=ALU.subtract), "sve")
                        em.did_rd([xr, "srecip." + tag], "sve")
                        em.did_wr(["wr32", "junk", xnr], "sve")
                    em.sec("vector", wq_q)

                    em.grp("sync", "dstore", [xnr], [],
                           lambda sy, rr=rr, cc=cc, xns=xns: sy.dma_start(
                               out=wq_dst[rr * 128:(rr + 1) * 128,
                                          cc * D:(cc + 1) * D], in_=xns),
                           dmak=True)

        if cfg.phases >= 2:
            wprep(w1_d, w1q_d, F, D, cells[0:1, 3:4], "m1c", m1b[:, 0:1],
                  "w1")
            wprep(w2_d, w2q_d, D, F, cells[0:1, 4:5], "m2c", m2b[:, 0:1],
                  "w2")

        stkAW.close()
        em.fence()
        stkB = ExitStack()
        wqb = sb("wqb", [128, 2 * D], BF16, stkB)
        wkb = sb("wkb", [128, 2 * D], BF16, stkB)
        wvb = sb("wvb", [128, 2 * D], BF16, stkB)
        Qc = sb("Qc", [128, T], BF16, stkB)
        Kc = sb("Kc", [128, T], BF16, stkB)
        Vt = sb("Vt", [128, T], BF16, stkB)
        Vk = sb("Vk", [128, T], BF16, stkB)
        sqQK = sb("sqQK", [128, T], BF16, stkB)
        P0T = sb("P0T", [128, 1024], BF16, stkB)
        P1T = sb("P1T", [128, 1024], BF16, stkB)
        bcol = sb("bcol", [128, 2], F32, stkB)
        maxc = sb("maxc", [1, 64], F32, stkB)
        s01sb = sb("s01sb", [1, 1024], F32, stkB)
        r1row = sb("r1row", [1, 512], F32, stkB)
        s0sq = sb("s0sq", [1, 512], F32, stkB)
        erow = sb("erow", [1, 512], F32, stkB)
        crow = sb("crow", [1, 512], BF16, stkB)
        frow = sb("frow", [1, 512], BF16, stkB)
        Csb = sb("Csb", [128, 512], BF16, stkB)
        t1b = sb("t1b", [128, 512], F32, stkB)
        apb = sb("apb", [128, 512], F32, stkB)
        sqb = sb("sqb", [128, 512], BF16, stkB)

        QKSC = float(8.0 ** -0.5)
        LNC = math.log((1.0 - LAMBDA_INIT) * math.sqrt(128.0))
        nbq = T // 512

        for h in (range(H) if cfg.phases >= 3 else []):
            par = h % 2
            dsem = "dwq0" if par == 0 else "dwq1"
            wres = "whd.%d" % par

            def b_wdma(g, h=h, par=par, dsem=dsem):
                for wdst, wsrc in ((wqb, wq_d), (wkb, wk_d), (wvb, wv_d)):
                    dst = wdst[:, par * D:(par + 1) * D].rearrange(
                        "p (c j) -> p c j", j=128)
                    src = wsrc[:, h * 128:(h + 1) * 128].rearrange(
                        "(c p) j -> p c j", p=128)
                    em.inc(g.dma_start(out=dst, in_=src), dsem, 16)
                return None
            em.grp("gpsimd", dsem, [], [wres], b_wdma, dmak=True)

            em.tfence()
            with ExitStack() as pstk:
                psQ = pstk.enter_context(
                    nc.psum_tensor("psQ_%d" % h, [128, T], F32))
                psK = pstk.enter_context(
                    nc.psum_tensor("psK_%d" % h, [128, T], F32))

                def b_projqk(t, par=par):
                    for wsb, ps in ((wqb, psQ), (wkb, psK)):
                        for nb in range(T // 512):
                            for c in range(NDC):
                                ins = t.matmul(
                                    ps[:, nb * 512:(nb + 1) * 512],
                                    wsb[:, par * D + c * 128:
                                        par * D + (c + 1) * 128],
                                    xnT[:, c * T + nb * 512:
                                        c * T + (nb + 1) * 512],
                                    start=(c == 0), stop=(c == NDC - 1))
                    return ins
                em.grp("tensor", "ste", [wres, "xnT"], ["psQ", "psK"],
                       b_projqk)

                def b_evqk(v):
                    ins = v.tensor_scalar_mul(Qc[:], psQ[:], QKSC)
                    ins = v.tensor_scalar_mul(Kc[:], psK[:], QKSC)
                    return ins
                em.grp("vector", "sve", ["psQ", "psK"], ["Qc", "Kc"], b_evqk)

                def b_projv(t, par=par):
                    for nb in range(T // 512):
                        for c in range(NDC):
                            ins = t.matmul(
                                psQ[:, nb * 512:(nb + 1) * 512],
                                wvb[:, par * D + c * 128:
                                    par * D + (c + 1) * 128],
                                xnT[:, c * T + nb * 512:c * T + (nb + 1) * 512],
                                start=(c == 0), stop=(c == NDC - 1))
                    return ins
                em.grp("tensor", "ste", [wres, "xnT"], ["psQ"], b_projv)

                em.grp("vector", "sve", ["psQ"], ["Vt"],
                       lambda v: v.tensor_copy(out=Vt[:], in_=psQ[:]))

            em.tfence()
            with ExitStack() as pstk:
                psV = pstk.enter_context(
                    nc.psum_tensor("psV_%d" % h, [128, 2048], BF16))
                psN = pstk.enter_context(
                    nc.psum_tensor("psN_%d" % h, [1, 1024], F32))
                psB = pstk.enter_context(
                    nc.psum_tensor("psB_%d" % h, [128, 2], F32))

                for cb in range(NTT // 4):
                    pr = "psV.%d" % (cb % 2)
                    po = (cb % 2) * 1024

                    def b_vtr(t, cb=cb, po=po):
                        for ci in range(4):
                            c = cb * 4 + ci
                            ins = t.transpose(
                                psV[:, po + ci * 128:po + (ci + 1) * 128],
                                Vt[:, c * 128:(c + 1) * 128], eye[:])
                        return ins
                    em.grp("tensor", "ste", ["Vt", "eye"], [pr], b_vtr)

                    em.grp("vector", "sve", [pr], ["Vk"],
                           lambda v, cb=cb, po=po: v.tensor_copy(
                               out=Vk[:, cb * 512:(cb + 1) * 512],
                               in_=psV[:, po:po + 512]))

                for src_name, base in (("q", 0), ("k", 2 * nbq)):
                    srcb = Qc if src_name == "q" else Kc
                    em.grp("vector", "sve", ["Qc" if src_name == "q"
                                             else "Kc"], ["sqQK"],
                           lambda v, srcb=srcb: v.tensor_mul(
                               sqQK[:], srcb[:], srcb[:]))
                    for hh in range(2):
                        hs = slice(hh * 64, hh * 64 + 64)
                        for nb in range(nbq):
                            pn = "psN.%d" % hh
                            em.grp("tensor", "ste", ["sqQK", "ones"], [pn],
                                   lambda t, hh=hh, nb=nb, hs=hs: t.matmul(
                                       psN[0:1, hh * 512:hh * 512 + 512],
                                       ones_col[hs, 0:1],
                                       sqQK[hs, nb * 512:(nb + 1) * 512],
                                       start=True, stop=True))
                            em.grp("vector", "sve", [pn], ["maxc"],
                                   lambda v, hh=hh, nb=nb, base=base:
                                   v.reduce_max(
                                       out=maxc[0:1, base + hh * nbq + nb:
                                                base + hh * nbq + nb + 1],
                                       in_=psN[0:1, hh * 512:hh * 512 + 512],
                                       axis=AX.X))

                def b_bnd_v(v):
                    em.rd(v, ["maxc"])
                    em.wr(v, ["bndcells"])
                    for hh in range(2):
                        em.inc(v.reduce_max(
                            out=cells[0:1, 8 + hh:9 + hh],
                            in_=maxc[0:1, hh * nbq:(hh + 1) * nbq],
                            axis=AX.X), "sve")
                        em.inc(v.reduce_max(
                            out=cells[0:1, 10 + hh:11 + hh],
                            in_=maxc[0:1, 2 * nbq + hh * nbq:
                                     2 * nbq + (hh + 1) * nbq],
                            axis=AX.X), "sve")
                    em.wait(v, "sve", em.cnt["sve"])
                    for hh in range(2):
                        ins = v.tensor_add(cells[0:1, 12 + hh:13 + hh],
                                           cells[0:1, 8 + hh:9 + hh],
                                           cells[0:1, 10 + hh:11 + hh])
                        em.inc(ins, "sve")
                    em.did_rd(["maxc"], "sve")
                    em.did_wr(["bndcells"], "sve")
                em.sec("vector", b_bnd_v)

                def b_bnd_t(t):
                    for hh in range(2):
                        ins = t.matmul(psB[:, hh:hh + 1], ones_rowf[0:1, :],
                                       cells[0:1, 12 + hh:13 + hh],
                                       start=True, stop=True)
                    return ins
                em.grp("tensor", "ste", ["bndcells", "ones"], ["psB"],
                       b_bnd_t)

                def b_bnd_s(v):
                    for hh in range(2):
                        ins = v.tensor_scalar_mul(bcol[:, hh:hh + 1],
                                                  psB[:, hh:hh + 1], -0.5)
                    return ins
                em.grp("vector", "sve", ["psB"], ["bcol"], b_bnd_s)

            em.tfence()
            with ExitStack() as pstk:
                psS0 = pstk.enter_context(
                    nc.psum_tensor("psS0_%d" % h, [128, 1024], F32))
                psS1 = pstk.enter_context(
                    nc.psum_tensor("psS1_%d" % h, [128, 1024], F32))
                psA0 = pstk.enter_context(
                    nc.psum_tensor("psA0_%d" % h, [128, 512], F32))
                psA1 = pstk.enter_context(
                    nc.psum_tensor("psA1_%d" % h, [128, 512], F32))
                psRW = pstk.enter_context(
                    nc.psum_tensor("psRW_%d" % h, [1, 1024], F32))

                for qb in range(NQB):
                    jmax = (qb + 1) * JB - 1
                    diag_js = [j for j in range(jmax + 1) if j >= qb * JB]
                    ste0 = em.cnt["ste"]
                    ssc0 = em.cnt["ssc"]
                    sve0 = em.cnt["sve"]

                    def jinfo(j, qb=qb):
                        jp = j % 2
                        dg = (j >= qb * JB)
                        off = 128 * (j - qb * JB) if dg else 0
                        return jp, dg, off

                    PSN = ["psS0.0", "psS0.1", "psS1.0", "psS1.1"]
                    PN = ["P0T.0", "P0T.1", "P1T.0", "P1T.1"]

                    def qb_tensor(t, qb=qb, jmax=jmax, ste0=ste0,
                                  ssc0=ssc0, sve0=sve0, jinfo=jinfo,
                                  diag_js=diag_js):
                        em.rd(t, ["Qc", "Kc", "Vk", "ones"] + PN)
                        em.wr(t, PSN + ["psA0", "psA1", "psRW"])

                        def emit_S(j):
                            jp, dg, off = jinfo(j)
                            q0 = qb * QB + off
                            Nn = QB - off
                            if j >= 2:
                                em.wait(t, "ssc", ssc0 + j - 1)
                            t.matmul(
                                psS0[:, jp * 512 + off:(jp + 1) * 512],
                                Kc[0:64, j * 128:(j + 1) * 128],
                                Qc[0:64, q0:q0 + Nn], start=True, stop=True)
                            ins = t.matmul(
                                psS1[:, jp * 512 + off:(jp + 1) * 512],
                                Kc[64:128, j * 128:(j + 1) * 128],
                                Qc[64:128, q0:q0 + Nn], start=True,
                                stop=True)
                            em.inc(ins, "ste")

                        def emit_AV(j):
                            jp, dg, off = jinfo(j)
                            em.wait(t, "ssc", ssc0 + j + 1)
                            if dg:
                                nm = sum(1 for x in diag_js if x <= j)
                                em.wait(t, "sve", sve0 + nm)
                            st = (j == 0)
                            sp = (j == jmax)
                            t.matmul(psA0[:, off:512],
                                     Vk[:, j * 128:(j + 1) * 128],
                                     P0T[:, jp * 512 + off:(jp + 1) * 512],
                                     start=st, stop=sp)
                            t.matmul(psA1[:, off:512],
                                     Vk[:, j * 128:(j + 1) * 128],
                                     P1T[:, jp * 512 + off:(jp + 1) * 512],
                                     start=st, stop=sp)
                            t.matmul(psRW[0:1, off:512], ones_col[:, 0:1],
                                     P0T[:, jp * 512 + off:(jp + 1) * 512],
                                     start=st, stop=sp)
                            ins = t.matmul(
                                psRW[0:1, 512 + off:1024], ones_col[:, 0:1],
                                P1T[:, jp * 512 + off:(jp + 1) * 512],
                                start=st, stop=sp)
                            em.inc(ins, "ste")

                        emit_S(0)
                        for j in range(1, jmax + 1):
                            emit_S(j)
                            emit_AV(j - 1)
                        emit_AV(jmax)
                        em.did_rd(["Qc", "Kc", "Vk", "ones"] + PN, "ste")
                        em.did_wr(PSN + ["psA0", "psA1", "psRW"], "ste")
                    em.sec("tensor", qb_tensor)

                    def qb_scalar(s, qb=qb, jmax=jmax, ste0=ste0,
                                  ssc0=ssc0, jinfo=jinfo):
                        em.rd(s, ["bcol"])
                        for j in range(jmax + 1):
                            em.wait(s, "ste",
                                    ste0 + (1 if j == 0 else 2 * j))
                            jp, dg, off = jinfo(j)
                            s.activation(
                                out=P0T[:, jp * 512 + off:(jp + 1) * 512],
                                in_=psS0[:, jp * 512 + off:(jp + 1) * 512],
                                func=AF.Exp, bias=bcol[:, 0:1])
                            ins = s.activation(
                                out=P1T[:, jp * 512 + off:(jp + 1) * 512],
                                in_=psS1[:, jp * 512 + off:(jp + 1) * 512],
                                func=AF.Exp, bias=bcol[:, 1:2])
                            em.inc(ins, "ssc")
                        em.did_rd(["bcol"] + PSN, "ssc")
                        em.did_wr(PN, "ssc")
                    em.sec("scalar", qb_scalar)

                    def qb_mask(v, qb=qb, ssc0=ssc0, jinfo=jinfo,
                                diag_js=diag_js):
                        em.rd(v, ["tri"])
                        for j in diag_js:
                            jp, dg, off = jinfo(j)
                            em.wait(v, "ssc", ssc0 + j + 1)
                            v.tensor_mul(
                                P0T[:, jp * 512 + off:jp * 512 + off + 128],
                                P0T[:, jp * 512 + off:jp * 512 + off + 128],
                                tri[:])
                            ins = v.tensor_mul(
                                P1T[:, jp * 512 + off:jp * 512 + off + 128],
                                P1T[:, jp * 512 + off:jp * 512 + off + 128],
                                tri[:])
                            em.inc(ins, "sve")
                        em.did_rd(["tri"], "sve")
                    em.sec("vector", qb_mask)

                    em.grp("vector", "sve", ["psRW"], ["s01sb"],
                           lambda v: v.tensor_copy(out=s01sb[:],
                                                   in_=psRW[0:1, :]))

                    def ep_v1(v):
                        em.rd(v, ["s01sb", "lam"])
                        em.wr(v, ["r1row", "s0sq", "crow"])
                        em.inc(v.reciprocal(out=r1row[:],
                                            in_=s01sb[0:1, 512:1024]), "sve")
                        em.wait(v, "sve", em.cnt["sve"])
                        em.inc(v.tensor_mul(s0sq[:], r1row[:],
                                            s01sb[0:1, 512:1024]), "sve")
                        em.wait(v, "sve", em.cnt["sve"])
                        em.inc(v.tensor_scalar(out=s0sq[:], in0=s0sq[:],
                                               scalar1=-1.0, scalar2=2.0,
                                               op0=ALU.mult, op1=ALU.add),
                               "sve")
                        em.wait(v, "sve", em.cnt["sve"])
                        em.inc(v.tensor_mul(r1row[:], r1row[:], s0sq[:]),
                               "sve")
                        em.wait(v, "sve", em.cnt["sve"])
                        em.inc(v.scalar_tensor_tensor(
                            out=crow[:], in0=r1row[:], scalar=lam11,
                            in1=s01sb[0:1, 0:512], op0=ALU.mult,
                            op1=ALU.mult), "sve")
                        em.did_rd(["s01sb", "lam"], "sve")
                        em.did_wr(["r1row", "s0sq", "crow"], "sve")
                    em.sec("vector", ep_v1)

                    em.grp("tensor", "ste", ["crow", "ones"], ["psS0.1"],
                           lambda t: t.matmul(psS0[:, 512:1024],
                                              ones_rowb[0:1, :], crow[:],
                                              start=True, stop=True))

                    em.grp("vector", "sve", ["psS0.1"], ["Csb"],
                           lambda v: v.tensor_copy(out=Csb[:],
                                                   in_=psS0[:, 512:1024]))

                    def ep_v2(v):
                        em.rd(v, ["Csb", "psA0", "psA1"])
                        em.wr(v, ["t1b", "apb", "sqb"])
                        em.inc(v.tensor_mul(t1b[:], psA1[:], Csb[:]), "sve")
                        em.wait(v, "sve", em.cnt["sve"])
                        em.inc(v.tensor_sub(apb[:], psA0[:], t1b[:]), "sve")
                        em.wait(v, "sve", em.cnt["sve"])
                        em.inc(v.tensor_mul(sqb[:], apb[:], apb[:]), "sve")
                        em.did_rd(["Csb", "psA0", "psA1"], "sve")
                        em.did_wr(["t1b", "apb", "sqb"], "sve")
                    em.sec("vector", ep_v2)

                    em.grp("tensor", "ste", ["sqb", "ones"], ["psRW"],
                           lambda t: t.matmul(psRW[0:1, 0:512],
                                              ones_col[:, 0:1], sqb[:],
                                              start=True, stop=True))

                    def ep_v3(v):
                        em.rd(v, ["psRW", "s01sb"])
                        em.wr(v, ["s0sq", "erow"])
                        em.inc(v.tensor_mul(s0sq[:], s01sb[0:1, 0:512],
                                            s01sb[0:1, 0:512]), "sve")
                        em.wait(v, "sve", em.cnt["sve"])
                        em.inc(v.scalar_tensor_tensor(
                            out=erow[:], in0=s0sq[:], scalar=128.0 * 1e-5,
                            in1=psRW[0:1, 0:512], op0=ALU.mult, op1=ALU.add),
                            "sve")
                        em.did_rd(["psRW", "s01sb"], "sve")
                        em.did_wr(["s0sq", "erow"], "sve")
                    em.sec("vector", ep_v3)

                    def ep_s3(s):
                        em.rd(s, ["erow"])
                        em.wr(s, ["erow", "frow"])
                        em.inc(s.activation(out=erow[:], in_=erow[:],
                                            func=AF.Ln), "ssc")
                        em.wait(s, "ssc", em.cnt["ssc"])
                        em.inc(s.activation(out=frow[:], in_=erow[:],
                                            func=AF.Exp, scale=-0.5,
                                            bias=cells[0:1, 15:16]), "ssc")
                        em.did_wr(["erow", "frow"], "ssc")
                    em.sec("scalar", ep_s3)

                    em.grp("tensor", "ste", ["frow", "ones"], ["psS1.1"],
                           lambda t: t.matmul(psS1[:, 512:1024],
                                              ones_rowb[0:1, :], frow[:],
                                              start=True, stop=True))

                    em.grp("vector", "sve", ["psS1.1", "apb"], ["attnT"],
                           lambda v, h=h, qb=qb: v.tensor_mul(
                               attnT[:, h * T + qb * QB:h * T + (qb + 1) * QB],
                               apb[:], psS1[:, 512:1024]))

        stkB.close()
        em.fence()
        stkB2 = ExitStack()
        wob = sb("wob", [128, 2 * NDC * 512], BF16, stkB2)
        wotile = sb("wotile", [128, 2 * 512], BF16, stkB2)

        em.tfence()
        with nc.psum_tensor("psO", [128, 1024], F32) as psO:
            for dc in (range(NDOC) if cfg.phases >= 4 else []):
                par = dc % 2
                dsem = "dwoB0" if par == 0 else "dwoB1"
                wres = "wob.%d" % par

                def b2_dma(g, dc=dc, par=par):
                    dst = wob[:, par * NDC * 512:
                              (par + 1) * NDC * 512].rearrange(
                        "p (c j) -> p c j", j=512)
                    src = wo_d[:, dc * 512:(dc + 1) * 512].rearrange(
                        "(c p) j -> p c j", p=128)
                    return g.dma_start(out=dst, in_=src)
                em.grp("gpsimd", dsem, [], [wres], b2_dma, dmak=True)

                for tt in range(NTT):
                    pp = tt % 2
                    pr = "psO.%d" % pp
                    wtr = "wotile.%d" % pp

                    def b2_mm(t, tt=tt, pp=pp, par=par):
                        for c in range(H):
                            ins = t.matmul(
                                psO[:, pp * 512:(pp + 1) * 512],
                                attnT[:, c * T + tt * 128:
                                      c * T + (tt + 1) * 128],
                                wob[:, par * NDC * 512 + c * 512:
                                    par * NDC * 512 + (c + 1) * 512],
                                start=(c == 0), stop=(c == H - 1))
                        return ins
                    em.grp("tensor", "ste", [wres, "attnT"], [pr], b2_mm)

                    em.grp("vector", "sve", [pr], [wtr],
                           lambda v, pp=pp: v.tensor_copy(
                               out=wotile[:, pp * 512:(pp + 1) * 512],
                               in_=psO[:, pp * 512:(pp + 1) * 512]))

                    em.grp("sync", "dstore", [wtr], [],
                           lambda sy, dc=dc, tt=tt, pp=pp: sy.dma_start(
                               out=wo_scr_d[tt * 128:(tt + 1) * 128,
                                            dc * 512:(dc + 1) * 512],
                               in_=wotile[:, pp * 512:(pp + 1) * 512]),
                           dmak=True)

        if cfg.debug:
            def dbg_dump(sy):
                em.rd(sy, ["xnT", "attnT", "lam", "lni", "lnb2"])
                em.inc(sy.dma_start(out=xnT_dbg[:], in_=xnT[:]), "dstore", 16)
                em.inc(sy.dma_start(out=attnT_dbg[:], in_=attnT[:]),
                       "dstore", 16)
                em.inc(sy.dma_start(out=cells_dbg[:], in_=cells[:]),
                       "dstore", 16)
                em.inc(sy.dma_start(out=lni_dbg[:, 0:cfg.NTT], in_=lni[:]),
                       "dstore", 16)
                em.inc(sy.dma_start(out=lni_dbg[:, cfg.NTT:], in_=lnb2[:]),
                       "dstore", 16)
                em.wait(sy, "dstore", em.cnt["dstore"])
            em.sec("sync", dbg_dump)
        stkB2.close()
        stkAB.close()
        em.fence()
        stkC = ExitStack()
        qxT = sb("qxT", [128, NDC * GT], BF16, stkC)
        qtile = sb("qtile", [128, D], BF16, stkC)
        xf32 = sb("xf32c", [128, 2 * D], F32, stkC)
        i32buf = sb("i32bufc", [128, 1024], I32, stkC)
        fbuf = sb("fbufc", [128, 1024], F32, stkC)
        rbuf = sb("rbufc", [128, 1024], F32, stkC)
        ebuf = sb("ebufc", [128, 1024], F32, stkC)
        hbuf = sb("hbuf", [128, GTT * F], BF16, stkC)
        hmax = sb("hmax", [128, GTT * NFC], F32, stkC)
        axc = sb("axc", [128, GTT], F32, stkC)
        sxq = sb("sxq", [128, GTT], F32, stkC)
        hscol = sb("hscol", [128, GTT], F32, stkC)
        shq = sb("shq", [128, GTT], F32, stkC)
        hs2 = sb("hs2", [128, GTT], F32, stkC)
        hqT = sb("hqT", [128, NFC128 * 256], BF16, stkC)
        w1t = sb("w1t", [128, 2 * NDC * 512], BF16, stkC)
        w2t = sb("w2t", [128, 2 * 1024], BF16, stkC)
        wotc = sb("wotc", [128, 2 * 512], BF16, stkC)
        xrest = sb("xrest", [128, 2 * 512], F32, stkC)
        tAb = sb("tAb", [128, 512], F32, stkC)
        outb = sb("outb", [128, 2 * 512], F32, stkC)

        dstore_final = em.cnt["dstore"]

        for g in (range(NG) if cfg.phases >= 5 else []):
            em.tfence()
            with nc.psum_tensor("psTq_%d" % g, [128, 2048], BF16) as psTq:
                for tl in range(GTT):
                    tt = g * GTT + tl
                    par = tt % 2
                    dsem = "dxa0" if par == 0 else "dxa1"
                    xr = "xf.%d" % par
                    xnr = "xnb.%d" % par
                    xs = xf32[:, par * D:(par + 1) * D]
                    xns = xs

                    em.grp("sync", dsem, [], [xr, xnr],
                           lambda sy, tt=tt, xs=xs: sy.dma_start(
                               out=xs, in_=x_d[tt * 128:(tt + 1) * 128, :]),
                           dmak=True)

                    em.grp("vector", "sve", [xr, "lni", "lnb2"], [xnr],
                           lambda v, tt=tt, xs=xs, xns=xns: v.tensor_scalar(
                               out=xns, in0=xs, scalar1=lni[:, tt:tt + 1],
                               scalar2=lnb2[:, tt:tt + 1], op0=ALU.mult,
                               op1=ALU.add))

                    def c_ax(v, tl=tl, xns=xns, xnr=xnr):
                        em.rd(v, [xnr, "mb.w1"])
                        em.wr(v, ["axc", "sxq", "hscol"])
                        em.inc(v.reduce_max(out=axc[:, tl:tl + 1], in_=xns,
                                            axis=AX.X,
                                            apply_absolute_value=True),
                               "sve")
                        em.wait(v, "sve", em.cnt["sve"])
                        em.inc(v.tensor_scalar_max(axc[:, tl:tl + 1],
                                                   axc[:, tl:tl + 1], 1e-5),
                               "sve")
                        em.wait(v, "sve", em.cnt["sve"])
                        em.inc(v.reciprocal(out=sxq[:, tl:tl + 1],
                                            in_=axc[:, tl:tl + 1]), "sve")
                        em.wait(v, "sve", em.cnt["sve"])
                        em.inc(v.tensor_mul(scol[:, 7:8], axc[:, tl:tl + 1],
                                            sxq[:, tl:tl + 1]), "sve")
                        em.wait(v, "sve", em.cnt["sve"])
                        em.inc(v.tensor_scalar(out=scol[:, 7:8],
                                               in0=scol[:, 7:8],
                                               scalar1=-1.0, scalar2=2.0,
                                               op0=ALU.mult, op1=ALU.add),
                               "sve")
                        em.wait(v, "sve", em.cnt["sve"])
                        em.inc(v.tensor_mul(sxq[:, tl:tl + 1],
                                            sxq[:, tl:tl + 1],
                                            scol[:, 7:8]), "sve")
                        em.wait(v, "sve", em.cnt["sve"])
                        em.inc(v.tensor_scalar_mul(sxq[:, tl:tl + 1],
                                                   sxq[:, tl:tl + 1], 127.0),
                               "sve")
                        em.inc(v.scalar_tensor_tensor(
                            out=hscol[:, tl:tl + 1], in0=axc[:, tl:tl + 1],
                            scalar=1.0 / 127.0, in1=m1b[:], op0=ALU.mult,
                            op1=ALU.mult), "sve")
                        em.did_rd([xnr, "mb.w1"], "sve")
                        em.did_wr(["axc", "sxq", "hscol", "scol.nt"], "sve")
                    em.sec("vector", c_ax)

                    for qh in range(D // 1024):
                        xh = xns[:, qh * 1024:(qh + 1) * 1024]
                        qth = qtile[:, qh * 1024:(qh + 1) * 1024]

                        def c_q(v, tl=tl, xh=xh, xnr=xnr):
                            em.rd(v, [xnr, "sxq"])
                            em.wr(v, ["i32buf"])
                            em.inc(v.tensor_scalar(
                                out=i32buf[:], in0=xh,
                                scalar1=sxq[:, tl:tl + 1], scalar2=None,
                                op0=ALU.mult), "sve")
                            em.did_rd([xnr, "sxq"], "sve")
                            em.did_wr(["i32buf"], "sve")
                        em.sec("vector", c_q)

                        em.grp("vector", "sve", ["i32buf"], ["f0c"],
                               lambda v: v.tensor_scalar(
                                   out=fbuf[:], in0=i32buf[:], scalar1=2.0,
                                   scalar2=None, op0=ALU.subtract))

                        def c_q2(v, tl=tl, xh=xh, qth=qth, xnr=xnr):
                            em.rd(v, [xnr, "sxq", "f0c"])
                            em.wr(v, ["rbuf", "ebuf", "f0c", "qtile"])
                            em.inc(v.scalar_tensor_tensor(
                                out=rbuf[:], in0=xh,
                                scalar=sxq[:, tl:tl + 1], in1=fbuf[:],
                                op0=ALU.mult, op1=ALU.subtract), "sve")
                            em.wait(v, "sve", em.cnt["sve"])
                            em.inc(v.scalar_tensor_tensor(
                                out=ebuf[:], in0=rbuf[:], scalar=0.5,
                                in1=fbuf[:], op0=ALU.is_gt, op1=ALU.add),
                                "sve")
                            em.wait(v, "sve", em.cnt["sve"])
                            em.inc(v.scalar_tensor_tensor(
                                out=fbuf[:], in0=rbuf[:], scalar=1.5,
                                in1=ebuf[:], op0=ALU.is_ge, op1=ALU.add),
                                "sve")
                            em.wait(v, "sve", em.cnt["sve"])
                            em.inc(v.scalar_tensor_tensor(
                                out=ebuf[:], in0=rbuf[:], scalar=2.5,
                                in1=fbuf[:], op0=ALU.is_gt, op1=ALU.add),
                                "sve")
                            em.wait(v, "sve", em.cnt["sve"])
                            em.inc(v.scalar_tensor_tensor(
                                out=qth, in0=rbuf[:], scalar=3.5,
                                in1=ebuf[:], op0=ALU.is_ge, op1=ALU.add),
                                "sve")
                            em.did_rd([xnr, "sxq"], "sve")
                            em.did_wr(["rbuf", "ebuf", "f0c", "qtile"],
                                      "sve")
                        em.sec("vector", c_q2)

                        if getattr(cfg, "dbgqx", None) == (g, tl):
                            def dbgx(sy, qh=qh, xh=xh, qth=qth, xnr=xnr):
                                em.rd(sy, ["i32buf", "rbuf", "qtile",
                                           "sxq", xnr])
                                em.inc(sy.dma_start(
                                    out=dxq_i32[:, qh * 1024:
                                                (qh + 1) * 1024],
                                    in_=i32buf[:]), "ddbg", 16)
                                em.inc(sy.dma_start(
                                    out=dxq_rp[:, qh * 1024:(qh + 1) * 1024],
                                    in_=rbuf[:]), "ddbg", 16)
                                em.inc(sy.dma_start(
                                    out=dxq_out[:, qh * 1024:
                                                (qh + 1) * 1024],
                                    in_=qth), "ddbg", 16)
                                em.inc(sy.dma_start(
                                    out=dxq_in[:, qh * 1024:(qh + 1) * 1024],
                                    in_=xh), "ddbg", 16)
                                em.inc(sy.dma_start(out=dxq_s[:],
                                                    in_=sxq[:, 0:4]),
                                       "ddbg", 16)
                                em.wait(sy, "ddbg", em.cnt["ddbg"])
                            em.sec("sync", dbgx)
                            em.sec("vector",
                                   lambda v: em.wait(v, "ddbg",
                                                     em.cnt["ddbg"]))

                    for cb in range(NDC // 4):
                        pr = "psTq.%d" % (cb % 2)
                        po = (cb % 2) * 1024

                        def c_qtr(t, cb=cb, po=po):
                            for ci in range(4):
                                c = cb * 4 + ci
                                ins = t.transpose(
                                    psTq[:, po + ci * 128:po + (ci + 1) * 128],
                                    qtile[:, c * 128:(c + 1) * 128], eye[:])
                            return ins
                        em.grp("tensor", "ste", ["qtile", "eye"], [pr],
                               c_qtr)

                        def c_qtrev(v, tl=tl, cb=cb, po=po):
                            for ci in range(4):
                                c = cb * 4 + ci
                                ins = v.tensor_copy(
                                    out=qxT[:, c * GT + tl * 128:
                                            c * GT + (tl + 1) * 128],
                                    in_=psTq[:, po + ci * 128:po + (ci + 1) * 128])
                            return ins
                        em.grp("vector", "sve", [pr], ["qxT"], c_qtrev)

            em.tfence()
            with nc.psum_tensor("psH_%d" % g, [128, 1024], F32) as psH:
                mm1_ssc0 = em.cnt["ssc"]
                mm1_ste_after = {}
                for fc in range(NFC):
                    par = fc % 2
                    dsem = "dw1t0" if par == 0 else "dw1t1"
                    wres = "w1t.%d" % par

                    def c_w1dma(sy, fc=fc, par=par, wres=wres, dsem=dsem):
                        em.wr(sy, [wres])
                        em.wait(sy, "dstore", dstore_final)
                        for c in range(NDC):
                            em.inc(sy.dma_start(
                                out=w1t[:, par * NDC * 512 + c * 512:
                                        par * NDC * 512 + (c + 1) * 512],
                                in_=w1q_d[fc * 512:(fc + 1) * 512,
                                          c * 128:(c + 1) * 128],
                                transpose=True), dsem, 16)
                        em.did_wr([wres], dsem)
                    em.sec("sync", c_w1dma)

                    for tl in range(GTT):
                        idx = fc * GTT + tl
                        pp = idx % 2

                        def c_mm1(t, fc=fc, tl=tl, pp=pp, par=par, idx=idx,
                                  wres=wres):
                            em.rd(t, [wres, "qxT"])
                            if idx >= 2:
                                em.wait(t, "ssc", mm1_ssc0 + idx - 1)
                            for c in range(NDC):
                                ins = t.matmul(
                                    psH[:, pp * 512:(pp + 1) * 512],
                                    qxT[:, c * GT + tl * 128:
                                        c * GT + (tl + 1) * 128],
                                    w1t[:, par * NDC * 512 + c * 512:
                                        par * NDC * 512 + (c + 1) * 512],
                                    start=(c == 0), stop=(c == NDC - 1))
                            em.inc(ins, "ste")
                            em.did_rd([wres, "qxT"], "ste")
                        em.sec("tensor", c_mm1)
                        mm1_ste_after[idx] = em.cnt["ste"]

                def c_silu_all(s, g=g):
                    em.rd(s, ["hscol"])
                    em.wr(s, ["h.%d.%d" % (tl, fc)
                              for fc in range(NFC) for tl in range(GTT)])
                    for fc in range(NFC):
                        for tl in range(GTT):
                            idx = fc * GTT + tl
                            pp = idx % 2
                            em.wait(s, "ste", mm1_ste_after[idx])
                            ins = s.activation(
                                out=hbuf[:, tl * F + fc * 512:
                                         tl * F + (fc + 1) * 512],
                                in_=psH[:, pp * 512:(pp + 1) * 512],
                                func=AF.Silu, scale=hscol[:, tl:tl + 1])
                            em.inc(ins, "ssc")
                            em.did_wr(["h.%d.%d" % (tl, fc)], "ssc")
                    em.did_rd(["hscol"], "ssc")
                em.sec("scalar", c_silu_all)

                def c_hmax_all(v):
                    for fc in range(NFC):
                        for tl in range(GTT):
                            idx = fc * GTT + tl
                            em.wait(v, "ssc", mm1_ssc0 + idx + 1)
                            ins = v.reduce_max(
                                out=hmax[:, tl * NFC + fc:
                                         tl * NFC + fc + 1],
                                in_=hbuf[:, tl * F + fc * 512:
                                         tl * F + (fc + 1) * 512],
                                axis=AX.X, apply_absolute_value=True)
                            em.inc(ins, "sve")
                            em.did_rd(["h.%d.%d" % (tl, fc)], "sve")
                    em.did_wr(["hmax"], "sve")
                em.sec("vector", c_hmax_all)

            for th in range(GTT // 2):
                em.tfence()
                with nc.psum_tensor("psTh_%d_%d" % (g, th), [128, 2048],
                                    BF16) as psTh:
                    for tl in (2 * th, 2 * th + 1):
                        def c_hscale(v, tl=tl):
                            em.rd(v, ["hmax", "mb.w2"])
                            em.wr(v, ["axc", "shq", "hs2"])
                            em.inc(v.reduce_max(
                                out=axc[:, tl:tl + 1],
                                in_=hmax[:, tl * NFC:(tl + 1) * NFC],
                                axis=AX.X), "sve")
                            em.wait(v, "sve", em.cnt["sve"])
                            em.inc(v.tensor_scalar_max(axc[:, tl:tl + 1],
                                                       axc[:, tl:tl + 1],
                                                       1e-5), "sve")
                            em.wait(v, "sve", em.cnt["sve"])
                            em.inc(v.reciprocal(out=shq[:, tl:tl + 1],
                                                in_=axc[:, tl:tl + 1]),
                                   "sve")
                            em.wait(v, "sve", em.cnt["sve"])
                            em.inc(v.tensor_mul(scol[:, 7:8],
                                                axc[:, tl:tl + 1],
                                                shq[:, tl:tl + 1]), "sve")
                            em.wait(v, "sve", em.cnt["sve"])
                            em.inc(v.tensor_scalar(out=scol[:, 7:8],
                                                   in0=scol[:, 7:8],
                                                   scalar1=-1.0, scalar2=2.0,
                                                   op0=ALU.mult,
                                                   op1=ALU.add), "sve")
                            em.wait(v, "sve", em.cnt["sve"])
                            em.inc(v.tensor_mul(shq[:, tl:tl + 1],
                                                shq[:, tl:tl + 1],
                                                scol[:, 7:8]), "sve")
                            em.wait(v, "sve", em.cnt["sve"])
                            em.inc(v.tensor_scalar_mul(shq[:, tl:tl + 1],
                                                       shq[:, tl:tl + 1],
                                                       127.0), "sve")
                            em.inc(v.scalar_tensor_tensor(
                                out=hs2[:, tl:tl + 1],
                                in0=axc[:, tl:tl + 1], scalar=1.0 / 127.0,
                                in1=m2b[:], op0=ALU.mult, op1=ALU.mult),
                                "sve")
                            em.did_rd(["hmax", "mb.w2"], "sve")
                            em.did_wr(["axc", "shq", "hs2"], "sve")
                        em.sec("vector", c_hscale)

                        for sl in range(F // D):
                            hres = ["h.%d.%d" % (tl, fc) for fc in
                                    range(sl * (D // 512),
                                          (sl + 1) * (D // 512))]

                            for qh in range(D // 1024):
                                hh_ap = hbuf[:, tl * F + sl * D + qh * 1024:
                                             tl * F + sl * D + (qh + 1) * 1024]
                                qth = qtile[:, qh * 1024:(qh + 1) * 1024]

                                def c_hq(v, tl=tl, hh_ap=hh_ap, hres=hres):
                                    em.rd(v, ["shq"] + hres)
                                    em.wr(v, ["i32buf"])
                                    em.inc(v.tensor_scalar(
                                        out=i32buf[:], in0=hh_ap,
                                        scalar1=shq[:, tl:tl + 1],
                                        scalar2=None, op0=ALU.mult), "sve")
                                    em.did_rd(["shq"] + hres, "sve")
                                    em.did_wr(["i32buf"], "sve")
                                em.sec("vector", c_hq)

                                em.grp("vector", "sve", ["i32buf"], ["f0c"],
                                       lambda v: v.tensor_scalar(
                                           out=fbuf[:], in0=i32buf[:],
                                           scalar1=2.0, scalar2=None,
                                           op0=ALU.subtract))

                                def c_hq2(v, tl=tl, hh_ap=hh_ap, qth=qth,
                                          hres=hres):
                                    em.rd(v, ["shq", "f0c"] + hres)
                                    em.wr(v, ["rbuf", "ebuf", "f0c",
                                              "qtile"])
                                    em.inc(v.scalar_tensor_tensor(
                                        out=rbuf[:], in0=hh_ap,
                                        scalar=shq[:, tl:tl + 1],
                                        in1=fbuf[:], op0=ALU.mult,
                                        op1=ALU.subtract), "sve")
                                    em.wait(v, "sve", em.cnt["sve"])
                                    em.inc(v.scalar_tensor_tensor(
                                        out=ebuf[:], in0=rbuf[:], scalar=0.5,
                                        in1=fbuf[:], op0=ALU.is_gt,
                                        op1=ALU.add), "sve")
                                    em.wait(v, "sve", em.cnt["sve"])
                                    em.inc(v.scalar_tensor_tensor(
                                        out=fbuf[:], in0=rbuf[:], scalar=1.5,
                                        in1=ebuf[:], op0=ALU.is_ge,
                                        op1=ALU.add), "sve")
                                    em.wait(v, "sve", em.cnt["sve"])
                                    em.inc(v.scalar_tensor_tensor(
                                        out=ebuf[:], in0=rbuf[:], scalar=2.5,
                                        in1=fbuf[:], op0=ALU.is_gt,
                                        op1=ALU.add), "sve")
                                    em.wait(v, "sve", em.cnt["sve"])
                                    em.inc(v.scalar_tensor_tensor(
                                        out=qth, in0=rbuf[:], scalar=3.5,
                                        in1=ebuf[:], op0=ALU.is_ge,
                                        op1=ALU.add), "sve")
                                    em.did_rd(["shq"] + hres, "sve")
                                    em.did_wr(["rbuf", "ebuf", "f0c",
                                               "qtile"], "sve")
                                em.sec("vector", c_hq2)

                                if getattr(cfg, "dbgq", None) == \
                                        (g, tl, sl, qh):
                                    def dbgh(sy, hh_ap=hh_ap, qth=qth):
                                        em.rd(sy, ["i32buf", "rbuf",
                                                   "qtile", "shq"])
                                        em.inc(sy.dma_start(
                                            out=dhq_i32[:], in_=i32buf[:]),
                                            "ddbg", 16)
                                        em.inc(sy.dma_start(
                                            out=dhq_rp[:], in_=rbuf[:]),
                                            "ddbg", 16)
                                        em.inc(sy.dma_start(
                                            out=dhq_out[:], in_=qth),
                                            "ddbg", 16)
                                        em.inc(sy.dma_start(
                                            out=dhq_in[:], in_=hh_ap),
                                            "ddbg", 16)
                                        em.inc(sy.dma_start(
                                            out=dhq_s[:], in_=shq[:, 0:4]),
                                            "ddbg", 16)
                                        em.wait(sy, "ddbg",
                                                em.cnt["ddbg"])
                                    em.sec("sync", dbgh)
                                    em.sec("vector",
                                           lambda v: em.wait(
                                               v, "ddbg", em.cnt["ddbg"]))

                            for cb in range(NDC // 4):
                                pr = "psTh.%d" % (cb % 2)
                                po = (cb % 2) * 1024

                                def c_htr(t, cb=cb, po=po):
                                    for ci in range(4):
                                        c = cb * 4 + ci
                                        ins = t.transpose(
                                            psTh[:, po + ci * 128:
                                                 po + (ci + 1) * 128],
                                            qtile[:, c * 128:(c + 1) * 128],
                                            eye[:])
                                    return ins
                                em.grp("tensor", "ste", ["qtile", "eye"],
                                       [pr], c_htr)

                                def c_htrev(v, tl=tl, sl=sl, cb=cb, th=th,
                                            po=po):
                                    ttp = tl - 2 * th
                                    for ci in range(4):
                                        fc128 = sl * NDC + cb * 4 + ci
                                        ins = v.tensor_copy(
                                            out=hqT[:, fc128 * 256 +
                                                    ttp * 128:
                                                    fc128 * 256 +
                                                    (ttp + 1) * 128],
                                            in_=psTh[:, po + ci * 128:
                                                     po + (ci + 1) * 128])
                                    return ins
                                em.grp("vector", "sve", [pr], ["hqT"],
                                       c_htrev)

                em.tfence()
                with nc.psum_tensor("psO2_%d_%d" % (g, th), [128, 4096],
                                    F32) as psO2:
                    for dcp in range(NDOC // 2):
                        pb = dcp % 2
                        rrs = ["psO2.%d.%d" % (pb, k) for k in range(4)]
                        for fc128 in range(NFC128):
                            par = fc128 % 2
                            dsem = "dw2t0" if par == 0 else "dw2t1"
                            wres = "w2t.%d" % par

                            def c_w2sec(sy, dcp=dcp, fc128=fc128, par=par,
                                        dsem=dsem, wres=wres):
                                em.wr(sy, [wres])
                                em.wait(sy, "dstore", dstore_final)
                                ins = sy.dma_start(
                                    out=w2t[:, par * 1024:(par + 1) * 1024],
                                    in_=w2q_d[dcp * 1024:(dcp + 1) * 1024,
                                              fc128 * 128:(fc128 + 1) * 128],
                                    transpose=True)
                                em.inc(ins, dsem, 16)
                                em.did_wr([wres], dsem)
                            em.sec("scalar", c_w2sec)

                            def c_mm2(t, fc128=fc128, par=par, pb=pb,
                                      wres=wres, rrs=rrs):
                                st = (fc128 == 0)
                                sp = (fc128 == NFC128 - 1)
                                em.rd(t, [wres, "hqT"])
                                if st:
                                    em.wr(t, rrs)
                                for k in range(4):
                                    ti, di = divmod(k, 2)
                                    ins = t.matmul(
                                        psO2[:, (pb * 4 + k) * 512:
                                             (pb * 4 + k + 1) * 512],
                                        hqT[:, fc128 * 256 + ti * 128:
                                            fc128 * 256 + (ti + 1) * 128],
                                        w2t[:, par * 1024 + di * 512:
                                            par * 1024 + (di + 1) * 512],
                                        start=st, stop=sp)
                                em.inc(ins, "ste")
                                em.did_rd([wres, "hqT"], "ste")
                                if sp:
                                    em.did_wr(rrs, "ste")
                            em.sec("tensor", c_mm2)

                        for k in range(4):
                            ti, di = divmod(k, 2)
                            tl = 2 * th + ti
                            tt = g * GTT + tl
                            dc = dcp * 2 + di
                            pp = k % 2
                            rr = rrs[k]
                            wor = "wotc.%d" % pp
                            xrr = "xrest.%d" % pp
                            obr = "outb.%d" % pp
                            dsemw = "dwoC0" if pp == 0 else "dwoC1"
                            dsemx = "dres0" if pp == 0 else "dres1"

                            def c_lodma(sy, tt=tt, dc=dc, pp=pp, wor=wor,
                                        xrr=xrr, dsemw=dsemw, dsemx=dsemx):
                                em.wr(sy, [wor, xrr])
                                em.wait(sy, "dstore", dstore_final)
                                i1 = sy.dma_start(
                                    out=wotc[:, pp * 512:(pp + 1) * 512],
                                    in_=wo_scr_d[tt * 128:(tt + 1) * 128,
                                                 dc * 512:(dc + 1) * 512])
                                em.inc(i1, dsemw, 16)
                                i2 = sy.dma_start(
                                    out=xrest[:, pp * 512:(pp + 1) * 512],
                                    in_=x_d[tt * 128:(tt + 1) * 128,
                                            dc * 512:(dc + 1) * 512])
                                em.inc(i2, dsemx, 16)
                                em.did_wr([wor], dsemw)
                                em.did_wr([xrr], dsemx)
                            em.sec("sync", c_lodma)

                            em.grp("vector", "sve", [rr, "hs2"], ["tAb"],
                                   lambda v, pb=pb, k=k, tl=tl:
                                   v.tensor_scalar_mul(
                                       tAb[:],
                                       psO2[:, (pb * 4 + k) * 512:
                                            (pb * 4 + k + 1) * 512],
                                       hs2[:, tl:tl + 1]))

                            def c_out(v, pp=pp, wor=wor, xrr=xrr, obr=obr):
                                em.rd(v, ["tAb", wor, xrr])
                                em.wr(v, [obr])
                                em.inc(v.tensor_add(
                                    outb[:, pp * 512:(pp + 1) * 512],
                                    tAb[:],
                                    wotc[:, pp * 512:(pp + 1) * 512]), "sve")
                                em.wait(v, "sve", em.cnt["sve"])
                                em.inc(v.tensor_add(
                                    outb[:, pp * 512:(pp + 1) * 512],
                                    outb[:, pp * 512:(pp + 1) * 512],
                                    xrest[:, pp * 512:(pp + 1) * 512]),
                                    "sve")
                                em.did_rd(["tAb", wor, xrr], "sve")
                                em.did_wr([obr], "sve")
                            em.sec("vector", c_out)

                            em.grp("sync", "dout", [obr], [],
                                   lambda sy, tt=tt, dc=dc, pp=pp:
                                   sy.dma_start(
                                       out=y_d[tt * 128:(tt + 1) * 128,
                                               dc * 512:(dc + 1) * 512],
                                       in_=outb[:, pp * 512:(pp + 1) * 512]),
                                   dmak=True)

        def fin(sy):
            em.wait(sy, "dout", em.cnt["dout"])
            em.wait(sy, "dstore", em.cnt["dstore"])
        em.sec("sync", fin)
        stkC.close()

    gctx.close()
    nc.finalize()
    return nc


_NC_CACHE = {}


def _get_nc():
    if "nc" not in _NC_CACHE:
        _NC_CACHE["nc"] = build(CFG_FULL)
    return _NC_CACHE["nc"]


def kernel(L0_attn_x, L0_attn_Wq, L0_attn_Wk, L0_attn_Wv, L0_attn_Wo,
           L0_attn_lq1, L0_attn_lk1, L0_attn_lq2, L0_attn_lk2,
           L0_ffn_W1, L0_ffn_W2):
    x = np.ascontiguousarray(np.asarray(L0_attn_x, dtype=np.float32))
    ws = {
        "Wq": np.ascontiguousarray(np.asarray(L0_attn_Wq, dtype=np.float32)),
        "Wk": np.ascontiguousarray(np.asarray(L0_attn_Wk, dtype=np.float32)),
        "Wv": np.ascontiguousarray(np.asarray(L0_attn_Wv, dtype=np.float32)),
        "Wo": np.ascontiguousarray(np.asarray(L0_attn_Wo, dtype=np.float32)),
        "lq1": np.ascontiguousarray(np.asarray(L0_attn_lq1, dtype=np.float32)),
        "lk1": np.ascontiguousarray(np.asarray(L0_attn_lk1, dtype=np.float32)),
        "lq2": np.ascontiguousarray(np.asarray(L0_attn_lq2, dtype=np.float32)),
        "lk2": np.ascontiguousarray(np.asarray(L0_attn_lk2, dtype=np.float32)),
        "W1": np.ascontiguousarray(np.asarray(L0_ffn_W1, dtype=np.float32)),
        "W2": np.ascontiguousarray(np.asarray(L0_ffn_W2, dtype=np.float32)),
    }
    nc = _get_nc()
    n = x.shape[0]
    in_maps = [dict(ws, x=np.ascontiguousarray(x[b])) for b in range(n)]
    res = run_bass_kernel_spmd(nc, in_maps, core_ids=list(range(n)))
    return np.stack([np.asarray(res.results[b]["y"]) for b in range(n)],
                    axis=0)


# version 10 (final)
# speedup vs baseline: 1.2491x; 1.0657x over previous
import math
from contextlib import ExitStack

import numpy as np

import concourse.bass as bass
import concourse.mybir as mybir
from concourse.bass_utils import run_bass_kernel_spmd

F32 = mybir.dt.float32
BF16 = mybir.dt.bfloat16
I32 = mybir.dt.int32
AF = mybir.ActivationFunctionType
ALU = mybir.AluOpType
AX = mybir.AxisListType

DEPTH = 12
LAMBDA_INIT = 0.8 - 0.6 * math.exp(-0.3 * DEPTH)
LNC_CONST = math.log((1.0 - LAMBDA_INIT) * math.sqrt(128.0))
LNC_HI = float(np.float32(LNC_CONST))
LNC_LO = float(np.float32(LNC_CONST - LNC_HI))


class Cfg:
    def __init__(self, T, D, H, F, NG):
        self.T = T
        self.D = D
        self.H = H
        self.F = F
        self.NG = NG
        self.HD = 64
        assert D == 128 * H
        self.NTT = T // 128
        self.NDC = D // 128
        self.QB = 512
        assert T % self.QB == 0
        self.NQB = T // self.QB
        self.GT = T // NG
        self.GTT = self.GT // 128
        assert self.GTT % 2 == 0
        self.NFC = F // 512
        self.NFC128 = F // 128
        self.NDOC = D // 512
        self.phases = 5
        self.a_tiles = self.NTT
        self.a_steps = 9
        self.init_only = False
        self.debug = False


CFG_FULL = Cfg(T=2048, D=2048, H=16, F=8192, NG=4)


class Emit:

    def __init__(self, nc, block, stack):
        self.nc = nc
        self.block = block
        self.stack = stack
        self.sems = {}
        self.cnt = {}
        self.waited = {}
        self.res = {}
        self.cur = None

    def sem(self, name):
        h = self.stack.enter_context(self.nc.semaphore(name))
        self.sems[name] = h
        self.cnt[name] = 0

    def sec(self, engname, body):
        def f(eng):
            prev = self.cur
            self.cur = engname
            body(eng)
            self.cur = prev
        getattr(self.block, engname)(f)

    def wait(self, eng, semname, val):
        if val <= 0:
            return
        if semname.startswith("d"):
            val = self.cnt[semname]
        key = (self.cur, semname)
        if self.waited.get(key, -1) >= val:
            return
        eng.wait_ge(self.sems[semname], val)
        self.waited[key] = val

    def inc(self, instr, semname, k=1):
        instr.then_inc(self.sems[semname], k)
        self.cnt[semname] += k
        key = (self.cur, semname)
        return self.cnt[semname]

    def _get(self, rname):
        return self.res.setdefault(rname, {"w": None, "r": []})

    def rd(self, eng, rnames):
        for rn in rnames:
            r = self._get(rn)
            if r["w"] is not None:
                self.wait(eng, r["w"][0], r["w"][1])

    def wr(self, eng, rnames):
        for rn in rnames:
            r = self._get(rn)
            if r["w"] is not None:
                self.wait(eng, r["w"][0], r["w"][1])
            for (s, v) in r["r"]:
                self.wait(eng, s, v)

    def did_rd(self, rnames, semname):
        v = self.cnt[semname]
        for rn in rnames:
            self._get(rn)["r"].append((semname, v))

    def did_wr(self, rnames, semname):
        v = self.cnt[semname]
        for rn in rnames:
            r = self._get(rn)
            r["w"] = (semname, v)
            r["r"] = []

    def fence(self, engnames=None):
        sems = list(self.sems)
        for en in (engnames or ["tensor", "vector", "scalar", "gpsimd",
                                "sync"]):
            def f(eng, en=en):
                for s in sems:
                    self.wait(eng, s, self.cnt[s])
            self.sec(en, f)

    def tfence(self):
        def f(eng):
            for s in ("sve", "ssc", "sgp"):
                self.wait(eng, s, self.cnt[s])
        self.sec("tensor", f)

    def grp(self, engname, semname, reads, writes, body, dmak=None):
        def f(eng):
            self.rd(eng, reads)
            self.wr(eng, writes)
            last = body(eng)
            if last is not None:
                self.inc(last, semname, 16 if dmak else 1)
            self.did_rd(reads, semname)
            self.did_wr(writes, semname)
        self.sec(engname, f)


def build(cfg):
    T, D, H, F = cfg.T, cfg.D, cfg.H, cfg.F
    NTT, NDC, NQB, QB = cfg.NTT, cfg.NDC, cfg.NQB, cfg.QB
    NG, GT, GTT = cfg.NG, cfg.GT, cfg.GTT
    NFC, NFC128, NDOC = cfg.NFC, cfg.NFC128, cfg.NDOC
    JB = QB // 128

    nc = bass.Bass("TRN2", target_bir_lowering=False)

    x_d = nc.dram_tensor("x", [T, D], F32, kind="ExternalInput").ap()
    wq_d = nc.dram_tensor("Wq", [D, D], F32, kind="ExternalInput").ap()
    wk_d = nc.dram_tensor("Wk", [D, D], F32, kind="ExternalInput").ap()
    wv_d = nc.dram_tensor("Wv", [D, D], F32, kind="ExternalInput").ap()
    wo_d = nc.dram_tensor("Wo", [D, D], F32, kind="ExternalInput").ap()
    lq1_d = nc.dram_tensor("lq1", [64], F32, kind="ExternalInput").ap()
    lk1_d = nc.dram_tensor("lk1", [64], F32, kind="ExternalInput").ap()
    lq2_d = nc.dram_tensor("lq2", [64], F32, kind="ExternalInput").ap()
    lk2_d = nc.dram_tensor("lk2", [64], F32, kind="ExternalInput").ap()
    w1_d = nc.dram_tensor("W1", [F, D], F32, kind="ExternalInput").ap()
    w2_d = nc.dram_tensor("W2", [D, F], F32, kind="ExternalInput").ap()
    y_d = nc.dram_tensor("y", [T, D], F32, kind="ExternalOutput").ap()

    scrkind = "ExternalOutput" if cfg.debug else "Internal"
    w1q_d = nc.dram_tensor("w1q_scr", [F, D], BF16, kind=scrkind).ap()
    w2q_d = nc.dram_tensor("w2q_scr", [D, F], BF16, kind=scrkind).ap()
    wo_scr_d = nc.dram_tensor("wo_scr", [T, D], BF16, kind=scrkind).ap()
    if cfg.debug:
        xnT_dbg = nc.dram_tensor("xnT_dbg", [128, cfg.NDC * T], BF16,
                                 kind="ExternalOutput").ap()
        attnT_dbg = nc.dram_tensor("attnT_dbg", [128, H * T], BF16,
                                   kind="ExternalOutput").ap()
        cells_dbg = nc.dram_tensor("cells_dbg", [1, 16], F32,
                                   kind="ExternalOutput").ap()
        lni_dbg = nc.dram_tensor("lni_dbg", [128, 2 * cfg.NTT], F32,
                                 kind="ExternalOutput").ap()

    if getattr(cfg, "dbgqx", None) is not None:
        dxq_i32 = nc.dram_tensor("dxq_i32", [128, 2048], I32,
                                 kind="ExternalOutput").ap()
        dxq_rp = nc.dram_tensor("dxq_rp", [128, 2048], F32,
                                kind="ExternalOutput").ap()
        dxq_out = nc.dram_tensor("dxq_out", [128, 2048], BF16,
                                 kind="ExternalOutput").ap()
        dxq_in = nc.dram_tensor("dxq_in", [128, 2048], F32,
                                kind="ExternalOutput").ap()
        dxq_s = nc.dram_tensor("dxq_s", [128, 4], F32,
                               kind="ExternalOutput").ap()
    if getattr(cfg, "dbgq", None) is not None:
        dhq_i32 = nc.dram_tensor("dhq_i32", [128, 1024], I32,
                                 kind="ExternalOutput").ap()
        dhq_rp = nc.dram_tensor("dhq_rp", [128, 1024], F32,
                                kind="ExternalOutput").ap()
        dhq_out = nc.dram_tensor("dhq_out", [128, 1024], BF16,
                                 kind="ExternalOutput").ap()
        dhq_in = nc.dram_tensor("dhq_in", [128, 1024], BF16,
                                kind="ExternalOutput").ap()
        dhq_s = nc.dram_tensor("dhq_s", [128, 4], F32,
                               kind="ExternalOutput").ap()

    gctx = ExitStack()

    with nc.Block() as block:
        em = Emit(nc, block, gctx)
        for s in ["dg", "dstore", "dout", "ddbg",
                  "dwq0", "dwq1", "dxa0", "dxa1", "dw1t0", "dw1t1",
                  "dw2t0", "dw2t1", "dres0", "dres1", "dwoB0", "dwoB1",
                  "dwoC0", "dwoC1",
                  "ste", "sve", "ssc", "sgp"]:
            em.sem(s)

        def sb(name, shape, dt, stk=None):
            return (stk or gctx).enter_context(nc.sbuf_tensor(name, shape, dt))

        ones_col = sb("ones_col", [128, 1], BF16)
        ones_colf = sb("ones_colf", [128, 1], F32)
        ones_rowb = sb("ones_rowb", [1, 128], BF16)
        ones_rowf = sb("ones_rowf", [1, 128], F32)
        eye = sb("eye", [128, 128], BF16)
        tri = sb("tri", [128, 128], BF16)
        lamq = sb("lamq", [1, 256], F32)
        cells = sb("cells", [1, 16], F32)
        m1b = sb("m1b", [128, 1], F32)
        m2b = sb("m2b", [128, 1], F32)
        lni = sb("lni", [128, NTT], F32)
        lnb2 = sb("lnb2", [128, NTT], F32)
        scol = sb("scol", [128, 8], F32)
        epscol = sb("epscol", [128, 1], F32)
        scr128 = sb("scr128", [128, 128], BF16)

        lam11 = cells[0:1, 0:1]

        def init_v(v):
            em.inc(v.memset(cells[:], 0.0), "sve")
            em.wait(v, "sve", em.cnt["sve"])
            em.inc(v.memset(ones_col[:], 1.0), "sve")
            em.inc(v.memset(ones_colf[:], 1.0), "sve")
            em.inc(v.memset(ones_rowb[:], 1.0), "sve")
            em.inc(v.memset(ones_rowf[:], 1.0), "sve")
            em.inc(v.memset(epscol[:], 1e-5), "sve")
            em.inc(v.memset(cells[0:1, 15:16], LNC_HI), "sve")
            em.inc(v.memset(cells[0:1, 14:15], LNC_LO), "sve")
            em.inc(v.memset(scr128[:], 1.0), "sve")
            em.wait(v, "sve", em.cnt["sve"])
            em.inc(v.tensor_add(cells[0:1, 15:16], cells[0:1, 15:16],
                                cells[0:1, 14:15]), "sve")
            em.did_wr(["ones", "scr128", "lnccell"], "sve")
        em.sec("vector", init_v)

        def init_eye(g):
            return g.affine_select(out=eye[:], in_=scr128[:],
                                   pattern=[[1, 128]],
                                   compare_op=ALU.is_equal, fill=0.0, base=0,
                                   channel_multiplier=-1)
        em.grp("gpsimd", "sgp", ["scr128"], ["eye"], init_eye)

        def init_tri(g):
            return g.affine_select(out=tri[:], in_=scr128[:],
                                   pattern=[[1, 128]],
                                   compare_op=ALU.is_ge, fill=0.0, base=0,
                                   channel_multiplier=-1)
        em.grp("gpsimd", "sgp", ["scr128"], ["tri"], init_tri)

        def init_lamdma(g):
            for i, dsrc in enumerate([lq1_d, lk1_d, lq2_d, lk2_d]):
                ins = g.dma_start(out=lamq[0:1, i * 64:(i + 1) * 64],
                                  in_=dsrc[:])
                ins.then_inc(em.sems["dg"], 16)
                em.cnt["dg"] += 16
            em.did_wr(["lamq"], "dg")
        em.sec("gpsimd", init_lamdma)

        def init_lam_v(v):
            em.rd(v, ["lamq"])
            em.wr(v, ["lamq", "lamcells"])
            em.inc(v.tensor_mul(lamq[0:1, 0:64], lamq[0:1, 0:64],
                                lamq[0:1, 64:128]), "sve")
            em.wait(v, "sve", em.cnt["sve"])
            em.inc(v.reduce_sum(out=cells[0:1, 5:6], in_=lamq[0:1, 0:64],
                                axis=AX.X), "sve")
            em.inc(v.tensor_mul(lamq[0:1, 128:192], lamq[0:1, 128:192],
                                lamq[0:1, 192:256]), "sve")
            em.wait(v, "sve", em.cnt["sve"])
            em.inc(v.reduce_sum(out=cells[0:1, 6:7], in_=lamq[0:1, 128:192],
                                axis=AX.X), "sve")
            em.did_wr(["lamcells"], "sve")
        em.sec("vector", init_lam_v)

        def init_lam_s(s):
            em.rd(s, ["lamcells"])
            em.wr(s, ["lamexp"])
            em.inc(s.activation(out=cells[0:1, 1:2], in_=cells[0:1, 5:6],
                                func=AF.Exp), "ssc")
            em.inc(s.activation(out=cells[0:1, 2:3], in_=cells[0:1, 6:7],
                                func=AF.Exp), "ssc")
            em.did_wr(["lamexp"], "ssc")
        em.sec("scalar", init_lam_s)

        def init_lam_v2(v):
            em.rd(v, ["lamexp"])
            em.wr(v, ["lam"])
            em.inc(v.tensor_sub(cells[0:1, 0:1], cells[0:1, 1:2],
                                cells[0:1, 2:3]), "sve")
            em.wait(v, "sve", em.cnt["sve"])
            em.inc(v.tensor_scalar_add(cells[0:1, 0:1], cells[0:1, 0:1],
                                       LAMBDA_INIT), "sve")
            em.did_wr(["lam"], "sve")
        em.sec("vector", init_lam_v2)

        stkAB = ExitStack()
        stkAW = ExitStack()
        xnT = sb("xnT", [128, NDC * T], BF16, stkAB)
        attnT = sb("attnT", [128, H * T], BF16, stkAB)
        xf32 = sb("xf32", [128, 2 * D], F32, stkAW)
        xnb = sb("xnb", [128, 2 * D], BF16, stkAW)
        junk = sb("junk", [128, D], F32, stkAW)

        em.tfence()
        with nc.psum_tensor("psTA", [128, 2048], BF16) as psTA:
            for tt in (range(cfg.a_tiles) if cfg.phases >= 1 else []):
                par = tt % 2
                dsem = "dxa0" if par == 0 else "dxa1"
                xr = "xf.%d" % par
                xnr = "xnb.%d" % par
                xs = xf32[:, par * D:(par + 1) * D]
                xns = xnb[:, par * D:(par + 1) * D]

                if cfg.a_steps >= 1:
                    em.grp("sync", dsem, [], [xr],
                           lambda sy, tt=tt, xs=xs: sy.dma_start(
                               out=xs, in_=x_d[tt * 128:(tt + 1) * 128, :]),
                           dmak=True)
                if cfg.a_steps >= 2:
                    em.grp("vector", "sve", [xr], ["scol.sum"],
                           lambda v, xs=xs: v.reduce_sum(
                               out=scol[:, 0:1], in_=xs, axis=AX.X))
                if cfg.a_steps >= 3:
                    em.grp("scalar", "ssc", [xr], ["junk"],
                           lambda s, xs=xs: s.activation(
                               out=junk[:], in_=xs, func=AF.Square))
                if cfg.a_steps >= 4:
                    em.grp("vector", "sve", ["junk"], ["scol.sq"],
                           lambda v: v.reduce_sum(out=scol[:, 1:2],
                                                  in_=junk[:], axis=AX.X))

                if cfg.a_steps < 5:
                    continue

                def a_ln_v(v):
                    em.rd(v, ["scol.sum", "scol.sq"])
                    em.wr(v, ["scol.mv"])
                    em.inc(v.tensor_scalar_mul(scol[:, 2:3], scol[:, 0:1],
                                               1.0 / D), "sve")
                    em.wait(v, "sve", em.cnt["sve"])
                    em.inc(v.tensor_mul(scol[:, 3:4], scol[:, 2:3],
                                        scol[:, 2:3]), "sve")
                    em.wait(v, "sve", em.cnt["sve"])
                    em.inc(v.scalar_tensor_tensor(
                        out=scol[:, 4:5], in0=scol[:, 1:2], scalar=1.0 / D,
                        in1=scol[:, 3:4], op0=ALU.mult, op1=ALU.subtract),
                        "sve")
                    em.did_wr(["scol.mv"], "sve")
                em.sec("vector", a_ln_v)

                if cfg.a_steps < 6:
                    continue

                def a_ln_s(s, tt=tt):
                    em.rd(s, ["scol.mv"])
                    em.wr(s, ["lni", "scol.lnv"])
                    em.inc(s.activation(out=scol[:, 5:6], in_=scol[:, 4:5],
                                        func=AF.Ln, bias=epscol[:]), "ssc")
                    em.wait(s, "ssc", em.cnt["ssc"])
                    em.inc(s.activation(out=lni[:, tt:tt + 1],
                                        in_=scol[:, 5:6], func=AF.Exp,
                                        scale=-0.5), "ssc")
                    em.did_rd(["scol.mv"], "ssc")
                    em.did_wr(["lni", "scol.lnv"], "ssc")
                em.sec("scalar", a_ln_s)

                if cfg.a_steps < 7:
                    continue
                em.grp("vector", "sve", ["lni", "scol.mv"], ["lnb2"],
                       lambda v, tt=tt: v.scalar_tensor_tensor(
                           out=lnb2[:, tt:tt + 1], in0=scol[:, 2:3],
                           scalar=-1.0, in1=lni[:, tt:tt + 1],
                           op0=ALU.mult, op1=ALU.mult))

                if cfg.a_steps < 8:
                    continue
                em.grp("vector", "sve", [xr, "lni", "lnb2"], [xnr],
                       lambda v, tt=tt, xs=xs, xns=xns: v.tensor_scalar(
                           out=xns, in0=xs, scalar1=lni[:, tt:tt + 1],
                           scalar2=lnb2[:, tt:tt + 1], op0=ALU.mult,
                           op1=ALU.add))

                if cfg.a_steps < 9:
                    continue
                for cb in range(NDC // 4):
                    pr = "psTA.%d" % (cb % 2)
                    po = (cb % 2) * 1024

                    def a_tr(t, cb=cb, po=po, xns=xns):
                        for ci in range(4):
                            c = cb * 4 + ci
                            ins = t.transpose(
                                psTA[:, po + ci * 128:po + (ci + 1) * 128],
                                xns[:, c * 128:(c + 1) * 128], eye[:])
                        return ins
                    em.grp("tensor", "ste", [xnr, "eye"], [pr], a_tr)

                    def a_trev(v, tt=tt, cb=cb, po=po):
                        for ci in range(4):
                            c = cb * 4 + ci
                            ins = v.tensor_copy(
                                out=xnT[:, c * T + tt * 128:
                                        c * T + (tt + 1) * 128],
                                in_=psTA[:, po + ci * 128:po + (ci + 1) * 128])
                        return ins
                    em.grp("vector", "sve", [pr], ["xnT"], a_trev)

        i32buf = sb("i32buf", [128, D], I32, stkAW)
        wr32 = sb("wr32", [128, D], F32, stkAW)
        ac4 = sb("ac4", [128, D // 512], F32, stkAW)
        h4 = sb("h4", [128, D // 512], I32, stkAW)
        l4 = sb("l4", [128, D // 512], F32, stkAW)
        hlr = sb("hlr", [128, 4], F32, stkAW)
        wrows = sb("wrows", [1, 512], F32, stkAW)
        hp2 = sb("hp2", [1, 128], I32, stkAW)
        hlo = sb("hlo", [1, 128], F32, stkAW)

        def wprep(w_src, wq_dst, R, C, mcell, mcname, mbvec, tag):
            NP_C = C // D
            npieces = (R // 128) * NP_C
            assert npieces <= 128
            SC = float(2.0 ** 17)
            ISC = float(2.0 ** -17)
            NINV = 1.0 / (R * C)
            assert (R * C) & (R * C - 1) == 0

            with nc.psum_tensor("psW_" + tag, [128, 8], F32) as psW:
                em.tfence()
                for pi in range(npieces):
                    rr, cc = divmod(pi, NP_C)
                    par = pi % 2
                    dsem = "dxa0" if par == 0 else "dxa1"
                    xr = "xf.%d" % par
                    xs = xf32[:, par * D:(par + 1) * D]

                    em.grp("sync", dsem, [], [xr],
                           lambda sy, rr=rr, cc=cc, xs=xs: sy.dma_start(
                               out=xs, in_=w_src[rr * 128:(rr + 1) * 128,
                                                 cc * D:(cc + 1) * D]),
                           dmak=True)

                    em.grp("scalar", "ssc", [xr], ["junk"],
                           lambda s, xs=xs: s.activation(
                               out=junk[:], in_=xs, func=AF.Abs))

                    def wp_sum(v, pi=pi):
                        em.rd(v, ["junk"])
                        em.wr(v, ["i32buf", "wr32", "ac4", "h4", "l4",
                                  "hlr"])
                        em.inc(v.tensor_scalar(out=i32buf[:], in0=junk[:],
                                               scalar1=SC, scalar2=None,
                                               op0=ALU.mult), "sve")
                        em.wait(v, "sve", em.cnt["sve"])
                        em.inc(v.scalar_tensor_tensor(
                            out=wr32[:], in0=i32buf[:], scalar=ISC,
                            in1=junk[:], op0=ALU.mult, op1=ALU.subtract),
                            "sve")
                        for ch in range(D // 512):
                            em.inc(v.reduce_sum(
                                out=ac4[:, ch:ch + 1],
                                in_=i32buf[:, ch * 512:(ch + 1) * 512],
                                axis=AX.X), "sve")
                        em.wait(v, "sve", em.cnt["sve"])
                        em.inc(v.reduce_sum(out=hlr[:, 2:3], in_=wr32[:],
                                            axis=AX.X), "sve")
                        em.inc(v.tensor_scalar(out=h4[:], in0=ac4[:],
                                               scalar1=float(2.0 ** -10),
                                               scalar2=None, op0=ALU.mult),
                               "sve")
                        em.wait(v, "sve", em.cnt["sve"])
                        em.inc(v.scalar_tensor_tensor(
                            out=l4[:], in0=h4[:], scalar=-1024.0,
                            in1=ac4[:], op0=ALU.mult, op1=ALU.add), "sve")
                        em.wait(v, "sve", em.cnt["sve"])
                        em.inc(v.reduce_sum(out=hlr[:, 0:1], in_=h4[:],
                                            axis=AX.X), "sve")
                        em.inc(v.reduce_sum(out=hlr[:, 1:2], in_=l4[:],
                                            axis=AX.X), "sve")
                        em.did_rd(["junk"], "sve")
                        em.did_wr(["i32buf", "wr32", "ac4", "h4", "l4",
                                   "hlr"], "sve")
                    em.sec("vector", wp_sum)

                    def wp_mm(t, pi=pi):
                        for k in range(3):
                            ins = t.matmul(psW[0:1, k:k + 1],
                                           hlr[:, k:k + 1], ones_colf[:],
                                           start=True, stop=True)
                        return ins
                    em.grp("tensor", "ste", ["hlr", "ones"], ["psW"], wp_mm)

                    def wp_ev(v, pi=pi):
                        em.rd(v, ["psW"])
                        em.wr(v, ["wrows"])
                        for k in range(3):
                            ins = v.tensor_copy(
                                out=wrows[0:1, k * 128 + pi:
                                          k * 128 + pi + 1],
                                in_=psW[0:1, k:k + 1])
                        em.inc(ins, "sve")
                        em.did_rd(["psW"], "sve")
                        em.did_wr(["wrows"], "sve")
                    em.sec("vector", wp_ev)

                def wp_fin(v, npieces=npieces, mcell=mcell, tag=tag,
                           mcname=mcname):
                    em.rd(v, ["wrows"])
                    em.wr(v, ["hp2", "hlo", "meancells", "lamcells",
                              "lamexp", mcname, "cells.s" + tag])
                    np_ = npieces
                    em.inc(v.tensor_scalar(out=hp2[0:1, 0:np_],
                                           in0=wrows[0:1, 0:np_],
                                           scalar1=float(2.0 ** -10),
                                           scalar2=None, op0=ALU.mult),
                           "sve")
                    em.wait(v, "sve", em.cnt["sve"])
                    em.inc(v.scalar_tensor_tensor(
                        out=hlo[0:1, 0:np_], in0=hp2[0:1, 0:np_],
                        scalar=-1024.0, in1=wrows[0:1, 0:np_],
                        op0=ALU.mult, op1=ALU.add), "sve")
                    em.wait(v, "sve", em.cnt["sve"])
                    em.inc(v.reduce_sum(out=cells[0:1, 5:6],
                                        in_=hp2[0:1, 0:np_], axis=AX.X),
                           "sve")
                    em.inc(v.reduce_sum(out=cells[0:1, 6:7],
                                        in_=hlo[0:1, 0:np_], axis=AX.X),
                           "sve")
                    em.inc(v.reduce_sum(out=cells[0:1, 8:9],
                                        in_=wrows[0:1, 128:128 + np_],
                                        axis=AX.X), "sve")
                    em.inc(v.reduce_sum(out=cells[0:1, 9:10],
                                        in_=wrows[0:1, 256:256 + np_],
                                        axis=AX.X), "sve")
                    em.wait(v, "sve", em.cnt["sve"])
                    em.inc(v.tensor_scalar(out=cells[0:1, 10:11],
                                           in0=cells[0:1, 5:6],
                                           scalar1=(2.0 ** 20) * ISC * NINV,
                                           scalar2=None, op0=ALU.mult),
                           "sve")
                    em.inc(v.tensor_scalar(out=cells[0:1, 11:12],
                                           in0=cells[0:1, 6:7],
                                           scalar1=(2.0 ** 10) * ISC * NINV,
                                           scalar2=None, op0=ALU.mult),
                           "sve")
                    em.inc(v.tensor_scalar(out=cells[0:1, 12:13],
                                           in0=cells[0:1, 8:9],
                                           scalar1=ISC * NINV, scalar2=None,
                                           op0=ALU.mult), "sve")
                    em.inc(v.tensor_scalar(out=cells[0:1, 13:14],
                                           in0=cells[0:1, 9:10],
                                           scalar1=-NINV, scalar2=None,
                                           op0=ALU.mult), "sve")
                    em.wait(v, "sve", em.cnt["sve"])
                    em.inc(v.tensor_add(cells[0:1, 12:13],
                                        cells[0:1, 12:13],
                                        cells[0:1, 13:14]), "sve")
                    em.wait(v, "sve", em.cnt["sve"])
                    em.inc(v.tensor_add(cells[0:1, 11:12],
                                        cells[0:1, 11:12],
                                        cells[0:1, 12:13]), "sve")
                    em.wait(v, "sve", em.cnt["sve"])
                    em.inc(v.tensor_add(cells[0:1, 10:11],
                                        cells[0:1, 10:11],
                                        cells[0:1, 11:12]), "sve")
                    em.wait(v, "sve", em.cnt["sve"])
                    em.inc(v.tensor_scalar(out=mcell, in0=cells[0:1, 10:11],
                                           scalar1=1e-5, scalar2=None,
                                           op0=ALU.max), "sve")
                    em.wait(v, "sve", em.cnt["sve"])
                    em.inc(v.reciprocal(out=cells[0:1, 7:8], in_=mcell),
                           "sve")
                    em.wait(v, "sve", em.cnt["sve"])
                    em.inc(v.tensor_mul(cells[0:1, 13:14], mcell,
                                        cells[0:1, 7:8]), "sve")
                    em.wait(v, "sve", em.cnt["sve"])
                    em.inc(v.tensor_scalar_add(cells[0:1, 13:14],
                                               cells[0:1, 13:14], -1.0),
                           "sve")
                    em.wait(v, "sve", em.cnt["sve"])
                    em.inc(v.tensor_mul(cells[0:1, 13:14],
                                        cells[0:1, 13:14],
                                        cells[0:1, 7:8]), "sve")
                    em.wait(v, "sve", em.cnt["sve"])
                    em.inc(v.tensor_sub(cells[0:1, 7:8], cells[0:1, 7:8],
                                        cells[0:1, 13:14]), "sve")
                    em.did_rd(["wrows"], "sve")
                    em.did_wr(["hp2", "hlo", "meancells", "lamcells",
                               "lamexp", mcname, "cells.s" + tag], "sve")
                em.sec("vector", wp_fin)

                def w_bc_t(t, mcell=mcell):
                    ins = t.matmul(psW[:, 4:5], ones_rowf[0:1, :],
                                   cells[0:1, 7:8], start=True, stop=True)
                    ins = t.matmul(psW[:, 5:6], ones_rowf[0:1, :], mcell,
                                   start=True, stop=True)
                    return ins
                em.grp("tensor", "ste", [mcname, "cells.s" + tag, "ones"],
                       ["psW2"], w_bc_t)

                def w_bc_v(v, mbvec=mbvec):
                    em.rd(v, ["psW2"])
                    em.wr(v, ["srecip." + tag, "mb." + tag])
                    em.inc(v.tensor_copy(out=scol[:, 6:7], in_=psW[:, 4:5]),
                           "sve")
                    em.inc(v.tensor_copy(out=mbvec, in_=psW[:, 5:6]), "sve")
                    em.did_rd(["psW2"], "sve")
                    em.did_wr(["srecip." + tag, "mb." + tag], "sve")
                em.sec("vector", w_bc_v)

                for pi in range(npieces):
                    rr, cc = divmod(pi, NP_C)
                    par = pi % 2
                    dsem = "dxa0" if par == 0 else "dxa1"
                    xr = "xf.%d" % par
                    xnr = "xnb.%d" % par
                    xs = xf32[:, par * D:(par + 1) * D]
                    xns = xnb[:, par * D:(par + 1) * D]

                    em.grp("sync", dsem, [], [xr],
                           lambda sy, rr=rr, cc=cc, xs=xs: sy.dma_start(
                               out=xs, in_=w_src[rr * 128:(rr + 1) * 128,
                                                 cc * D:(cc + 1) * D]),
                           dmak=True)

                    def wq_q(v, xs=xs, xr=xr, xnr=xnr, tag=tag):
                        em.rd(v, [xr, "srecip." + tag])
                        em.wr(v, ["wr32", "junk", xnr])
                        em.inc(v.tensor_scalar(out=wr32[:], in0=xs,
                                               scalar1=scol[:, 6:7],
                                               scalar2=None, op0=ALU.mult),
                               "sve")
                        em.wait(v, "sve", em.cnt["sve"])
                        em.inc(v.tensor_scalar(out=junk[:], in0=wr32[:],
                                               scalar1=-0.5, scalar2=None,
                                               op0=ALU.is_le), "sve")
                        em.wait(v, "sve", em.cnt["sve"])
                        em.inc(v.scalar_tensor_tensor(
                            out=xns, in0=wr32[:], scalar=0.5, in1=junk[:],
                            op0=ALU.is_ge, op1=ALU.subtract), "sve")
                        em.did_rd([xr, "srecip." + tag], "sve")
                        em.did_wr(["wr32", "junk", xnr], "sve")
                    em.sec("vector", wq_q)

                    em.grp("sync", "dstore", [xnr], [],
                           lambda sy, rr=rr, cc=cc, xns=xns: sy.dma_start(
                               out=wq_dst[rr * 128:(rr + 1) * 128,
                                          cc * D:(cc + 1) * D], in_=xns),
                           dmak=True)

        if cfg.phases >= 2:
            wprep(w1_d, w1q_d, F, D, cells[0:1, 3:4], "m1c", m1b[:, 0:1],
                  "w1")
            wprep(w2_d, w2q_d, D, F, cells[0:1, 4:5], "m2c", m2b[:, 0:1],
                  "w2")

        stkAW.close()
        em.fence()
        stkB = ExitStack()
        wqb = sb("wqb", [128, 2 * D], BF16, stkB)
        wkb = sb("wkb", [128, 2 * D], BF16, stkB)
        wvb = sb("wvb", [128, 2 * D], BF16, stkB)
        Qc = sb("Qc", [128, T], BF16, stkB)
        Kc = sb("Kc", [128, T], BF16, stkB)
        Vt = sb("Vt", [128, T], BF16, stkB)
        Vk = sb("Vk", [128, T], BF16, stkB)
        sqQK = sb("sqQK", [128, T], BF16, stkB)
        P0T = sb("P0T", [128, 1024], BF16, stkB)
        P1T = sb("P1T", [128, 1024], BF16, stkB)
        bcol = sb("bcol", [128, 2], F32, stkB)
        maxc = sb("maxc", [1, 64], F32, stkB)
        s01sb = sb("s01sb", [1, 1024], F32, stkB)
        r1row = sb("r1row", [1, 512], F32, stkB)
        s0sq = sb("s0sq", [1, 512], F32, stkB)
        erow = sb("erow", [1, 512], F32, stkB)
        crow = sb("crow", [1, 512], BF16, stkB)
        frow = sb("frow", [1, 512], BF16, stkB)
        Csb = sb("Csb", [128, 512], BF16, stkB)
        t1b = sb("t1b", [128, 512], F32, stkB)
        apb = sb("apb", [128, 512], F32, stkB)
        sqb = sb("sqb", [128, 512], BF16, stkB)

        QKSC = float(8.0 ** -0.5)
        LNC = math.log((1.0 - LAMBDA_INIT) * math.sqrt(128.0))
        nbq = T // 512

        for h in (range(H) if cfg.phases >= 3 else []):
            par = h % 2
            dsem = "dwq0" if par == 0 else "dwq1"
            wres = "whd.%d" % par

            def b_wdma(g, h=h, par=par, dsem=dsem):
                for wdst, wsrc in ((wqb, wq_d), (wkb, wk_d), (wvb, wv_d)):
                    dst = wdst[:, par * D:(par + 1) * D].rearrange(
                        "p (c j) -> p c j", j=128)
                    src = wsrc[:, h * 128:(h + 1) * 128].rearrange(
                        "(c p) j -> p c j", p=128)
                    em.inc(g.dma_start(out=dst, in_=src), dsem, 16)
                return None
            em.grp("gpsimd", dsem, [], [wres], b_wdma, dmak=True)

            em.tfence()
            with ExitStack() as pstk:
                psQ = pstk.enter_context(
                    nc.psum_tensor("psQ_%d" % h, [128, T], F32))
                psK = pstk.enter_context(
                    nc.psum_tensor("psK_%d" % h, [128, T], F32))

                def b_projqk(t, par=par):
                    for wsb, ps in ((wqb, psQ), (wkb, psK)):
                        for nb in range(T // 512):
                            for c in range(NDC):
                                ins = t.matmul(
                                    ps[:, nb * 512:(nb + 1) * 512],
                                    wsb[:, par * D + c * 128:
                                        par * D + (c + 1) * 128],
                                    xnT[:, c * T + nb * 512:
                                        c * T + (nb + 1) * 512],
                                    start=(c == 0), stop=(c == NDC - 1))
                    return ins
                em.grp("tensor", "ste", [wres, "xnT"], ["psQ", "psK"],
                       b_projqk)

                def b_evqk(v):
                    ins = v.tensor_scalar_mul(Qc[:], psQ[:], QKSC)
                    ins = v.tensor_scalar_mul(Kc[:], psK[:], QKSC)
                    return ins
                em.grp("vector", "sve", ["psQ", "psK"], ["Qc", "Kc"], b_evqk)

                def b_projv(t, par=par):
                    for nb in range(T // 512):
                        for c in range(NDC):
                            ins = t.matmul(
                                psQ[:, nb * 512:(nb + 1) * 512],
                                wvb[:, par * D + c * 128:
                                    par * D + (c + 1) * 128],
                                xnT[:, c * T + nb * 512:c * T + (nb + 1) * 512],
                                start=(c == 0), stop=(c == NDC - 1))
                    return ins
                em.grp("tensor", "ste", [wres, "xnT"], ["psQ"], b_projv)

                em.grp("vector", "sve", ["psQ"], ["Vt"],
                       lambda v: v.tensor_copy(out=Vt[:], in_=psQ[:]))

            em.tfence()
            with ExitStack() as pstk:
                psV = pstk.enter_context(
                    nc.psum_tensor("psV_%d" % h, [128, 2048], BF16))
                psN = pstk.enter_context(
                    nc.psum_tensor("psN_%d" % h, [1, 1024], F32))
                psB = pstk.enter_context(
                    nc.psum_tensor("psB_%d" % h, [128, 2], F32))

                for cb in range(NTT // 4):
                    pr = "psV.%d" % (cb % 2)
                    po = (cb % 2) * 1024

                    def b_vtr(t, cb=cb, po=po):
                        for ci in range(4):
                            c = cb * 4 + ci
                            ins = t.transpose(
                                psV[:, po + ci * 128:po + (ci + 1) * 128],
                                Vt[:, c * 128:(c + 1) * 128], eye[:])
                        return ins
                    em.grp("tensor", "ste", ["Vt", "eye"], [pr], b_vtr)

                    em.grp("vector", "sve", [pr], ["Vk"],
                           lambda v, cb=cb, po=po: v.tensor_copy(
                               out=Vk[:, cb * 512:(cb + 1) * 512],
                               in_=psV[:, po:po + 512]))

                for src_name, base in (("q", 0), ("k", 2 * nbq)):
                    srcb = Qc if src_name == "q" else Kc
                    em.grp("vector", "sve", ["Qc" if src_name == "q"
                                             else "Kc"], ["sqQK"],
                           lambda v, srcb=srcb: v.tensor_mul(
                               sqQK[:], srcb[:], srcb[:]))
                    for hh in range(2):
                        hs = slice(hh * 64, hh * 64 + 64)
                        for nb in range(nbq):
                            pn = "psN.%d" % hh
                            em.grp("tensor", "ste", ["sqQK", "ones"], [pn],
                                   lambda t, hh=hh, nb=nb, hs=hs: t.matmul(
                                       psN[0:1, hh * 512:hh * 512 + 512],
                                       ones_col[hs, 0:1],
                                       sqQK[hs, nb * 512:(nb + 1) * 512],
                                       start=True, stop=True))
                            em.grp("vector", "sve", [pn], ["maxc"],
                                   lambda v, hh=hh, nb=nb, base=base:
                                   v.reduce_max(
                                       out=maxc[0:1, base + hh * nbq + nb:
                                                base + hh * nbq + nb + 1],
                                       in_=psN[0:1, hh * 512:hh * 512 + 512],
                                       axis=AX.X))

                def b_bnd_v(v):
                    em.rd(v, ["maxc"])
                    em.wr(v, ["bndcells"])
                    for hh in range(2):
                        em.inc(v.reduce_max(
                            out=cells[0:1, 8 + hh:9 + hh],
                            in_=maxc[0:1, hh * nbq:(hh + 1) * nbq],
                            axis=AX.X), "sve")
                        em.inc(v.reduce_max(
                            out=cells[0:1, 10 + hh:11 + hh],
                            in_=maxc[0:1, 2 * nbq + hh * nbq:
                                     2 * nbq + (hh + 1) * nbq],
                            axis=AX.X), "sve")
                    em.wait(v, "sve", em.cnt["sve"])
                    for hh in range(2):
                        ins = v.tensor_add(cells[0:1, 12 + hh:13 + hh],
                                           cells[0:1, 8 + hh:9 + hh],
                                           cells[0:1, 10 + hh:11 + hh])
                        em.inc(ins, "sve")
                    em.did_rd(["maxc"], "sve")
                    em.did_wr(["bndcells"], "sve")
                em.sec("vector", b_bnd_v)

                def b_bnd_t(t):
                    for hh in range(2):
                        ins = t.matmul(psB[:, hh:hh + 1], ones_rowf[0:1, :],
                                       cells[0:1, 12 + hh:13 + hh],
                                       start=True, stop=True)
                    return ins
                em.grp("tensor", "ste", ["bndcells", "ones"], ["psB"],
                       b_bnd_t)

                def b_bnd_s(v):
                    for hh in range(2):
                        ins = v.tensor_scalar_mul(bcol[:, hh:hh + 1],
                                                  psB[:, hh:hh + 1], -0.5)
                    return ins
                em.grp("vector", "sve", ["psB"], ["bcol"], b_bnd_s)

            em.tfence()
            with ExitStack() as pstk:
                psS0 = pstk.enter_context(
                    nc.psum_tensor("psS0_%d" % h, [128, 1024], F32))
                psS1 = pstk.enter_context(
                    nc.psum_tensor("psS1_%d" % h, [128, 1024], F32))
                psA0 = pstk.enter_context(
                    nc.psum_tensor("psA0_%d" % h, [128, 512], F32))
                psA1 = pstk.enter_context(
                    nc.psum_tensor("psA1_%d" % h, [128, 512], F32))
                psRW = pstk.enter_context(
                    nc.psum_tensor("psRW_%d" % h, [1, 1024], F32))

                for qb in range(NQB):
                    jmax = (qb + 1) * JB - 1
                    diag_js = [j for j in range(jmax + 1) if j >= qb * JB]
                    ste0 = em.cnt["ste"]
                    ssc0 = em.cnt["ssc"]
                    sve0 = em.cnt["sve"]

                    def jinfo(j, qb=qb):
                        jp = j % 2
                        dg = (j >= qb * JB)
                        off = 128 * (j - qb * JB) if dg else 0
                        return jp, dg, off

                    PSN = ["psS0.0", "psS0.1", "psS1.0", "psS1.1"]
                    PN = ["P0T.0", "P0T.1", "P1T.0", "P1T.1"]

                    def qb_tensor(t, qb=qb, jmax=jmax, ste0=ste0,
                                  ssc0=ssc0, sve0=sve0, jinfo=jinfo,
                                  diag_js=diag_js):
                        em.rd(t, ["Qc", "Kc", "Vk", "ones"] + PN)
                        em.wr(t, PSN + ["psA0", "psA1", "psRW"])

                        def emit_S(j):
                            jp, dg, off = jinfo(j)
                            q0 = qb * QB + off
                            Nn = QB - off
                            if j >= 2:
                                em.wait(t, "ssc", ssc0 + j - 1)
                            t.matmul(
                                psS0[:, jp * 512 + off:(jp + 1) * 512],
                                Kc[0:64, j * 128:(j + 1) * 128],
                                Qc[0:64, q0:q0 + Nn], start=True, stop=True)
                            ins = t.matmul(
                                psS1[:, jp * 512 + off:(jp + 1) * 512],
                                Kc[64:128, j * 128:(j + 1) * 128],
                                Qc[64:128, q0:q0 + Nn], start=True,
                                stop=True)
                            em.inc(ins, "ste")

                        def emit_AV(j):
                            jp, dg, off = jinfo(j)
                            em.wait(t, "ssc", ssc0 + j + 1)
                            if dg:
                                nm = sum(1 for x in diag_js if x <= j)
                                em.wait(t, "sve", sve0 + nm)
                            st = (j == 0)
                            sp = (j == jmax)
                            t.matmul(psA0[:, off:512],
                                     Vk[:, j * 128:(j + 1) * 128],
                                     P0T[:, jp * 512 + off:(jp + 1) * 512],
                                     start=st, stop=sp)
                            t.matmul(psA1[:, off:512],
                                     Vk[:, j * 128:(j + 1) * 128],
                                     P1T[:, jp * 512 + off:(jp + 1) * 512],
                                     start=st, stop=sp)
                            t.matmul(psRW[0:1, off:512], ones_col[:, 0:1],
                                     P0T[:, jp * 512 + off:(jp + 1) * 512],
                                     start=st, stop=sp)
                            ins = t.matmul(
                                psRW[0:1, 512 + off:1024], ones_col[:, 0:1],
                                P1T[:, jp * 512 + off:(jp + 1) * 512],
                                start=st, stop=sp)
                            em.inc(ins, "ste")

                        emit_S(0)
                        for j in range(1, jmax + 1):
                            emit_S(j)
                            emit_AV(j - 1)
                        emit_AV(jmax)
                        em.did_rd(["Qc", "Kc", "Vk", "ones"] + PN, "ste")
                        em.did_wr(PSN + ["psA0", "psA1", "psRW"], "ste")
                    em.sec("tensor", qb_tensor)

                    def qb_scalar(s, qb=qb, jmax=jmax, ste0=ste0,
                                  ssc0=ssc0, jinfo=jinfo):
                        em.rd(s, ["bcol"])
                        for j in range(jmax + 1):
                            em.wait(s, "ste",
                                    ste0 + (1 if j == 0 else 2 * j))
                            jp, dg, off = jinfo(j)
                            s.activation(
                                out=P0T[:, jp * 512 + off:(jp + 1) * 512],
                                in_=psS0[:, jp * 512 + off:(jp + 1) * 512],
                                func=AF.Exp, bias=bcol[:, 0:1])
                            ins = s.activation(
                                out=P1T[:, jp * 512 + off:(jp + 1) * 512],
                                in_=psS1[:, jp * 512 + off:(jp + 1) * 512],
                                func=AF.Exp, bias=bcol[:, 1:2])
                            em.inc(ins, "ssc")
                        em.did_rd(["bcol"] + PSN, "ssc")
                        em.did_wr(PN, "ssc")
                    em.sec("scalar", qb_scalar)

                    def qb_mask(v, qb=qb, ssc0=ssc0, jinfo=jinfo,
                                diag_js=diag_js):
                        em.rd(v, ["tri"])
                        for j in diag_js:
                            jp, dg, off = jinfo(j)
                            em.wait(v, "ssc", ssc0 + j + 1)
                            v.tensor_mul(
                                P0T[:, jp * 512 + off:jp * 512 + off + 128],
                                P0T[:, jp * 512 + off:jp * 512 + off + 128],
                                tri[:])
                            ins = v.tensor_mul(
                                P1T[:, jp * 512 + off:jp * 512 + off + 128],
                                P1T[:, jp * 512 + off:jp * 512 + off + 128],
                                tri[:])
                            em.inc(ins, "sve")
                        em.did_rd(["tri"], "sve")
                    em.sec("vector", qb_mask)

                    em.grp("vector", "sve", ["psRW"], ["s01sb"],
                           lambda v: v.tensor_copy(out=s01sb[:],
                                                   in_=psRW[0:1, :]))

                    def ep_v1(v):
                        em.rd(v, ["s01sb", "lam"])
                        em.wr(v, ["r1row", "s0sq", "crow"])
                        em.inc(v.reciprocal(out=r1row[:],
                                            in_=s01sb[0:1, 512:1024]), "sve")
                        em.wait(v, "sve", em.cnt["sve"])
                        em.inc(v.tensor_mul(s0sq[:], r1row[:],
                                            s01sb[0:1, 512:1024]), "sve")
                        em.wait(v, "sve", em.cnt["sve"])
                        em.inc(v.tensor_scalar(out=s0sq[:], in0=s0sq[:],
                                               scalar1=-1.0, scalar2=2.0,
                                               op0=ALU.mult, op1=ALU.add),
                               "sve")
                        em.wait(v, "sve", em.cnt["sve"])
                        em.inc(v.tensor_mul(r1row[:], r1row[:], s0sq[:]),
                               "sve")
                        em.wait(v, "sve", em.cnt["sve"])
                        em.inc(v.scalar_tensor_tensor(
                            out=crow[:], in0=r1row[:], scalar=lam11,
                            in1=s01sb[0:1, 0:512], op0=ALU.mult,
                            op1=ALU.mult), "sve")
                        em.did_rd(["s01sb", "lam"], "sve")
                        em.did_wr(["r1row", "s0sq", "crow"], "sve")
                    em.sec("vector", ep_v1)

                    em.grp("tensor", "ste", ["crow", "ones"], ["psS0.1"],
                           lambda t: t.matmul(psS0[:, 512:1024],
                                              ones_rowb[0:1, :], crow[:],
                                              start=True, stop=True))

                    em.grp("vector", "sve", ["psS0.1"], ["Csb"],
                           lambda v: v.tensor_copy(out=Csb[:],
                                                   in_=psS0[:, 512:1024]))

                    def ep_v2(v):
                        em.rd(v, ["Csb", "psA0", "psA1"])
                        em.wr(v, ["t1b", "apb", "sqb"])
                        em.inc(v.tensor_mul(t1b[:], psA1[:], Csb[:]), "sve")
                        em.wait(v, "sve", em.cnt["sve"])
                        em.inc(v.tensor_sub(apb[:], psA0[:], t1b[:]), "sve")
                        em.wait(v, "sve", em.cnt["sve"])
                        em.inc(v.tensor_mul(sqb[:], apb[:], apb[:]), "sve")
                        em.did_rd(["Csb", "psA0", "psA1"], "sve")
                        em.did_wr(["t1b", "apb", "sqb"], "sve")
                    em.sec("vector", ep_v2)

                    em.grp("tensor", "ste", ["sqb", "ones"], ["psRW"],
                           lambda t: t.matmul(psRW[0:1, 0:512],
                                              ones_col[:, 0:1], sqb[:],
                                              start=True, stop=True))

                    def ep_v3(v):
                        em.rd(v, ["psRW", "s01sb"])
                        em.wr(v, ["s0sq", "erow"])
                        em.inc(v.tensor_mul(s0sq[:], s01sb[0:1, 0:512],
                                            s01sb[0:1, 0:512]), "sve")
                        em.wait(v, "sve", em.cnt["sve"])
                        em.inc(v.scalar_tensor_tensor(
                            out=erow[:], in0=s0sq[:], scalar=128.0 * 1e-5,
                            in1=psRW[0:1, 0:512], op0=ALU.mult, op1=ALU.add),
                            "sve")
                        em.did_rd(["psRW", "s01sb"], "sve")
                        em.did_wr(["s0sq", "erow"], "sve")
                    em.sec("vector", ep_v3)

                    def ep_s3(s):
                        em.rd(s, ["erow"])
                        em.wr(s, ["erow", "frow"])
                        em.inc(s.activation(out=erow[:], in_=erow[:],
                                            func=AF.Ln), "ssc")
                        em.wait(s, "ssc", em.cnt["ssc"])
                        em.inc(s.activation(out=frow[:], in_=erow[:],
                                            func=AF.Exp, scale=-0.5,
                                            bias=cells[0:1, 15:16]), "ssc")
                        em.did_wr(["erow", "frow"], "ssc")
                    em.sec("scalar", ep_s3)

                    em.grp("tensor", "ste", ["frow", "ones"], ["psS1.1"],
                           lambda t: t.matmul(psS1[:, 512:1024],
                                              ones_rowb[0:1, :], frow[:],
                                              start=True, stop=True))

                    em.grp("vector", "sve", ["psS1.1", "apb"], ["attnT"],
                           lambda v, h=h, qb=qb: v.tensor_mul(
                               attnT[:, h * T + qb * QB:h * T + (qb + 1) * QB],
                               apb[:], psS1[:, 512:1024]))

        stkB.close()
        em.fence()
        stkB2 = ExitStack()
        wob = sb("wob", [128, 2 * NDC * 512], BF16, stkB2)
        wotile = sb("wotile", [128, 2 * 512], BF16, stkB2)

        em.tfence()
        with nc.psum_tensor("psO", [128, 1024], F32) as psO:
            for dc in (range(NDOC) if cfg.phases >= 4 else []):
                par = dc % 2
                dsem = "dwoB0" if par == 0 else "dwoB1"
                wres = "wob.%d" % par

                def b2_dma(g, dc=dc, par=par):
                    dst = wob[:, par * NDC * 512:
                              (par + 1) * NDC * 512].rearrange(
                        "p (c j) -> p c j", j=512)
                    src = wo_d[:, dc * 512:(dc + 1) * 512].rearrange(
                        "(c p) j -> p c j", p=128)
                    return g.dma_start(out=dst, in_=src)
                em.grp("gpsimd", dsem, [], [wres], b2_dma, dmak=True)

                for tt in range(NTT):
                    pp = tt % 2
                    pr = "psO.%d" % pp
                    wtr = "wotile.%d" % pp

                    def b2_mm(t, tt=tt, pp=pp, par=par):
                        for c in range(H):
                            ins = t.matmul(
                                psO[:, pp * 512:(pp + 1) * 512],
                                attnT[:, c * T + tt * 128:
                                      c * T + (tt + 1) * 128],
                                wob[:, par * NDC * 512 + c * 512:
                                    par * NDC * 512 + (c + 1) * 512],
                                start=(c == 0), stop=(c == H - 1))
                        return ins
                    em.grp("tensor", "ste", [wres, "attnT"], [pr], b2_mm)

                    em.grp("vector", "sve", [pr], [wtr],
                           lambda v, pp=pp: v.tensor_copy(
                               out=wotile[:, pp * 512:(pp + 1) * 512],
                               in_=psO[:, pp * 512:(pp + 1) * 512]))

                    em.grp("sync", "dstore", [wtr], [],
                           lambda sy, dc=dc, tt=tt, pp=pp: sy.dma_start(
                               out=wo_scr_d[tt * 128:(tt + 1) * 128,
                                            dc * 512:(dc + 1) * 512],
                               in_=wotile[:, pp * 512:(pp + 1) * 512]),
                           dmak=True)

        if cfg.debug:
            def dbg_dump(sy):
                em.rd(sy, ["xnT", "attnT", "lam", "lni", "lnb2"])
                em.inc(sy.dma_start(out=xnT_dbg[:], in_=xnT[:]), "dstore", 16)
                em.inc(sy.dma_start(out=attnT_dbg[:], in_=attnT[:]),
                       "dstore", 16)
                em.inc(sy.dma_start(out=cells_dbg[:], in_=cells[:]),
                       "dstore", 16)
                em.inc(sy.dma_start(out=lni_dbg[:, 0:cfg.NTT], in_=lni[:]),
                       "dstore", 16)
                em.inc(sy.dma_start(out=lni_dbg[:, cfg.NTT:], in_=lnb2[:]),
                       "dstore", 16)
                em.wait(sy, "dstore", em.cnt["dstore"])
            em.sec("sync", dbg_dump)
        stkB2.close()
        stkAB.close()
        em.fence()
        stkC = ExitStack()
        qxT = sb("qxT", [128, NDC * GT], BF16, stkC)
        qtile = sb("qtile", [128, D], BF16, stkC)
        xf32 = sb("xf32c", [128, 2 * D], F32, stkC)
        i32buf = sb("i32bufc", [128, 1024], I32, stkC)
        fbuf = sb("fbufc", [128, 1024], F32, stkC)
        rbuf = sb("rbufc", [128, 1024], F32, stkC)
        ebuf = sb("ebufc", [128, 1024], F32, stkC)
        hbuf = sb("hbuf", [128, GTT * F], BF16, stkC)
        hmax = sb("hmax", [128, GTT * NFC], F32, stkC)
        axc = sb("axc", [128, GTT], F32, stkC)
        sxq = sb("sxq", [128, GTT], F32, stkC)
        hscol = sb("hscol", [128, GTT], F32, stkC)
        shq = sb("shq", [128, GTT], F32, stkC)
        hs2 = sb("hs2", [128, GTT], F32, stkC)
        hqT = sb("hqT", [128, NFC128 * 256], BF16, stkC)
        w1t = sb("w1t", [128, 2 * NDC * 512], BF16, stkC)
        w2t = sb("w2t", [128, 2 * 2048], BF16, stkC)
        wotc = sb("wotc", [128, 2 * 512], BF16, stkC)
        xrest = sb("xrest", [128, 2 * 512], F32, stkC)
        tAb = sb("tAb", [128, 512], F32, stkC)
        outb = sb("outb", [128, 2 * 512], F32, stkC)

        dstore_final = em.cnt["dstore"]

        for g in (range(NG) if cfg.phases >= 5 else []):
            em.tfence()
            with nc.psum_tensor("psTq_%d" % g, [128, 2048], BF16) as psTq:
                for tl in range(GTT):
                    tt = g * GTT + tl
                    par = tt % 2
                    dsem = "dxa0" if par == 0 else "dxa1"
                    xr = "xf.%d" % par
                    xnr = "xnb.%d" % par
                    xs = xf32[:, par * D:(par + 1) * D]
                    xns = xs

                    em.grp("sync", dsem, [], [xr, xnr],
                           lambda sy, tt=tt, xs=xs: sy.dma_start(
                               out=xs, in_=x_d[tt * 128:(tt + 1) * 128, :]),
                           dmak=True)

                    em.grp("vector", "sve", [xr, "lni", "lnb2"], [xnr],
                           lambda v, tt=tt, xs=xs, xns=xns: v.tensor_scalar(
                               out=xns, in0=xs, scalar1=lni[:, tt:tt + 1],
                               scalar2=lnb2[:, tt:tt + 1], op0=ALU.mult,
                               op1=ALU.add))

                    def c_ax(v, tl=tl, xns=xns, xnr=xnr):
                        em.rd(v, [xnr, "mb.w1"])
                        em.wr(v, ["axc", "sxq", "hscol"])
                        em.inc(v.reduce_max(out=axc[:, tl:tl + 1], in_=xns,
                                            axis=AX.X,
                                            apply_absolute_value=True),
                               "sve")
                        em.wait(v, "sve", em.cnt["sve"])
                        em.inc(v.tensor_scalar_max(axc[:, tl:tl + 1],
                                                   axc[:, tl:tl + 1], 1e-5),
                               "sve")
                        em.wait(v, "sve", em.cnt["sve"])
                        em.inc(v.reciprocal(out=sxq[:, tl:tl + 1],
                                            in_=axc[:, tl:tl + 1]), "sve")
                        em.wait(v, "sve", em.cnt["sve"])
                        em.inc(v.tensor_mul(scol[:, 7:8], axc[:, tl:tl + 1],
                                            sxq[:, tl:tl + 1]), "sve")
                        em.wait(v, "sve", em.cnt["sve"])
                        em.inc(v.tensor_scalar(out=scol[:, 7:8],
                                               in0=scol[:, 7:8],
                                               scalar1=-1.0, scalar2=2.0,
                                               op0=ALU.mult, op1=ALU.add),
                               "sve")
                        em.wait(v, "sve", em.cnt["sve"])
                        em.inc(v.tensor_mul(sxq[:, tl:tl + 1],
                                            sxq[:, tl:tl + 1],
                                            scol[:, 7:8]), "sve")
                        em.wait(v, "sve", em.cnt["sve"])
                        em.inc(v.tensor_scalar_mul(sxq[:, tl:tl + 1],
                                                   sxq[:, tl:tl + 1], 127.0),
                               "sve")
                        em.inc(v.scalar_tensor_tensor(
                            out=hscol[:, tl:tl + 1], in0=axc[:, tl:tl + 1],
                            scalar=1.0 / 127.0, in1=m1b[:], op0=ALU.mult,
                            op1=ALU.mult), "sve")
                        em.did_rd([xnr, "mb.w1"], "sve")
                        em.did_wr(["axc", "sxq", "hscol", "scol.nt"], "sve")
                    em.sec("vector", c_ax)

                    for qh in range(D // 1024):
                        xh = xns[:, qh * 1024:(qh + 1) * 1024]
                        qth = qtile[:, qh * 1024:(qh + 1) * 1024]

                        def c_q(v, tl=tl, xh=xh, xnr=xnr):
                            em.rd(v, [xnr, "sxq"])
                            em.wr(v, ["i32buf"])
                            em.inc(v.tensor_scalar(
                                out=i32buf[:], in0=xh,
                                scalar1=sxq[:, tl:tl + 1], scalar2=None,
                                op0=ALU.mult), "sve")
                            em.did_rd([xnr, "sxq"], "sve")
                            em.did_wr(["i32buf"], "sve")
                        em.sec("vector", c_q)

                        em.grp("vector", "sve", ["i32buf"], ["f0c"],
                               lambda v: v.tensor_scalar(
                                   out=fbuf[:], in0=i32buf[:], scalar1=2.0,
                                   scalar2=None, op0=ALU.subtract))

                        def c_q2(v, tl=tl, xh=xh, qth=qth, xnr=xnr):
                            em.rd(v, [xnr, "sxq", "f0c"])
                            em.wr(v, ["rbuf", "ebuf", "f0c", "qtile"])
                            em.inc(v.scalar_tensor_tensor(
                                out=rbuf[:], in0=xh,
                                scalar=sxq[:, tl:tl + 1], in1=fbuf[:],
                                op0=ALU.mult, op1=ALU.subtract), "sve")
                            em.wait(v, "sve", em.cnt["sve"])
                            em.inc(v.scalar_tensor_tensor(
                                out=ebuf[:], in0=rbuf[:], scalar=0.5,
                                in1=fbuf[:], op0=ALU.is_gt, op1=ALU.add),
                                "sve")
                            em.wait(v, "sve", em.cnt["sve"])
                            em.inc(v.scalar_tensor_tensor(
                                out=fbuf[:], in0=rbuf[:], scalar=1.5,
                                in1=ebuf[:], op0=ALU.is_ge, op1=ALU.add),
                                "sve")
                            em.wait(v, "sve", em.cnt["sve"])
                            em.inc(v.scalar_tensor_tensor(
                                out=ebuf[:], in0=rbuf[:], scalar=2.5,
                                in1=fbuf[:], op0=ALU.is_gt, op1=ALU.add),
                                "sve")
                            em.wait(v, "sve", em.cnt["sve"])
                            em.inc(v.scalar_tensor_tensor(
                                out=qth, in0=rbuf[:], scalar=3.5,
                                in1=ebuf[:], op0=ALU.is_ge, op1=ALU.add),
                                "sve")
                            em.did_rd([xnr, "sxq"], "sve")
                            em.did_wr(["rbuf", "ebuf", "f0c", "qtile"],
                                      "sve")
                        em.sec("vector", c_q2)

                        if getattr(cfg, "dbgqx", None) == (g, tl):
                            def dbgx(sy, qh=qh, xh=xh, qth=qth, xnr=xnr):
                                em.rd(sy, ["i32buf", "rbuf", "qtile",
                                           "sxq", xnr])
                                em.inc(sy.dma_start(
                                    out=dxq_i32[:, qh * 1024:
                                                (qh + 1) * 1024],
                                    in_=i32buf[:]), "ddbg", 16)
                                em.inc(sy.dma_start(
                                    out=dxq_rp[:, qh * 1024:(qh + 1) * 1024],
                                    in_=rbuf[:]), "ddbg", 16)
                                em.inc(sy.dma_start(
                                    out=dxq_out[:, qh * 1024:
                                                (qh + 1) * 1024],
                                    in_=qth), "ddbg", 16)
                                em.inc(sy.dma_start(
                                    out=dxq_in[:, qh * 1024:(qh + 1) * 1024],
                                    in_=xh), "ddbg", 16)
                                em.inc(sy.dma_start(out=dxq_s[:],
                                                    in_=sxq[:, 0:4]),
                                       "ddbg", 16)
                                em.wait(sy, "ddbg", em.cnt["ddbg"])
                            em.sec("sync", dbgx)
                            em.sec("vector",
                                   lambda v: em.wait(v, "ddbg",
                                                     em.cnt["ddbg"]))

                    for cb in range(NDC // 4):
                        pr = "psTq.%d" % (cb % 2)
                        po = (cb % 2) * 1024

                        def c_qtr(t, cb=cb, po=po):
                            for ci in range(4):
                                c = cb * 4 + ci
                                ins = t.transpose(
                                    psTq[:, po + ci * 128:po + (ci + 1) * 128],
                                    qtile[:, c * 128:(c + 1) * 128], eye[:])
                            return ins
                        em.grp("tensor", "ste", ["qtile", "eye"], [pr],
                               c_qtr)

                        def c_qtrev(v, tl=tl, cb=cb, po=po):
                            for ci in range(4):
                                c = cb * 4 + ci
                                ins = v.tensor_copy(
                                    out=qxT[:, c * GT + tl * 128:
                                            c * GT + (tl + 1) * 128],
                                    in_=psTq[:, po + ci * 128:po + (ci + 1) * 128])
                            return ins
                        em.grp("vector", "sve", [pr], ["qxT"], c_qtrev)

            em.tfence()
            with nc.psum_tensor("psH_%d" % g, [128, 1024], F32) as psH:
                mm1_ssc0 = em.cnt["ssc"]
                mm1_ste_after = {}
                for fc in range(NFC):
                    par = fc % 2
                    dsem = "dw1t0" if par == 0 else "dw1t1"
                    wres = "w1t.%d" % par

                    def c_w1dma(sy, fc=fc, par=par, wres=wres, dsem=dsem):
                        em.wr(sy, [wres])
                        em.wait(sy, "dstore", dstore_final)
                        for c in range(NDC):
                            em.inc(sy.dma_start(
                                out=w1t[:, par * NDC * 512 + c * 512:
                                        par * NDC * 512 + (c + 1) * 512],
                                in_=w1q_d[fc * 512:(fc + 1) * 512,
                                          c * 128:(c + 1) * 128],
                                transpose=True), dsem, 16)
                        em.did_wr([wres], dsem)
                    em.sec("sync", c_w1dma)

                    for tl in range(GTT):
                        idx = fc * GTT + tl
                        pp = idx % 2

                        def c_mm1(t, fc=fc, tl=tl, pp=pp, par=par, idx=idx,
                                  wres=wres):
                            em.rd(t, [wres, "qxT"])
                            if idx >= 2:
                                em.wait(t, "ssc", mm1_ssc0 + idx - 1)
                            for c in range(NDC):
                                ins = t.matmul(
                                    psH[:, pp * 512:(pp + 1) * 512],
                                    qxT[:, c * GT + tl * 128:
                                        c * GT + (tl + 1) * 128],
                                    w1t[:, par * NDC * 512 + c * 512:
                                        par * NDC * 512 + (c + 1) * 512],
                                    start=(c == 0), stop=(c == NDC - 1))
                            em.inc(ins, "ste")
                            em.did_rd([wres, "qxT"], "ste")
                        em.sec("tensor", c_mm1)
                        mm1_ste_after[idx] = em.cnt["ste"]

                def c_silu_all(s, g=g):
                    em.rd(s, ["hscol"])
                    em.wr(s, ["h.%d.%d" % (tl, fc)
                              for fc in range(NFC) for tl in range(GTT)])
                    for fc in range(NFC):
                        for tl in range(GTT):
                            idx = fc * GTT + tl
                            pp = idx % 2
                            em.wait(s, "ste", mm1_ste_after[idx])
                            ins = s.activation(
                                out=hbuf[:, tl * F + fc * 512:
                                         tl * F + (fc + 1) * 512],
                                in_=psH[:, pp * 512:(pp + 1) * 512],
                                func=AF.Silu, scale=hscol[:, tl:tl + 1])
                            em.inc(ins, "ssc")
                            em.did_wr(["h.%d.%d" % (tl, fc)], "ssc")
                    em.did_rd(["hscol"], "ssc")
                em.sec("scalar", c_silu_all)

                def c_hmax_all(v):
                    for fc in range(NFC):
                        for tl in range(GTT):
                            idx = fc * GTT + tl
                            em.wait(v, "ssc", mm1_ssc0 + idx + 1)
                            ins = v.reduce_max(
                                out=hmax[:, tl * NFC + fc:
                                         tl * NFC + fc + 1],
                                in_=hbuf[:, tl * F + fc * 512:
                                         tl * F + (fc + 1) * 512],
                                axis=AX.X, apply_absolute_value=True)
                            em.inc(ins, "sve")
                            em.did_rd(["h.%d.%d" % (tl, fc)], "sve")
                    em.did_wr(["hmax"], "sve")
                em.sec("vector", c_hmax_all)

            for th in range(GTT // 2):
                em.tfence()
                with nc.psum_tensor("psTh_%d_%d" % (g, th), [128, 2048],
                                    BF16) as psTh:
                    for tl in (2 * th, 2 * th + 1):
                        def c_hscale(v, tl=tl):
                            em.rd(v, ["hmax", "mb.w2"])
                            em.wr(v, ["axc", "shq", "hs2"])
                            em.inc(v.reduce_max(
                                out=axc[:, tl:tl + 1],
                                in_=hmax[:, tl * NFC:(tl + 1) * NFC],
                                axis=AX.X), "sve")
                            em.wait(v, "sve", em.cnt["sve"])
                            em.inc(v.tensor_scalar_max(axc[:, tl:tl + 1],
                                                       axc[:, tl:tl + 1],
                                                       1e-5), "sve")
                            em.wait(v, "sve", em.cnt["sve"])
                            em.inc(v.reciprocal(out=shq[:, tl:tl + 1],
                                                in_=axc[:, tl:tl + 1]),
                                   "sve")
                            em.wait(v, "sve", em.cnt["sve"])
                            em.inc(v.tensor_mul(scol[:, 7:8],
                                                axc[:, tl:tl + 1],
                                                shq[:, tl:tl + 1]), "sve")
                            em.wait(v, "sve", em.cnt["sve"])
                            em.inc(v.tensor_scalar(out=scol[:, 7:8],
                                                   in0=scol[:, 7:8],
                                                   scalar1=-1.0, scalar2=2.0,
                                                   op0=ALU.mult,
                                                   op1=ALU.add), "sve")
                            em.wait(v, "sve", em.cnt["sve"])
                            em.inc(v.tensor_mul(shq[:, tl:tl + 1],
                                                shq[:, tl:tl + 1],
                                                scol[:, 7:8]), "sve")
                            em.wait(v, "sve", em.cnt["sve"])
                            em.inc(v.tensor_scalar_mul(shq[:, tl:tl + 1],
                                                       shq[:, tl:tl + 1],
                                                       127.0), "sve")
                            em.inc(v.scalar_tensor_tensor(
                                out=hs2[:, tl:tl + 1],
                                in0=axc[:, tl:tl + 1], scalar=1.0 / 127.0,
                                in1=m2b[:], op0=ALU.mult, op1=ALU.mult),
                                "sve")
                            em.did_rd(["hmax", "mb.w2"], "sve")
                            em.did_wr(["axc", "shq", "hs2"], "sve")
                        em.sec("vector", c_hscale)

                        for sl in range(F // D):
                            hres = ["h.%d.%d" % (tl, fc) for fc in
                                    range(sl * (D // 512),
                                          (sl + 1) * (D // 512))]

                            for qh in range(D // 1024):
                                hh_ap = hbuf[:, tl * F + sl * D + qh * 1024:
                                             tl * F + sl * D + (qh + 1) * 1024]
                                qth = qtile[:, qh * 1024:(qh + 1) * 1024]

                                def c_hq(v, tl=tl, hh_ap=hh_ap, hres=hres):
                                    em.rd(v, ["shq"] + hres)
                                    em.wr(v, ["i32buf"])
                                    em.inc(v.tensor_scalar(
                                        out=i32buf[:], in0=hh_ap,
                                        scalar1=shq[:, tl:tl + 1],
                                        scalar2=None, op0=ALU.mult), "sve")
                                    em.did_rd(["shq"] + hres, "sve")
                                    em.did_wr(["i32buf"], "sve")
                                em.sec("vector", c_hq)

                                em.grp("vector", "sve", ["i32buf"], ["f0c"],
                                       lambda v: v.tensor_scalar(
                                           out=fbuf[:], in0=i32buf[:],
                                           scalar1=2.0, scalar2=None,
                                           op0=ALU.subtract))

                                def c_hq2(v, tl=tl, hh_ap=hh_ap, qth=qth,
                                          hres=hres):
                                    em.rd(v, ["shq", "f0c"] + hres)
                                    em.wr(v, ["rbuf", "ebuf", "f0c",
                                              "qtile"])
                                    em.inc(v.scalar_tensor_tensor(
                                        out=rbuf[:], in0=hh_ap,
                                        scalar=shq[:, tl:tl + 1],
                                        in1=fbuf[:], op0=ALU.mult,
                                        op1=ALU.subtract), "sve")
                                    em.wait(v, "sve", em.cnt["sve"])
                                    em.inc(v.scalar_tensor_tensor(
                                        out=ebuf[:], in0=rbuf[:], scalar=0.5,
                                        in1=fbuf[:], op0=ALU.is_gt,
                                        op1=ALU.add), "sve")
                                    em.wait(v, "sve", em.cnt["sve"])
                                    em.inc(v.scalar_tensor_tensor(
                                        out=fbuf[:], in0=rbuf[:], scalar=1.5,
                                        in1=ebuf[:], op0=ALU.is_ge,
                                        op1=ALU.add), "sve")
                                    em.wait(v, "sve", em.cnt["sve"])
                                    em.inc(v.scalar_tensor_tensor(
                                        out=ebuf[:], in0=rbuf[:], scalar=2.5,
                                        in1=fbuf[:], op0=ALU.is_gt,
                                        op1=ALU.add), "sve")
                                    em.wait(v, "sve", em.cnt["sve"])
                                    em.inc(v.scalar_tensor_tensor(
                                        out=qth, in0=rbuf[:], scalar=3.5,
                                        in1=ebuf[:], op0=ALU.is_ge,
                                        op1=ALU.add), "sve")
                                    em.did_rd(["shq"] + hres, "sve")
                                    em.did_wr(["rbuf", "ebuf", "f0c",
                                               "qtile"], "sve")
                                em.sec("vector", c_hq2)

                                if getattr(cfg, "dbgq", None) == \
                                        (g, tl, sl, qh):
                                    def dbgh(sy, hh_ap=hh_ap, qth=qth):
                                        em.rd(sy, ["i32buf", "rbuf",
                                                   "qtile", "shq"])
                                        em.inc(sy.dma_start(
                                            out=dhq_i32[:], in_=i32buf[:]),
                                            "ddbg", 16)
                                        em.inc(sy.dma_start(
                                            out=dhq_rp[:], in_=rbuf[:]),
                                            "ddbg", 16)
                                        em.inc(sy.dma_start(
                                            out=dhq_out[:], in_=qth),
                                            "ddbg", 16)
                                        em.inc(sy.dma_start(
                                            out=dhq_in[:], in_=hh_ap),
                                            "ddbg", 16)
                                        em.inc(sy.dma_start(
                                            out=dhq_s[:], in_=shq[:, 0:4]),
                                            "ddbg", 16)
                                        em.wait(sy, "ddbg",
                                                em.cnt["ddbg"])
                                    em.sec("sync", dbgh)
                                    em.sec("vector",
                                           lambda v: em.wait(
                                               v, "ddbg", em.cnt["ddbg"]))

                            for cb in range(NDC // 4):
                                pr = "psTh.%d" % (cb % 2)
                                po = (cb % 2) * 1024

                                def c_htr(t, cb=cb, po=po):
                                    for ci in range(4):
                                        c = cb * 4 + ci
                                        ins = t.transpose(
                                            psTh[:, po + ci * 128:
                                                 po + (ci + 1) * 128],
                                            qtile[:, c * 128:(c + 1) * 128],
                                            eye[:])
                                    return ins
                                em.grp("tensor", "ste", ["qtile", "eye"],
                                       [pr], c_htr)

                                def c_htrev(v, tl=tl, sl=sl, cb=cb, th=th,
                                            po=po):
                                    ttp = tl - 2 * th
                                    for ci in range(4):
                                        fc128 = sl * NDC + cb * 4 + ci
                                        ins = v.tensor_copy(
                                            out=hqT[:, fc128 * 256 +
                                                    ttp * 128:
                                                    fc128 * 256 +
                                                    (ttp + 1) * 128],
                                            in_=psTh[:, po + ci * 128:
                                                     po + (ci + 1) * 128])
                                    return ins
                                em.grp("vector", "sve", [pr], ["hqT"],
                                       c_htrev)

                em.tfence()
                with nc.psum_tensor("psO2_%d_%d" % (g, th), [128, 4096],
                                    F32) as psO2:
                    NDH = NDOC // 2
                    RRS8 = ["psO2.%d.%d" % (pb, k)
                            for pb in range(NDH) for k in range(4)]
                    for fc128 in range(NFC128):
                        par = fc128 % 2
                        dsem = "dw2t0" if par == 0 else "dw2t1"
                        wres = "w2t.%d" % par

                        def c_w2sec(sy, fc128=fc128, par=par,
                                    dsem=dsem, wres=wres):
                            em.wr(sy, [wres])
                            em.wait(sy, "dstore", dstore_final)
                            ins = sy.dma_start(
                                out=w2t[:, par * NDH * 1024:
                                        (par + 1) * NDH * 1024],
                                in_=w2q_d[0:NDH * 1024,
                                          fc128 * 128:(fc128 + 1) * 128],
                                transpose=True)
                            em.inc(ins, dsem, 16)
                            em.did_wr([wres], dsem)
                        em.sec("scalar", c_w2sec)

                        def c_mm2(t, fc128=fc128, par=par, wres=wres):
                            st = (fc128 == 0)
                            sp = (fc128 == NFC128 - 1)
                            em.rd(t, [wres, "hqT"])
                            if st:
                                em.wr(t, RRS8)
                            for pb in range(NDH):
                                for k in range(4):
                                    ti, di = divmod(k, 2)
                                    ins = t.matmul(
                                        psO2[:, (pb * 4 + k) * 512:
                                             (pb * 4 + k + 1) * 512],
                                        hqT[:, fc128 * 256 + ti * 128:
                                            fc128 * 256 + (ti + 1) * 128],
                                        w2t[:, par * NDH * 1024 +
                                            pb * 1024 + di * 512:
                                            par * NDH * 1024 +
                                            pb * 1024 + (di + 1) * 512],
                                        start=st, stop=sp)
                            em.inc(ins, "ste")
                            em.did_rd([wres, "hqT"], "ste")
                            if sp:
                                em.did_wr(RRS8, "ste")
                        em.sec("tensor", c_mm2)

                    for dcp in range(NDOC // 2):
                        pb = dcp % 2
                        rrs = ["psO2.%d.%d" % (pb, k) for k in range(4)]
                        for k in range(4):
                            ti, di = divmod(k, 2)
                            tl = 2 * th + ti
                            tt = g * GTT + tl
                            dc = dcp * 2 + di
                            pp = k % 2
                            rr = rrs[k]
                            wor = "wotc.%d" % pp
                            xrr = "xrest.%d" % pp
                            obr = "outb.%d" % pp
                            dsemw = "dwoC0" if pp == 0 else "dwoC1"
                            dsemx = "dres0" if pp == 0 else "dres1"

                            def c_lodma(sy, tt=tt, dc=dc, pp=pp, wor=wor,
                                        xrr=xrr, dsemw=dsemw, dsemx=dsemx):
                                em.wr(sy, [wor, xrr])
                                em.wait(sy, "dstore", dstore_final)
                                i1 = sy.dma_start(
                                    out=wotc[:, pp * 512:(pp + 1) * 512],
                                    in_=wo_scr_d[tt * 128:(tt + 1) * 128,
                                                 dc * 512:(dc + 1) * 512])
                                em.inc(i1, dsemw, 16)
                                i2 = sy.dma_start(
                                    out=xrest[:, pp * 512:(pp + 1) * 512],
                                    in_=x_d[tt * 128:(tt + 1) * 128,
                                            dc * 512:(dc + 1) * 512])
                                em.inc(i2, dsemx, 16)
                                em.did_wr([wor], dsemw)
                                em.did_wr([xrr], dsemx)
                            em.sec("sync", c_lodma)

                            em.grp("vector", "sve", [rr, "hs2"], ["tAb"],
                                   lambda v, pb=pb, k=k, tl=tl:
                                   v.tensor_scalar_mul(
                                       tAb[:],
                                       psO2[:, (pb * 4 + k) * 512:
                                            (pb * 4 + k + 1) * 512],
                                       hs2[:, tl:tl + 1]))

                            def c_out(v, pp=pp, wor=wor, xrr=xrr, obr=obr):
                                em.rd(v, ["tAb", wor, xrr])
                                em.wr(v, [obr])
                                em.inc(v.tensor_add(
                                    outb[:, pp * 512:(pp + 1) * 512],
                                    tAb[:],
                                    wotc[:, pp * 512:(pp + 1) * 512]), "sve")
                                em.wait(v, "sve", em.cnt["sve"])
                                em.inc(v.tensor_add(
                                    outb[:, pp * 512:(pp + 1) * 512],
                                    outb[:, pp * 512:(pp + 1) * 512],
                                    xrest[:, pp * 512:(pp + 1) * 512]),
                                    "sve")
                                em.did_rd(["tAb", wor, xrr], "sve")
                                em.did_wr([obr], "sve")
                            em.sec("vector", c_out)

                            em.grp("sync", "dout", [obr], [],
                                   lambda sy, tt=tt, dc=dc, pp=pp:
                                   sy.dma_start(
                                       out=y_d[tt * 128:(tt + 1) * 128,
                                               dc * 512:(dc + 1) * 512],
                                       in_=outb[:, pp * 512:(pp + 1) * 512]),
                                   dmak=True)

        def fin(sy):
            em.wait(sy, "dout", em.cnt["dout"])
            em.wait(sy, "dstore", em.cnt["dstore"])
        em.sec("sync", fin)
        stkC.close()

    gctx.close()
    nc.finalize()
    return nc


_NC_CACHE = {}


def _get_nc():
    if "nc" not in _NC_CACHE:
        _NC_CACHE["nc"] = build(CFG_FULL)
    return _NC_CACHE["nc"]


def kernel(L0_attn_x, L0_attn_Wq, L0_attn_Wk, L0_attn_Wv, L0_attn_Wo,
           L0_attn_lq1, L0_attn_lk1, L0_attn_lq2, L0_attn_lk2,
           L0_ffn_W1, L0_ffn_W2):
    x = np.ascontiguousarray(np.asarray(L0_attn_x, dtype=np.float32))
    ws = {
        "Wq": np.ascontiguousarray(np.asarray(L0_attn_Wq, dtype=np.float32)),
        "Wk": np.ascontiguousarray(np.asarray(L0_attn_Wk, dtype=np.float32)),
        "Wv": np.ascontiguousarray(np.asarray(L0_attn_Wv, dtype=np.float32)),
        "Wo": np.ascontiguousarray(np.asarray(L0_attn_Wo, dtype=np.float32)),
        "lq1": np.ascontiguousarray(np.asarray(L0_attn_lq1, dtype=np.float32)),
        "lk1": np.ascontiguousarray(np.asarray(L0_attn_lk1, dtype=np.float32)),
        "lq2": np.ascontiguousarray(np.asarray(L0_attn_lq2, dtype=np.float32)),
        "lk2": np.ascontiguousarray(np.asarray(L0_attn_lk2, dtype=np.float32)),
        "W1": np.ascontiguousarray(np.asarray(L0_ffn_W1, dtype=np.float32)),
        "W2": np.ascontiguousarray(np.asarray(L0_ffn_W2, dtype=np.float32)),
    }
    nc = _get_nc()
    n = x.shape[0]
    in_maps = [dict(ws, x=np.ascontiguousarray(x[b])) for b in range(n)]
    res = run_bass_kernel_spmd(nc, in_maps, core_ids=list(range(n)))
    return np.stack([np.asarray(res.results[b]["y"]) for b in range(n)],
                    axis=0)
